# version 04
# baseline (speedup 1.0000x reference)
import os
import sys

if os.path.isdir("/opt/trn_rl_repo") and "/opt/trn_rl_repo" not in sys.path:
    sys.path.insert(0, "/opt/trn_rl_repo")

from contextlib import ExitStack

import numpy as np

import concourse.bass as bass
import concourse.mybir as mybir
from concourse.bass_utils import run_bass_kernel_spmd

F32 = mybir.dt.float32
BF16 = mybir.dt.bfloat16
U8 = mybir.dt.uint8
AF = mybir.ActivationFunctionType
ALU = mybir.AluOpType
AX = mybir.AxisListType

NCORE = 8
T = 1024
NTT = 8
DM = 4096
KC = 32
FCN = 8
OCN = 32
CC = 88
IPAD = CC * 128
HKV = 8
DB = 8
NR = CC // DB
EPS_LN = 1e-5
EPS_DEN = 1e-6
QS = 7.5


def _host_prep(inputs):
    f32 = np.float32
    X = np.asarray(inputs["L0_attn_hidden_states"], dtype=f32).reshape(NCORE * T, DM)
    x_shards = [np.ascontiguousarray(X[c * T:(c + 1) * T]) for c in range(NCORE)]

    Wq = np.asarray(inputs["L0_attn_Wq"], dtype=f32)
    wq3 = Wq.reshape(32, 128, DM)[0::4]
    wqsel = np.ascontiguousarray(
        wq3.transpose(0, 2, 1).reshape(FCN, KC, 128, 128).transpose(0, 2, 1, 3))
    bq = np.asarray(inputs["L0_attn_bq"], dtype=f32)
    bqsel = np.ascontiguousarray(bq.reshape(32, 128)[0::4].T)

    Wo = np.asarray(inputs["L0_attn_Wo"], dtype=f32)
    wo_blk = np.ascontiguousarray(Wo.reshape(OCN, 128, KC, 128).transpose(0, 3, 2, 1))

    M = np.asarray(inputs["L0_attn_M"], dtype=f32)
    m_host = np.ascontiguousarray(M.transpose(1, 0, 2).reshape(128, HKV * 128))
    z_host = np.ascontiguousarray(np.asarray(inputs["L0_attn_z"], dtype=f32).T)

    def _gu_bytes(qw):
        qv = np.ascontiguousarray(np.asarray(qw)).view(np.uint8)
        qv = np.concatenate(
            [qv, np.zeros((DM, IPAD // 2 - qv.shape[1]), np.uint8)], axis=1)
        return np.ascontiguousarray(
            qv.reshape(KC, 128, CC, 64).transpose(2, 1, 0, 3))

    g_bytes = _gu_bytes(inputs["L0_ffn_gate_qweight"])
    u_bytes = _gu_bytes(inputs["L0_ffn_up_qweight"])

    dqv = np.ascontiguousarray(np.asarray(inputs["L0_ffn_down_qweight"])).view(np.uint8)
    dqv = np.concatenate([dqv, np.zeros((IPAD - dqv.shape[0], 2048), np.uint8)], axis=0)
    d_bytes = np.ascontiguousarray(dqv.reshape(CC, 128, 2048))

    g_sc = np.zeros((32, IPAD), f32)
    g_sc[:, :11008] = np.asarray(inputs["L0_ffn_gate_scales"], dtype=f32)
    u_sc = np.zeros((32, IPAD), f32)
    u_sc[:, :11008] = np.asarray(inputs["L0_ffn_up_scales"], dtype=f32)
    d_sc = np.zeros((CC, DM), f32)
    d_sc[:86] = np.asarray(inputs["L0_ffn_down_scales"], dtype=f32)

    def _zb(qz, rows, cols):
        v = np.ascontiguousarray(np.asarray(qz)).view(np.uint8)
        out = np.zeros((rows, cols), np.uint8)
        out[:v.shape[0], :v.shape[1]] = v
        return out

    g_zb = _zb(inputs["L0_ffn_gate_qzeros"], 32, IPAD // 2)
    u_zb = _zb(inputs["L0_ffn_up_qzeros"], 32, IPAD // 2)
    d_zb = _zb(inputs["L0_ffn_down_qzeros"], CC, 2048)

    import ml_dtypes
    ident_bf = np.eye(128, dtype=ml_dtypes.bfloat16)
    ident_f32 = np.eye(128, dtype=np.float32)

    common = dict(
        wqsel=wqsel, bqsel=bqsel, wo_blk=wo_blk, m_h=m_host, z_h=z_host,
        g_bytes=g_bytes, u_bytes=u_bytes, d_bytes=d_bytes,
        g_sc=g_sc, u_sc=u_sc, d_sc=d_sc, g_zb=g_zb, u_zb=u_zb, d_zb=d_zb,
        ident_bf=ident_bf, ident_f32=ident_f32)
    return x_shards, common


class Sem:
    def __init__(self, nc, stack, name):
        self.h = stack.enter_context(nc.semaphore(name))
        self.n = 0

    def inc(self, instr, k=1):
        self.n += k
        instr.then_inc(self.h, k)
        return self.n

    def dinc(self, instr):
        return self.inc(instr, 16)


def build_program(dbg=False):
    nc = bass.Bass("TRN2", target_bir_lowering=False)

    def din(name, shape, dt):
        return nc.dram_tensor(name, shape, dt, kind="ExternalInput").ap()

    d_x = din("x", [T, DM], F32)
    d_wqsel = din("wqsel", [FCN, 128, KC, 128], F32)
    d_bqsel = din("bqsel", [128, FCN], F32)
    d_wo = din("wo_blk", [OCN, 128, KC, 128], F32)
    d_m = din("m_h", [128, HKV * 128], F32)
    d_z = din("z_h", [128, HKV], F32)
    d_gb = din("g_bytes", [CC, 128, KC, 64], U8)
    d_ub = din("u_bytes", [CC, 128, KC, 64], U8)
    d_db = din("d_bytes", [CC, 128, 2048], U8)
    d_gsc = din("g_sc", [32, IPAD], F32)
    d_usc = din("u_sc", [32, IPAD], F32)
    d_dsc = din("d_sc", [CC, DM], F32)
    d_gzb = din("g_zb", [32, IPAD // 2], U8)
    d_uzb = din("u_zb", [32, IPAD // 2], U8)
    d_dzb = din("d_zb", [CC, 2048], U8)
    d_identb = din("ident_bf", [128, 128], BF16)
    d_identf = din("ident_f32", [128, 128], F32)
    d_out = nc.dram_tensor("out", [T, DM], F32, kind="ExternalOutput").ap()

    def dscr(name, shape, dt):
        kind = "ExternalOutput" if dbg else "Internal"
        return nc.dram_tensor(name, shape, dt, kind=kind).ap()

    s_x1 = dscr("x1_scr", [T, DM], F32)
    s_attn = dscr("attn_scr", [OCN, 128, T], F32)
    s_h = dscr("h_scr", [CC, 128, T], BF16)
    s_oacc = dscr("oacc_scr", [OCN, 128, T], F32)
    s_sgb = dscr("sgb_scr", [2, 32, IPAD], BF16)
    s_sgbblk = dscr("sgbblk_scr", [2, CC, 32, 128], BF16)
    s_zs = dscr("zs_scr", [2, 32, IPAD], BF16)
    s_dsb = dscr("dsb_scr", [CC, DM], BF16)
    s_zsd = dscr("zsd_scr", [CC, DM], BF16)

    top = ExitStack()

    def sb(name, shape, dt, stk=None):
        return (stk or top).enter_context(nc.sbuf_tensor(name, shape, dt))

    def psum(name, shape, dt, stk=None):
        return (stk or top).enter_context(nc.psum_tensor(name, shape, dt))

    ls = Sem(nc, top, "ls")
    lg = Sem(nc, top, "lg")
    pe = Sem(nc, top, "pe")
    dv = Sem(nc, top, "dv")
    ac = Sem(nc, top, "ac")
    st = Sem(nc, top, "st")
    lx = Sem(nc, top, "lx")

    ps = [psum(f"ps{i}", [128, 512], F32) for i in range(6)]

    identb = sb("identb", [128, 128], BF16)
    identf = sb("identf", [128, 128], F32)
    bqsel = sb("bqsel_sb", [128, FCN], F32)
    m_sb = sb("m_sb", [128, HKV * 128], BF16)
    z_sb = sb("z_sb", [128, HKV], BF16)
    ones1 = sb("ones1", [1, 128], BF16)
    oneh32 = sb("oneh32", [128, KC * 32], BF16)
    sv = [sb(f"sv{i}", [128, 1], F32) for i in range(6)]
    svp = sb("svp", [128, 1], F32)

    ls.dinc(nc.sync.dma_start(identb[:, :], d_identb[:, :]))
    ls.dinc(nc.sync.dma_start(identf[:, :], d_identf[:, :]))
    ls.dinc(nc.sync.dma_start(bqsel[:, :], d_bqsel[:, :]))
    lg.dinc(nc.gpsimd.dma_start(m_sb[:, :], d_m[:, :]))
    v_const_lg = lg.dinc(nc.gpsimd.dma_start(z_sb[:, :], d_z[:, :]))
    v_const_ls = ls.n

    dv.inc(nc.vector.memset(ones1[:, :], 1.0))
    dv.inc(nc.vector.memset(oneh32[:, :], 0.0))
    for kc in range(KC):
        dv.inc(nc.vector.memset(oneh32[:, kc * 32 + kc:kc * 32 + kc + 1], 1.0))
    for _ in range(16):
        nc.vector.engine_nop()
    dv.inc(nc.vector.engine_nop())

    with ExitStack() as p0a:
        sgbuf = sb("sgbuf", [32, IPAD], BF16, p0a)
        zbuf = sb("zbuf", [32, IPAD // 2], U8, p0a)
        znib = sb("znib", [32, IPAD], U8, p0a)
        zsbuf = sb("zsbuf", [32, IPAD], BF16, p0a)
        for mat, (dsc_, dzb_) in enumerate([(d_gsc, d_gzb), (d_usc, d_uzb)]):
            if mat > 0:
                nc.sync.wait_ge(dv.h, dv.n)
            vl = lg.dinc(nc.gpsimd.dma_start(sgbuf[:, :], dsc_[:, :]))
            vz = ls.dinc(nc.sync.dma_start(zbuf[:, :], dzb_[:, :]))
            nc.vector.wait_ge(lg.h, vl)
            nc.vector.wait_ge(ls.h, vz)
            znv = znib[:, :].rearrange("p (b two) -> p b two", two=2)
            dv.inc(nc.vector.tensor_scalar(znv[:, :, 0], zbuf[:, :], 15, None, ALU.bitwise_and))
            dv.inc(nc.vector.tensor_scalar(znv[:, :, 1], zbuf[:, :], 4, None, ALU.logical_shift_right))
            dv.inc(nc.vector.scalar_tensor_tensor(
                zsbuf[:, :], znib[:, :], QS, sgbuf[:, :], ALU.subtract, ALU.mult))
            vneg = dv.inc(nc.vector.tensor_scalar(zsbuf[:, :], zsbuf[:, :], -1.0, None, ALU.mult))
            nc.gpsimd.wait_ge(dv.h, vneg)
            lg.dinc(nc.gpsimd.dma_start(s_sgb[mat], sgbuf[:, :]))
            lg.dinc(nc.gpsimd.dma_start(
                s_sgbblk[mat].transpose([1, 0, 2]),
                sgbuf[:, :].rearrange("g (c i) -> g c i", c=CC)))
            vwb = lg.dinc(nc.gpsimd.dma_start(s_zs[mat], zsbuf[:, :]))
            nc.gpsimd.wait_ge(lg.h, vwb)
        dsbuf = sb("dsbuf", [CC, DM], BF16, p0a)
        dzbuf = sb("dzbuf", [CC, 2048], U8, p0a)
        dznib = sb("dznib", [CC, DM], U8, p0a)
        dzsb = sb("dzsb", [CC, DM], BF16, p0a)
        vl = lg.dinc(nc.gpsimd.dma_start(dsbuf[:, :], d_dsc[:, :]))
        vz = ls.dinc(nc.sync.dma_start(dzbuf[:, :], d_dzb[:, :]))
        nc.vector.wait_ge(lg.h, vl)
        nc.vector.wait_ge(ls.h, vz)
        dznv = dznib[:, :].rearrange("p (b two) -> p b two", two=2)
        dv.inc(nc.vector.tensor_scalar(dznv[:, :, 0], dzbuf[:, :], 15, None, ALU.bitwise_and))
        dv.inc(nc.vector.tensor_scalar(dznv[:, :, 1], dzbuf[:, :], 4, None, ALU.logical_shift_right))
        dv.inc(nc.vector.scalar_tensor_tensor(
            dzsb[:, :], dznib[:, :], QS, dsbuf[:, :], ALU.subtract, ALU.mult))
        vneg = dv.inc(nc.vector.tensor_scalar(dzsb[:, :], dzsb[:, :], -1.0, None, ALU.mult))
        nc.gpsimd.wait_ge(dv.h, vneg)
        lg.dinc(nc.gpsimd.dma_start(s_dsb[:, :], dsbuf[:, :]))
        v_tables = lg.dinc(nc.gpsimd.dma_start(s_zsd[:, :], dzsb[:, :]))

    def dpad(k=12):
        for _ in range(k):
            nc.vector.engine_nop()

    def apad(k=2):
        for _ in range(k):
            ac.inc(nc.scalar.activation(svp[:, :], svp[:, :], AF.Identity))

    def ln_stats(x_tile, sq_scr):
        nc.vector.wait_ge(ac.h, ac.n)
        dv.inc(nc.vector.reduce_sum(sv[0][:, :], x_tile[:, :], AX.X, negate=True))
        nc.scalar.wait_ge(dv.h, dv.n)
        va = ac.inc(nc.scalar.activation(sq_scr[:, :], x_tile[:, :], AF.Square))
        nc.vector.wait_ge(ac.h, va)
        dv.inc(nc.vector.reduce_sum(sv[1][:, :], sq_scr[:, :], AX.X))
        dv.inc(nc.vector.tensor_scalar(sv[4][:, :], sv[0][:, :], 1.0 / DM, None, ALU.mult))
        dv.inc(nc.vector.scalar_tensor_tensor(
            sv[5][:, :], sv[0][:, :], 1.0 / DM ** 2, sv[0][:, :], ALU.mult, ALU.mult))
        dpad()
        vv = dv.inc(nc.vector.scalar_tensor_tensor(
            sv[1][:, :], sv[1][:, :], 1.0 / DM, sv[5][:, :], ALU.mult, ALU.subtract))
        dpad()
        nc.scalar.wait_ge(dv.h, vv)
        apad()
        ac.inc(nc.scalar.activation(sv[5][:, :], sv[1][:, :], AF.Ln, bias=np.float32(EPS_LN)))
        apad()
        va = ac.inc(nc.scalar.activation(sv[2][:, :], sv[5][:, :], AF.Exp, scale=np.float32(-0.5)))
        nc.vector.wait_ge(ac.h, va)
        dpad()
        dv.inc(nc.vector.tensor_tensor(sv[3][:, :], sv[4][:, :], sv[2][:, :], ALU.mult))
        dpad()
        vb = dv.inc(nc.vector.engine_nop())
        apad()
        return vb

    def transpose_tok_to_feat(tok_bf, destT, tt, psbs, ready_sem, ready_val, state):
        for k4 in range(KC // 4):
            pb = psbs[k4 % 2]
            nc.tensor.wait_ge(ready_sem.h, ready_val)
            if state[k4 % 2]:
                nc.tensor.wait_ge(dv.h, state[k4 % 2])
            for j in range(4):
                kc = k4 * 4 + j
                i = nc.tensor.transpose(
                    pb[:, j * 128:(j + 1) * 128],
                    tok_bf[:, kc * 128:(kc + 1) * 128], identb[:, :])
            vp = pe.inc(i)
            nc.vector.wait_ge(pe.h, vp)
            dview = destT[:, :].rearrange("p (kc t) -> p kc t", t=1024)
            state[k4 % 2] = dv.inc(nc.vector.tensor_copy(
                dview[:, k4 * 4:k4 * 4 + 4, tt * 128:(tt + 1) * 128],
                pb[:, :].rearrange("p (j t) -> p j t", t=128)))

    attn_stack = ExitStack()
    xln1T = sb("xln1T", [128, KC * 1024], BF16, attn_stack)
    nc.tensor.wait_ge(ls.h, v_const_ls)
    with ExitStack() as p0b:
        xtt = [sb(f"xtt{i}", [128, DM], F32, p0b) for i in range(2)]
        sq_scr = sb("sq0", [128, DM], F32, p0b)
        tokbf = sb("tokbf0", [128, DM], BF16, p0b)
        psb = [psum(f"psb0{i}", [128, 512], BF16, p0b) for i in range(2)]
        tstate = [0, 0]
        v_x_ld = [0] * NTT
        v_x_free = [0] * NTT
        for tt in range(NTT):
            if tt >= 2:
                nc.sync.wait_ge(ac.h, v_x_free[tt - 2])
            else:
                nc.sync.wait_ge(lg.h, v_tables)
            v_x_ld[tt] = lx.dinc(nc.sync.dma_start(
                xtt[tt % 2][:, :], d_x[tt * 128:(tt + 1) * 128, :]))
            nc.vector.wait_ge(lx.h, v_x_ld[tt])
            nc.scalar.wait_ge(lx.h, v_x_ld[tt])
            ln_stats(xtt[tt % 2], sq_scr)
            nc.scalar.wait_ge(dv.h, dv.n)
            v_x_free[tt] = ac.inc(nc.scalar.activation(
                tokbf[:, :], xtt[tt % 2][:, :], AF.Identity,
                bias=sv[3][:, :], scale=sv[2][:, :]))
            if dbg and tt == 0:
                dbgt = nc.dram_tensor("dbg_tok0", [128, DM], BF16, kind="ExternalOutput").ap()
                dbgx = nc.dram_tensor("dbg_xtt0", [128, DM], F32, kind="ExternalOutput").ap()
                dbgs = nc.dram_tensor("dbg_sv0", [128, 8], F32, kind="ExternalOutput").ap()
                dbgs2 = nc.dram_tensor("dbg_sv0_late", [128, 8], F32, kind="ExternalOutput").ap()
                nc.gpsimd.wait_ge(ac.h, v_x_free[tt])
                lg.dinc(nc.gpsimd.dma_start(dbgt[:, :], tokbf[:, :]))
                lg.dinc(nc.gpsimd.dma_start(dbgx[:, :], xtt[0][:, :]))
                for i in range(6):
                    lg.dinc(nc.gpsimd.dma_start(dbgs[:, i:i + 1], sv[i][:, :], allow_slow_non_contiguous=True))
                nc.scalar.wait_ge(lg.h, lg.n)
                nc.vector.wait_ge(lg.h, lg.n)
            if dbg and tt == 1:
                nc.gpsimd.wait_ge(ac.h, ac.n)
                for i in range(6):
                    lg.dinc(nc.gpsimd.dma_start(dbgs2[:, i:i + 1], sv[i][:, :], allow_slow_non_contiguous=True))
                nc.vector.wait_ge(lg.h, lg.n)
                nc.scalar.wait_ge(lg.h, lg.n)
            transpose_tok_to_feat(tokbf, xln1T, tt, psb, ac, v_x_free[tt], tstate)
            if dbg and tt == 1:
                dbge = nc.dram_tensor("dbg_xln1T_early", [128, KC * 1024], BF16, kind="ExternalOutput").ap()
                nc.gpsimd.wait_ge(dv.h, dv.n)
                lg.dinc(nc.gpsimd.dma_start(dbge[:, :], xln1T[:, :]))
        v_xln1T = dv.n

    sig_q = sb("sig_q", [128, FCN * 1024], BF16, attn_stack)
    nc.vector.wait_ge(ls.h, v_const_ls)
    with ExitStack() as p1:
        wq_sb = [sb(f"wq{i}", [128, KC * 128], BF16, p1) for i in range(2)]
        scr_r = sb("scr_r", [128, 1024], F32, p1)
        scr_m = sb("scr_m", [128, 1024], F32, p1)
        v_wq_ld = [0] * FCN
        v_q_mm = [0] * FCN
        v_q_evac = [0] * FCN
        for it in range(FCN + 1):
            if it < FCN:
                if it >= 2:
                    nc.gpsimd.wait_ge(pe.h, v_q_mm[it - 2])
                else:
                    nc.gpsimd.wait_ge(dv.h, v_xln1T)
                v_wq_ld[it] = lg.dinc(nc.gpsimd.dma_start(
                    wq_sb[it % 2][:, :], d_wqsel[it].rearrange("p kc f -> p (kc f)")))
            fc = it - 1
            if fc < 0:
                continue
            pa, pb = (ps[0], ps[1]) if fc % 2 == 0 else (ps[2], ps[3])
            nc.tensor.wait_ge(lg.h, v_wq_ld[fc])
            if fc == 0:
                nc.tensor.wait_ge(dv.h, v_xln1T)
            if fc >= 2:
                nc.tensor.wait_ge(dv.h, v_q_evac[fc - 2])
            for half, pp in ((0, pa), (1, pb)):
                for kc in range(KC):
                    i = nc.tensor.matmul(
                        pp[:, :], wq_sb[fc % 2][:, kc * 128:(kc + 1) * 128],
                        xln1T[:, kc * 1024 + half * 512: kc * 1024 + (half + 1) * 512],
                        start=(kc == 0), stop=(kc == KC - 1))
            v_q_mm[fc] = pe.inc(i)
            nc.vector.wait_ge(pe.h, v_q_mm[fc])
            bqc = bqsel[:, fc:fc + 1]
            for half, pp in ((0, pa), (1, pb)):
                slh = slice(half * 512, (half + 1) * 512)
                dv.inc(nc.vector.tensor_scalar(scr_r[:, slh], pp[:, :], bqc, 0.0, ALU.add, ALU.max))
                dv.inc(nc.vector.tensor_scalar(scr_m[:, slh], pp[:, :], bqc, 0.0, ALU.add, ALU.min))
            nc.scalar.wait_ge(dv.h, dv.n)
            va = ac.inc(nc.scalar.activation(scr_m[:, :], scr_m[:, :], AF.Exp))
            nc.vector.wait_ge(ac.h, va)
            v_q_evac[fc] = dv.inc(nc.vector.tensor_tensor(
                sig_q[:, fc * 1024:(fc + 1) * 1024], scr_r[:, :], scr_m[:, :], ALU.add))

    if dbg:
        dbg1 = nc.dram_tensor("dbg_xln1T", [128, KC * 1024], BF16, kind="ExternalOutput").ap()
        dbg2 = nc.dram_tensor("dbg_sigq", [128, FCN * 1024], BF16, kind="ExternalOutput").ap()
        nc.gpsimd.wait_ge(dv.h, dv.n)
        lg.dinc(nc.gpsimd.dma_start(dbg1[:, :], xln1T[:, :]))
        lg.dinc(nc.gpsimd.dma_start(dbg2[:, :], sig_q[:, :]))

    ret = sb("ret", [128, HKV * 1024], BF16, attn_stack)
    with ExitStack() as p2:
        den_sb = sb("den_sb", [1, 1024], F32, p2)
        rden_bf = sb("rden_bf", [1, 1024], BF16, p2)
        rb_sb = sb("rb_sb", [128, 1024], BF16, p2)
        nc.tensor.wait_ge(lg.h, v_const_lg)
        nc.tensor.wait_ge(dv.h, dv.n)
        for h in range(HKV):
            sq = sig_q[:, h * 1024:(h + 1) * 1024]
            for half, pp in ((0, ps[0]), (1, ps[1])):
                i = nc.tensor.matmul(pp[:, :], m_sb[:, h * 128:(h + 1) * 128],
                                     sq[:, half * 512:(half + 1) * 512],
                                     start=True, stop=True)
            for half, pp in ((0, ps[2]), (1, ps[3])):
                i = nc.tensor.matmul(pp[0:1, :], z_sb[:, h:h + 1],
                                     sq[:, half * 512:(half + 1) * 512],
                                     start=True, stop=True)
            v_nd = pe.inc(i)
            nc.vector.wait_ge(pe.h, v_nd)
            dv.inc(nc.vector.tensor_scalar(den_sb[:, 0:512], ps[2][0:1, :], EPS_DEN, None, ALU.add))
            dv.inc(nc.vector.tensor_scalar(den_sb[:, 512:1024], ps[3][0:1, :], EPS_DEN, None, ALU.add))
            dv.inc(nc.vector.reciprocal(den_sb[:, :], den_sb[:, :]))
            v_rd = dv.inc(nc.vector.tensor_copy(rden_bf[:, :], den_sb[:, :]))
            nc.tensor.wait_ge(dv.h, v_rd)
            for half, pp in ((0, ps[4]), (1, ps[5])):
                i = nc.tensor.matmul(pp[:, :], ones1[0:1, :],
                                     rden_bf[0:1, half * 512:(half + 1) * 512],
                                     start=True, stop=True)
            v_bc = pe.inc(i)
            nc.vector.wait_ge(pe.h, v_bc)
            dv.inc(nc.vector.tensor_copy(rb_sb[:, 0:512], ps[4][:, :]))
            dv.inc(nc.vector.tensor_copy(rb_sb[:, 512:1024], ps[5][:, :]))
            dv.inc(nc.vector.tensor_tensor(ret[:, h * 1024:h * 1024 + 512],
                                           ps[0][:, :], rb_sb[:, 0:512], ALU.mult))
            v_ret = dv.inc(nc.vector.tensor_tensor(
                ret[:, h * 1024 + 512:(h + 1) * 1024],
                ps[1][:, :], rb_sb[:, 512:1024], ALU.mult))
            nc.tensor.wait_ge(dv.h, v_ret)
        v_p2_done = dv.n

    if dbg:
        dbg3 = nc.dram_tensor("dbg_ret", [128, HKV * 1024], BF16, kind="ExternalOutput").ap()
        nc.gpsimd.wait_ge(dv.h, dv.n)
        lg.dinc(nc.gpsimd.dma_start(dbg3[:, :], ret[:, :]))

    with ExitStack() as p3:
        wo_sb = [sb(f"wo{i}", [128, KC * 128], BF16, p3) for i in range(2)]
        woeff = [sb(f"woe{i}", [128, HKV * 128], BF16, p3) for i in range(2)]
        ao_sb = [sb(f"ao{i}", [128, 1024], F32, p3) for i in range(2)]
        v_wo_ld = [0] * OCN
        v_wo_add = [0] * OCN
        v_wo_mm = [0] * OCN
        v_wo_evac = [0] * OCN
        v_wo_spill = [0] * OCN
        for it in range(OCN + 1):
            if it < OCN:
                if it >= 2:
                    nc.gpsimd.wait_ge(dv.h, v_wo_add[it - 2])
                else:
                    nc.gpsimd.wait_ge(dv.h, v_p2_done)
                v_wo_ld[it] = lg.dinc(nc.gpsimd.dma_start(
                    wo_sb[it % 2][:, :], d_wo[it].rearrange("p fc o -> p (fc o)")))
            oc = it - 1
            if oc < 0:
                continue
            w = wo_sb[oc % 2]
            we = woeff[oc % 2]
            nc.vector.wait_ge(lg.h, v_wo_ld[oc])
            if oc >= 2:
                nc.vector.wait_ge(pe.h, v_wo_mm[oc - 2])
            for hk in range(HKV):
                b = 4 * hk * 128
                dsl = slice(hk * 128, (hk + 1) * 128)
                dv.inc(nc.vector.tensor_tensor(we[:, dsl], w[:, b:b + 128],
                                               w[:, b + 128:b + 256], ALU.add))
                dv.inc(nc.vector.tensor_tensor(we[:, dsl], we[:, dsl],
                                               w[:, b + 256:b + 384], ALU.add))
                dv.inc(nc.vector.tensor_tensor(we[:, dsl], we[:, dsl],
                                               w[:, b + 384:b + 512], ALU.add))
            v_wo_add[oc] = dv.n
            pa, pb = (ps[0], ps[1]) if oc % 2 == 0 else (ps[2], ps[3])
            nc.tensor.wait_ge(dv.h, v_wo_add[oc])
            if oc >= 2:
                nc.tensor.wait_ge(dv.h, v_wo_evac[oc - 2])
            for half, pp in ((0, pa), (1, pb)):
                for hk in range(HKV):
                    i = nc.tensor.matmul(
                        pp[:, :], we[:, hk * 128:(hk + 1) * 128],
                        ret[:, hk * 1024 + half * 512: hk * 1024 + (half + 1) * 512],
                        start=(hk == 0), stop=(hk == HKV - 1))
            v_wo_mm[oc] = pe.inc(i)
            nc.vector.wait_ge(pe.h, v_wo_mm[oc])
            if oc >= 2:
                nc.vector.wait_ge(lg.h, v_wo_spill[oc - 2])
            dv.inc(nc.vector.tensor_copy(ao_sb[oc % 2][:, 0:512], pa[:, :]))
            v_wo_evac[oc] = dv.inc(nc.vector.tensor_copy(ao_sb[oc % 2][:, 512:1024], pb[:, :]))
            nc.gpsimd.wait_ge(dv.h, v_wo_evac[oc])
            v_wo_spill[oc] = lg.dinc(nc.gpsimd.dma_start(s_attn[oc], ao_sb[oc % 2][:, :]))
        v_attn_done = v_wo_spill[OCN - 1]

    attn_stack.close()
    ffn_stack = ExitStack()
    xln2T = sb("xln2T", [128, KC * 1024], BF16, ffn_stack)
    with ExitStack() as p4:
        ablk = [sb(f"ablk{i}", [128, DM], F32, p4) for i in range(2)]
        xtt = [sb(f"xtt4{i}", [128, DM], F32, p4) for i in range(2)]
        atok = sb("atok", [128, DM], F32, p4)
        sq_scr = sb("sq4", [128, DM], F32, p4)
        tokbf = sb("tokbf4", [128, DM], BF16, p4)
        x1sb = sb("x1sb", [128, DM], F32, p4)
        psb = [psum(f"psb4{i}", [128, 512], BF16, p4) for i in range(2)]
        tstate = [0, 0]
        pst = [ps[0], ps[1]]
        astate = [0, 0]
        v_a_ld = [0] * NTT
        v_a_free = [0] * NTT
        v_x1_sp = [0] * NTT
        nc.sync.wait_ge(lg.h, v_attn_done)
        for it in range(NTT + 1):
            if it < NTT:
                if it >= 2:
                    nc.sync.wait_ge(dv.h, v_a_free[it - 2])
                ls.dinc(nc.sync.dma_start(
                    ablk[it % 2][:, :].rearrange("p (oc t) -> p oc t", t=128),
                    s_attn[:, :, it * 128:(it + 1) * 128].transpose([1, 0, 2])))
                v_a_ld[it] = ls.dinc(nc.sync.dma_start(
                    xtt[it % 2][:, :], d_x[it * 128:(it + 1) * 128, :]))
            tt = it - 1
            if tt < 0:
                continue
            for k4 in range(KC // 4):
                pb = pst[k4 % 2]
                nc.tensor.wait_ge(ls.h, v_a_ld[tt])
                if astate[k4 % 2]:
                    nc.tensor.wait_ge(dv.h, astate[k4 % 2])
                for j in range(4):
                    oc = k4 * 4 + j
                    i = nc.tensor.transpose(
                        pb[:, j * 128:(j + 1) * 128],
                        ablk[tt % 2][:, oc * 128:(oc + 1) * 128], identf[:, :])
                vp = pe.inc(i)
                nc.vector.wait_ge(pe.h, vp)
                astate[k4 % 2] = dv.inc(nc.vector.tensor_copy(
                    atok[:, :].rearrange("p (k t) -> p k t", t=128)[:, k4 * 4:k4 * 4 + 4, :],
                    pb[:, :].rearrange("p (j t) -> p j t", t=128)))
            ln_stats(atok, sq_scr)
            nc.scalar.wait_ge(dv.h, dv.n)
            va = ac.inc(nc.scalar.activation(sq_scr[:, :], atok[:, :], AF.Identity,
                                             bias=sv[3][:, :], scale=sv[2][:, :]))
            nc.vector.wait_ge(ac.h, va)
            nc.vector.wait_ge(ls.h, v_a_ld[tt])
            vx1 = dv.inc(nc.vector.tensor_tensor(x1sb[:, :], xtt[tt % 2][:, :],
                                                 sq_scr[:, :], ALU.add))
            v_a_free[tt] = vx1
            nc.gpsimd.wait_ge(dv.h, vx1)
            v_x1_sp[tt] = lg.dinc(nc.gpsimd.dma_start(
                s_x1[tt * 128:(tt + 1) * 128, :], x1sb[:, :]))
            ln_stats(x1sb, sq_scr)
            nc.scalar.wait_ge(dv.h, dv.n)
            va = ac.inc(nc.scalar.activation(tokbf[:, :], x1sb[:, :], AF.Identity,
                                             bias=sv[3][:, :], scale=sv[2][:, :]))
            transpose_tok_to_feat(tokbf, xln2T, tt, psb, ac, va, tstate)
        v_x1_done = v_x1_sp[NTT - 1]

    if dbg:
        dbg4 = nc.dram_tensor("dbg_xln2T", [128, KC * 1024], BF16, kind="ExternalOutput").ap()
        nc.gpsimd.wait_ge(dv.h, dv.n)
        lg.dinc(nc.gpsimd.dma_start(dbg4[:, :], xln2T[:, :]))

    xgT = sb("xgT", [32, 1024], BF16, ffn_stack)
    nc.tensor.wait_ge(dv.h, dv.n)
    for half, pp in ((0, ps[0]), (1, ps[1])):
        for kc in range(KC):
            i = nc.tensor.matmul(
                pp[0:32, :], oneh32[:, kc * 32:(kc + 1) * 32],
                xln2T[:, kc * 1024 + half * 512: kc * 1024 + (half + 1) * 512],
                start=(kc == 0), stop=(kc == KC - 1))
    v5 = pe.inc(i)
    nc.vector.wait_ge(pe.h, v5)
    dv.inc(nc.vector.tensor_copy(xgT[:, 0:512], ps[0][0:32, :]))
    v_xg = dv.inc(nc.vector.tensor_copy(xgT[:, 512:1024], ps[1][0:32, :]))

    with ExitStack() as p6:
        ps6 = [psum(f"ps6{i}", [128, 512], F32, p6) for i in range(2)]
        pall = ps + ps6
        NS6 = 3
        gub = [[sb(f"byt{m}{i}", [128, 2048], U8, p6) for i in range(NS6)] for m in range(2)]
        zssl = [[sb(f"zssl{m}{i}", [32, 128], BF16, p6) for i in range(NS6)] for m in range(2)]
        sbc = [[sb(f"sbc{m}{i}", [128, DM], BF16, p6) for i in range(NS6)] for m in range(2)]
        wgu = [[sb(f"wgu{m}{i}", [128, DM], BF16, p6) for i in range(NS6)] for m in range(2)]
        nib = [sb(f"nib{m}", [128, DM], U8, p6) for m in range(2)]
        silu_sb = sb("silu_sb", [128, 1024], BF16, p6)
        h_sb = [sb(f"hsb{i}", [128, 1024], BF16, p6) for i in range(2)]
        nc.sync.wait_ge(lg.h, v_tables)
        v6_ld = [0] * CC
        v6_deq = [0] * CC
        v6_mm = [0] * CC
        v6_h = [0] * CC
        v6_hs = [0] * CC
        PF = 2
        srcs = [(d_gb, s_sgbblk[0], s_zs[0]), (d_ub, s_sgbblk[1], s_zs[1])]
        for it in range(CC + PF):
            c = it
            if c < CC:
                slc = c % NS6
                if c >= NS6:
                    nc.sync.wait_ge(dv.h, v6_deq[c - NS6])
                    nc.sync.wait_ge(pe.h, v6_mm[c - NS6])
                elif c == 0:
                    nc.sync.wait_ge(dv.h, v_xg)
                for m, (dby, dsgb, dzs) in enumerate(srcs):
                    ls.dinc(nc.sync.dma_start(
                        gub[m][slc][:, :], dby[c].rearrange("p kc b -> p (kc b)")))
                    ls.dinc(nc.sync.dma_start(
                        zssl[m][slc][:, :], dzs[:, c * 128:(c + 1) * 128]))
                    ls.dinc(nc.sync.dma_start(
                        sbc[m][slc][:, :],
                        dsgb[c].rearrange("g i -> (g i)").unsqueeze(0)
                        .partition_broadcast(128).squeeze(1)))
                v6_ld[c] = ls.n
            cd = it - 1
            if 0 <= cd < CC:
                slc = cd % NS6
                nc.vector.wait_ge(ls.h, v6_ld[cd])
                if cd >= NS6:
                    nc.vector.wait_ge(pe.h, v6_mm[cd - NS6])
                for m in range(2):
                    bt = gub[m][slc][:, :].rearrange("p (kc b) -> p kc b", kc=KC)
                    nv = nib[m][:, :].rearrange("p (kc b two) -> p kc b two", kc=KC, two=2)
                    dv.inc(nc.vector.tensor_scalar(nv[:, :, :, 0], bt, 15, None, ALU.bitwise_and))
                    dv.inc(nc.vector.tensor_scalar(nv[:, :, :, 1], bt, 4, None, ALU.logical_shift_right))
                    dv.inc(nc.vector.scalar_tensor_tensor(
                        wgu[m][slc][:, :], nib[m][:, :], QS, sbc[m][slc][:, :],
                        ALU.subtract, ALU.mult))
                v6_deq[cd] = dv.n
            cc = it - PF
            if cc < 0:
                continue
            slc = cc % NS6
            pbank = pall[0:4] if cc % 2 == 0 else pall[4:8]
            nc.tensor.wait_ge(dv.h, v6_deq[cc])
            if cc >= 2:
                nc.tensor.wait_ge(dv.h, v6_h[cc - 2])
            for m in range(2):
                for half in range(2):
                    pp = pbank[m * 2 + half]
                    i = nc.tensor.matmul(pp[:, :], zssl[m][slc][:, :],
                                         xgT[:, half * 512:(half + 1) * 512],
                                         start=True, stop=False)
                    for kc in range(KC):
                        i = nc.tensor.matmul(
                            pp[:, :], wgu[m][slc][:, kc * 128:(kc + 1) * 128],
                            xln2T[:, kc * 1024 + half * 512: kc * 1024 + (half + 1) * 512],
                            start=False, stop=(kc == KC - 1))
            v6_mm[cc] = pe.inc(i)
            nc.scalar.wait_ge(pe.h, v6_mm[cc])
            ac.inc(nc.scalar.activation(silu_sb[:, 0:512], pbank[0][:, :], AF.Silu))
            v_silu = ac.inc(nc.scalar.activation(silu_sb[:, 512:1024], pbank[1][:, :], AF.Silu))
            nc.vector.wait_ge(ac.h, v_silu)
            hsl = cc % 2
            if cc >= 2:
                nc.vector.wait_ge(lg.h, v6_hs[cc - 2])
            dv.inc(nc.vector.tensor_tensor(h_sb[hsl][:, 0:512], silu_sb[:, 0:512],
                                           pbank[2][:, :], ALU.mult))
            v6_h[cc] = dv.inc(nc.vector.tensor_tensor(
                h_sb[hsl][:, 512:1024], silu_sb[:, 512:1024], pbank[3][:, :], ALU.mult))
            nc.gpsimd.wait_ge(dv.h, v6_h[cc])
            v6_hs[cc] = lg.dinc(nc.gpsimd.dma_start(s_h[cc], h_sb[hsl][:, :]))
        v_h_done = v6_hs[CC - 1]

    ffn_stack.close()
    hgT_bf = sb("hgT_bf", [CC, 1024], BF16)
    with ExitStack() as p7:
        wd = [[sb(f"wd{b}_{j}", [128, 2048], BF16, p7) for j in range(DB)] for b in range(2)]
        wdby = [sb(f"wdby{j}", [128, 1024], U8, p7) for j in range(DB)]
        h7 = [[sb(f"h7{b}_{j}", [128, 1024], BF16, p7) for j in range(DB)] for b in range(2)]
        sdbc = [sb(f"sdbc{j}", [128, 2048], BF16, p7) for j in range(DB)]
        nibd = sb("nibd", [128, 2048], U8, p7)
        oneh88 = sb("oneh88", [128, DB * CC], BF16, p7)
        zsd_sb = sb("zsd_sb", [CC, 2048], BF16, p7)
        stg = [sb(f"stg{i}", [128, 1024], F32, p7) for i in range(2)]
        nc.sync.wait_ge(lg.h, v_h_done)

        pd = ps[0:4]
        phg = (ps[4][0:CC, :], ps[5][0:CC, :])
        NG = 2 * NR

        v_acc = {}
        v_stg = {}
        v_evac = {}
        last_deq = 0
        v_band_ld = {}
        v_deq = {}
        v_mmband = {}
        v_hg_pe = 0
        v_hgT = 0
        v_zsd_ld = [0, 0]

        def issue_band_loads(g):
            nonlocal last_deq
            ph, r = g // NR, g % NR
            buf = g % 2
            if last_deq:
                nc.sync.wait_ge(dv.h, last_deq)
            if g >= 2:
                nc.sync.wait_ge(pe.h, v_mmband[g - 2])
            for j in range(DB):
                c = r * DB + j
                ls.dinc(nc.sync.dma_start(
                    wdby[j][:, :], d_db[c][:, ph * 1024:(ph + 1) * 1024]))
                ls.dinc(nc.sync.dma_start(h7[buf][j][:, :], s_h[c]))
                ls.dinc(nc.sync.dma_start(
                    sdbc[j][:, :],
                    s_dsb[c:c + 1, ph * 2048:(ph + 1) * 2048]
                    .partition_broadcast(128).squeeze(1)))
            v_band_ld[g] = ls.n

        def deq_band(g):
            nonlocal last_deq
            buf = g % 2
            nc.vector.wait_ge(ls.h, v_band_ld[g])
            if g >= 2:
                nc.vector.wait_ge(pe.h, v_mmband[g - 2])
            for j in range(DB):
                nv = nibd[:, :].rearrange("p (b two) -> p b two", two=2)
                dv.inc(nc.vector.tensor_scalar(nv[:, :, 0], wdby[j][:, :], 15, None, ALU.bitwise_and))
                dv.inc(nc.vector.tensor_scalar(nv[:, :, 1], wdby[j][:, :], 4, None, ALU.logical_shift_right))
                dv.inc(nc.vector.scalar_tensor_tensor(
                    wd[buf][j][:, :], nibd[:, :], QS, sdbc[j][:, :],
                    ALU.subtract, ALU.mult))
            last_deq = dv.n
            v_deq[g] = dv.n

        def mm_round(g):
            nonlocal v_hg_pe, v_hgT
            ph, r = g // NR, g % NR
            buf = g % 2
            if ph == 0:
                nc.vector.wait_ge(pe.h, v_hg_pe)
                dv.inc(nc.vector.memset(oneh88[:, :], 0.0))
                for j in range(DB):
                    c = r * DB + j
                    dv.inc(nc.vector.memset(oneh88[:, j * CC + c:j * CC + c + 1], 1.0))
                dpad(16)
                v_oneh = dv.inc(nc.vector.engine_nop())
                nc.tensor.wait_ge(dv.h, v_oneh)
                nc.tensor.wait_ge(ls.h, v_band_ld[g])
                for j in range(DB):
                    for half, pp in ((0, phg[0]), (1, phg[1])):
                        i = nc.tensor.matmul(
                            pp, oneh88[:, j * CC:(j + 1) * CC],
                            h7[buf][j][:, half * 512:(half + 1) * 512],
                            start=(r == 0 and j == 0),
                            stop=(r == NR - 1 and j == DB - 1))
                v_hg_pe = pe.inc(i)
                if r == NR - 1:
                    nc.vector.wait_ge(pe.h, v_hg_pe)
                    dv.inc(nc.vector.tensor_copy(hgT_bf[:, 0:512], phg[0]))
                    v_hgT = dv.inc(nc.vector.tensor_copy(hgT_bf[:, 512:1024], phg[1]))
            nc.tensor.wait_ge(dv.h, v_deq[g])
            nc.tensor.wait_ge(ls.h, v_band_ld[g])
            if g % NR == NR - 1:
                nc.tensor.wait_ge(dv.h, v_hgT)
                nc.tensor.wait_ge(ls.h, v_zsd_ld[ph])
            for ol in range(16):
                oc = ph * 16 + ol
                pair = ol % 2
                if pair in v_evac:
                    nc.tensor.wait_ge(dv.h, v_evac[pair])
                ppa, ppb = pd[pair * 2], pd[pair * 2 + 1]
                for half, pp in ((0, ppa), (1, ppb)):
                    for j in range(DB):
                        i = nc.tensor.matmul(
                            pp[:, :], wd[buf][j][:, ol * 128:(ol + 1) * 128],
                            h7[buf][j][:, half * 512:(half + 1) * 512],
                            start=(j == 0), stop=(j == DB - 1 and r != NR - 1))
                    if r == NR - 1:
                        i = nc.tensor.matmul(
                            pp[:, :], zsd_sb[:, ol * 128:(ol + 1) * 128],
                            hgT_bf[:, half * 512:(half + 1) * 512],
                            start=False, stop=True)
                v_mmoc = pe.inc(i)
                nc.vector.wait_ge(pe.h, v_mmoc)
                slot = ol % 2
                if slot in v_stg:
                    nc.vector.wait_ge(lg.h, v_stg[slot])
                sg = stg[slot]
                dv.inc(nc.vector.tensor_copy(sg[:, 0:512], ppa[:, :]))
                v_ev = dv.inc(nc.vector.tensor_copy(sg[:, 512:1024], ppb[:, :]))
                v_evac[pair] = v_ev
                nc.gpsimd.wait_ge(dv.h, v_ev)
                if oc in v_acc:
                    nc.gpsimd.wait_ge(lg.h, v_acc[oc])
                op = ALU.bypass if r == 0 else ALU.add
                v_acc[oc] = lg.dinc(nc.gpsimd.dma_start(
                    s_oacc[oc], sg[:, :], accum_op=op))
                v_stg[slot] = v_acc[oc]
            v_mmband[g] = pe.n

        v_zsd_ld[0] = ls.dinc(nc.sync.dma_start(
            zsd_sb[:, :], s_zsd[:, 0:2048]))
        issue_band_loads(0)
        deq_band(0)
        issue_band_loads(1)
        for g in range(NG):
            if g + 1 < NG:
                deq_band(g + 1)
            mm_round(g)
            if g == NR - 1:
                nc.sync.wait_ge(pe.h, pe.n)
                v_zsd_ld[1] = ls.dinc(nc.sync.dma_start(
                    zsd_sb[:, :], s_zsd[:, 2048:4096]))
            if g + 2 < NG:
                issue_band_loads(g + 2)
        v_oacc_done = max(v_acc.values())

    with ExitStack() as p8:
        oblk = [sb(f"oblk{i}", [128, DM], F32, p8) for i in range(2)]
        x1tt = [sb(f"x1tt{i}", [128, DM], F32, p8) for i in range(2)]
        ftok = sb("ftok", [128, DM], F32, p8)
        sq_scr = sb("sq8", [128, DM], F32, p8)
        out_sb = [sb(f"outsb{i}", [128, DM], F32, p8) for i in range(2)]
        pst = [ps[0], ps[1]]
        ostate = [0, 0]
        v_o_ld = [0] * NTT
        v_o_free = [0] * NTT
        v_out_sp = [0] * NTT
        nc.sync.wait_ge(lg.h, v_oacc_done)
        for it in range(NTT + 1):
            if it < NTT:
                if it >= 2:
                    nc.sync.wait_ge(dv.h, v_o_free[it - 2])
                ls.dinc(nc.sync.dma_start(
                    oblk[it % 2][:, :].rearrange("p (oc t) -> p oc t", t=128),
                    s_oacc[:, :, it * 128:(it + 1) * 128].transpose([1, 0, 2])))
                v_o_ld[it] = ls.dinc(nc.sync.dma_start(
                    x1tt[it % 2][:, :], s_x1[it * 128:(it + 1) * 128, :]))
            tt = it - 1
            if tt < 0:
                continue
            for k4 in range(KC // 4):
                pb = pst[k4 % 2]
                nc.tensor.wait_ge(ls.h, v_o_ld[tt])
                if ostate[k4 % 2]:
                    nc.tensor.wait_ge(dv.h, ostate[k4 % 2])
                for j in range(4):
                    oc = k4 * 4 + j
                    i = nc.tensor.transpose(
                        pb[:, j * 128:(j + 1) * 128],
                        oblk[tt % 2][:, oc * 128:(oc + 1) * 128], identf[:, :])
                vp = pe.inc(i)
                nc.vector.wait_ge(pe.h, vp)
                ostate[k4 % 2] = dv.inc(nc.vector.tensor_copy(
                    ftok[:, :].rearrange("p (k t) -> p k t", t=128)[:, k4 * 4:k4 * 4 + 4, :],
                    pb[:, :].rearrange("p (j t) -> p j t", t=128)))
            ln_stats(ftok, sq_scr)
            nc.scalar.wait_ge(dv.h, dv.n)
            va = ac.inc(nc.scalar.activation(sq_scr[:, :], ftok[:, :], AF.Identity,
                                             bias=sv[3][:, :], scale=sv[2][:, :]))
            nc.vector.wait_ge(ac.h, va)
            nc.vector.wait_ge(ls.h, v_o_ld[tt])
            if tt >= 2:
                nc.vector.wait_ge(st.h, v_out_sp[tt - 2])
            v_o_free[tt] = dv.inc(nc.vector.tensor_tensor(
                out_sb[tt % 2][:, :], x1tt[tt % 2][:, :], sq_scr[:, :], ALU.add))
            nc.gpsimd.wait_ge(dv.h, v_o_free[tt])
            v_out_sp[tt] = st.dinc(nc.gpsimd.dma_start(
                d_out[tt * 128:(tt + 1) * 128, :], out_sb[tt % 2][:, :]))
        nc.gpsimd.wait_ge(st.h, v_out_sp[NTT - 1])

    top.close()
    return nc


_NC_CACHE = None


def kernel(**inputs):
    global _NC_CACHE
    x_shards, common = _host_prep(inputs)
    if _NC_CACHE is None:
        _NC_CACHE = build_program()
    nc = _NC_CACHE
    in_maps = [dict(common, x=x_shards[c]) for c in range(NCORE)]
    res = run_bass_kernel_spmd(nc, in_maps, core_ids=list(range(NCORE)))
    out = np.concatenate([np.asarray(r["out"]) for r in res.results], axis=0)
    return out.reshape(4, 2048, DM).astype(np.float32)


# version 10
# speedup vs baseline: 1.0008x; 1.0008x over previous
import os
import sys

if os.path.isdir("/opt/trn_rl_repo") and "/opt/trn_rl_repo" not in sys.path:
    sys.path.insert(0, "/opt/trn_rl_repo")

from contextlib import ExitStack

import numpy as np

import concourse.bass as bass
import concourse.mybir as mybir
from concourse.bass_utils import run_bass_kernel_spmd

F32 = mybir.dt.float32
BF16 = mybir.dt.bfloat16
U8 = mybir.dt.uint8
AF = mybir.ActivationFunctionType
ALU = mybir.AluOpType
AX = mybir.AxisListType

NCORE = 8
T = 1024
NTT = 8
DM = 4096
KC = 32
FCN = 8
OCN = 32
CC = 88
IPAD = CC * 128
HKV = 8
DB = 8
NR = CC // DB
EPS_LN = 1e-5
EPS_DEN = 1e-6
QS = 7.5


def _host_prep(inputs):
    f32 = np.float32
    X = np.asarray(inputs["L0_attn_hidden_states"], dtype=f32).reshape(NCORE * T, DM)
    x_shards = [np.ascontiguousarray(X[c * T:(c + 1) * T]) for c in range(NCORE)]

    Wq = np.asarray(inputs["L0_attn_Wq"], dtype=f32)
    wq3 = Wq.reshape(32, 128, DM)[0::4]
    wqsel = np.ascontiguousarray(
        wq3.transpose(0, 2, 1).reshape(FCN, KC, 128, 128).transpose(0, 2, 1, 3))
    bq = np.asarray(inputs["L0_attn_bq"], dtype=f32)
    bqsel = np.ascontiguousarray(bq.reshape(32, 128)[0::4].T)

    Wo = np.asarray(inputs["L0_attn_Wo"], dtype=f32)
    wo_blk = np.ascontiguousarray(Wo.reshape(OCN, 128, KC, 128).transpose(0, 3, 2, 1))

    M = np.asarray(inputs["L0_attn_M"], dtype=f32)
    m_host = np.ascontiguousarray(M.transpose(1, 0, 2).reshape(128, HKV * 128))
    z_host = np.ascontiguousarray(np.asarray(inputs["L0_attn_z"], dtype=f32).T)

    def _gu_bytes(qw):
        qv = np.ascontiguousarray(np.asarray(qw)).view(np.uint8)
        qv = np.concatenate(
            [qv, np.zeros((DM, IPAD // 2 - qv.shape[1]), np.uint8)], axis=1)
        return np.ascontiguousarray(
            qv.reshape(KC, 128, CC, 64).transpose(2, 1, 0, 3))

    g_bytes = _gu_bytes(inputs["L0_ffn_gate_qweight"])
    u_bytes = _gu_bytes(inputs["L0_ffn_up_qweight"])

    dqv = np.ascontiguousarray(np.asarray(inputs["L0_ffn_down_qweight"])).view(np.uint8)
    dqv = np.concatenate([dqv, np.zeros((IPAD - dqv.shape[0], 2048), np.uint8)], axis=0)
    d_bytes = np.ascontiguousarray(dqv.reshape(CC, 128, 2048))

    g_sc = np.zeros((32, IPAD), f32)
    g_sc[:, :11008] = np.asarray(inputs["L0_ffn_gate_scales"], dtype=f32)
    u_sc = np.zeros((32, IPAD), f32)
    u_sc[:, :11008] = np.asarray(inputs["L0_ffn_up_scales"], dtype=f32)
    d_sc = np.zeros((CC, DM), f32)
    d_sc[:86] = np.asarray(inputs["L0_ffn_down_scales"], dtype=f32)

    def _zb(qz, rows, cols):
        v = np.ascontiguousarray(np.asarray(qz)).view(np.uint8)
        out = np.zeros((rows, cols), np.uint8)
        out[:v.shape[0], :v.shape[1]] = v
        return out

    g_zb = _zb(inputs["L0_ffn_gate_qzeros"], 32, IPAD // 2)
    u_zb = _zb(inputs["L0_ffn_up_qzeros"], 32, IPAD // 2)
    d_zb = _zb(inputs["L0_ffn_down_qzeros"], CC, 2048)

    import ml_dtypes
    ident_bf = np.eye(128, dtype=ml_dtypes.bfloat16)
    ident_f32 = np.eye(128, dtype=np.float32)

    common = dict(
        wqsel=wqsel, bqsel=bqsel, wo_blk=wo_blk, m_h=m_host, z_h=z_host,
        g_bytes=g_bytes, u_bytes=u_bytes, d_bytes=d_bytes,
        g_sc=g_sc, u_sc=u_sc, d_sc=d_sc, g_zb=g_zb, u_zb=u_zb, d_zb=d_zb,
        ident_bf=ident_bf, ident_f32=ident_f32)
    return x_shards, common


class Sem:
    def __init__(self, nc, stack, name):
        self.h = stack.enter_context(nc.semaphore(name))
        self.n = 0

    def inc(self, instr, k=1):
        self.n += k
        instr.then_inc(self.h, k)
        return self.n

    def dinc(self, instr):
        return self.inc(instr, 16)


def build_program(dbg=False):
    nc = bass.Bass("TRN2", target_bir_lowering=False)

    def din(name, shape, dt):
        return nc.dram_tensor(name, shape, dt, kind="ExternalInput").ap()

    d_x = din("x", [T, DM], F32)
    d_wqsel = din("wqsel", [FCN, 128, KC, 128], F32)
    d_bqsel = din("bqsel", [128, FCN], F32)
    d_wo = din("wo_blk", [OCN, 128, KC, 128], F32)
    d_m = din("m_h", [128, HKV * 128], F32)
    d_z = din("z_h", [128, HKV], F32)
    d_gb = din("g_bytes", [CC, 128, KC, 64], U8)
    d_ub = din("u_bytes", [CC, 128, KC, 64], U8)
    d_db = din("d_bytes", [CC, 128, 2048], U8)
    d_gsc = din("g_sc", [32, IPAD], F32)
    d_usc = din("u_sc", [32, IPAD], F32)
    d_dsc = din("d_sc", [CC, DM], F32)
    d_gzb = din("g_zb", [32, IPAD // 2], U8)
    d_uzb = din("u_zb", [32, IPAD // 2], U8)
    d_dzb = din("d_zb", [CC, 2048], U8)
    d_identb = din("ident_bf", [128, 128], BF16)
    d_identf = din("ident_f32", [128, 128], F32)
    d_out = nc.dram_tensor("out", [T, DM], F32, kind="ExternalOutput").ap()

    def dscr(name, shape, dt):
        kind = "ExternalOutput" if dbg else "Internal"
        return nc.dram_tensor(name, shape, dt, kind=kind).ap()

    s_x1 = dscr("x1_scr", [T, DM], F32)
    s_attn = dscr("attn_scr", [OCN, 128, T], F32)
    s_h = dscr("h_scr", [CC, 128, T], BF16)
    s_oacc = dscr("oacc_scr", [OCN, 128, T], F32)
    s_sgb = dscr("sgb_scr", [2, 32, IPAD], BF16)
    s_sgbblk = dscr("sgbblk_scr", [2, CC, 32, 128], BF16)
    s_zs = dscr("zs_scr", [2, 32, IPAD], BF16)
    s_dsb = dscr("dsb_scr", [CC, DM], BF16)
    s_zsd = dscr("zsd_scr", [CC, DM], BF16)

    top = ExitStack()

    def sb(name, shape, dt, stk=None):
        return (stk or top).enter_context(nc.sbuf_tensor(name, shape, dt))

    def psum(name, shape, dt, stk=None):
        return (stk or top).enter_context(nc.psum_tensor(name, shape, dt))

    ls = Sem(nc, top, "ls")
    lg = Sem(nc, top, "lg")
    pe = Sem(nc, top, "pe")
    dv = Sem(nc, top, "dv")
    ac = Sem(nc, top, "ac")
    st = Sem(nc, top, "st")
    lx = Sem(nc, top, "lx")

    ps = [psum(f"ps{i}", [128, 512], F32) for i in range(6)]

    identb = sb("identb", [128, 128], BF16)
    identf = sb("identf", [128, 128], F32)
    bqsel = sb("bqsel_sb", [128, FCN], F32)
    m_sb = sb("m_sb", [128, HKV * 128], BF16)
    z_sb = sb("z_sb", [128, HKV], BF16)
    ones1 = sb("ones1", [1, 128], BF16)
    oneh32 = sb("oneh32", [128, KC * 32], BF16)
    sv = [sb(f"sv{i}", [128, 1], F32) for i in range(6)]
    svp = sb("svp", [128, 1], F32)

    ls.dinc(nc.sync.dma_start(identb[:, :], d_identb[:, :]))
    ls.dinc(nc.sync.dma_start(identf[:, :], d_identf[:, :]))
    ls.dinc(nc.sync.dma_start(bqsel[:, :], d_bqsel[:, :]))
    lg.dinc(nc.gpsimd.dma_start(m_sb[:, :], d_m[:, :]))
    v_const_lg = lg.dinc(nc.gpsimd.dma_start(z_sb[:, :], d_z[:, :]))
    v_const_ls = ls.n

    dv.inc(nc.vector.memset(ones1[:, :], 1.0))
    dv.inc(nc.vector.memset(oneh32[:, :], 0.0))
    for kc in range(KC):
        dv.inc(nc.vector.memset(oneh32[:, kc * 32 + kc:kc * 32 + kc + 1], 1.0))
    for _ in range(16):
        nc.vector.engine_nop()
    dv.inc(nc.vector.engine_nop())

    with ExitStack() as p0a:
        sgbuf = sb("sgbuf", [32, IPAD], BF16, p0a)
        zbuf = sb("zbuf", [32, IPAD // 2], U8, p0a)
        znib = sb("znib", [32, IPAD], U8, p0a)
        zsbuf = sb("zsbuf", [32, IPAD], BF16, p0a)
        for mat, (dsc_, dzb_) in enumerate([(d_gsc, d_gzb), (d_usc, d_uzb)]):
            if mat > 0:
                nc.sync.wait_ge(dv.h, dv.n)
            vl = lg.dinc(nc.gpsimd.dma_start(sgbuf[:, :], dsc_[:, :]))
            vz = ls.dinc(nc.sync.dma_start(zbuf[:, :], dzb_[:, :]))
            nc.vector.wait_ge(lg.h, vl)
            nc.vector.wait_ge(ls.h, vz)
            znv = znib[:, :].rearrange("p (b two) -> p b two", two=2)
            dv.inc(nc.vector.tensor_scalar(znv[:, :, 0], zbuf[:, :], 15, None, ALU.bitwise_and))
            dv.inc(nc.vector.tensor_scalar(znv[:, :, 1], zbuf[:, :], 4, None, ALU.logical_shift_right))
            dv.inc(nc.vector.scalar_tensor_tensor(
                zsbuf[:, :], znib[:, :], QS, sgbuf[:, :], ALU.subtract, ALU.mult))
            vneg = dv.inc(nc.vector.tensor_scalar(zsbuf[:, :], zsbuf[:, :], -1.0, None, ALU.mult))
            nc.gpsimd.wait_ge(dv.h, vneg)
            lg.dinc(nc.gpsimd.dma_start(s_sgb[mat], sgbuf[:, :]))
            lg.dinc(nc.gpsimd.dma_start(
                s_sgbblk[mat].transpose([1, 0, 2]),
                sgbuf[:, :].rearrange("g (c i) -> g c i", c=CC)))
            vwb = lg.dinc(nc.gpsimd.dma_start(s_zs[mat], zsbuf[:, :]))
            nc.gpsimd.wait_ge(lg.h, vwb)
        dsbuf = sb("dsbuf", [CC, DM], BF16, p0a)
        dzbuf = sb("dzbuf", [CC, 2048], U8, p0a)
        dznib = sb("dznib", [CC, DM], U8, p0a)
        dzsb = sb("dzsb", [CC, DM], BF16, p0a)
        vl = lg.dinc(nc.gpsimd.dma_start(dsbuf[:, :], d_dsc[:, :]))
        vz = ls.dinc(nc.sync.dma_start(dzbuf[:, :], d_dzb[:, :]))
        nc.vector.wait_ge(lg.h, vl)
        nc.vector.wait_ge(ls.h, vz)
        dznv = dznib[:, :].rearrange("p (b two) -> p b two", two=2)
        dv.inc(nc.vector.tensor_scalar(dznv[:, :, 0], dzbuf[:, :], 15, None, ALU.bitwise_and))
        dv.inc(nc.vector.tensor_scalar(dznv[:, :, 1], dzbuf[:, :], 4, None, ALU.logical_shift_right))
        dv.inc(nc.vector.scalar_tensor_tensor(
            dzsb[:, :], dznib[:, :], QS, dsbuf[:, :], ALU.subtract, ALU.mult))
        vneg = dv.inc(nc.vector.tensor_scalar(dzsb[:, :], dzsb[:, :], -1.0, None, ALU.mult))
        nc.gpsimd.wait_ge(dv.h, vneg)
        lg.dinc(nc.gpsimd.dma_start(s_dsb[:, :], dsbuf[:, :]))
        v_tables = lg.dinc(nc.gpsimd.dma_start(s_zsd[:, :], dzsb[:, :]))

    def dpad(k=12):
        for _ in range(k):
            nc.vector.engine_nop()

    def apad(k=2):
        for _ in range(k):
            ac.inc(nc.scalar.activation(svp[:, :], svp[:, :], AF.Identity))

    def ln_stats(x_tile, sq_scr):
        nc.vector.wait_ge(ac.h, ac.n)
        dv.inc(nc.vector.reduce_sum(sv[0][:, :], x_tile[:, :], AX.X, negate=True))
        nc.scalar.wait_ge(dv.h, dv.n)
        va = ac.inc(nc.scalar.activation(sq_scr[:, :], x_tile[:, :], AF.Square))
        nc.vector.wait_ge(ac.h, va)
        dv.inc(nc.vector.reduce_sum(sv[1][:, :], sq_scr[:, :], AX.X))
        dv.inc(nc.vector.tensor_scalar(sv[4][:, :], sv[0][:, :], 1.0 / DM, None, ALU.mult))
        dv.inc(nc.vector.scalar_tensor_tensor(
            sv[5][:, :], sv[0][:, :], 1.0 / DM ** 2, sv[0][:, :], ALU.mult, ALU.mult))
        dpad()
        vv = dv.inc(nc.vector.scalar_tensor_tensor(
            sv[1][:, :], sv[1][:, :], 1.0 / DM, sv[5][:, :], ALU.mult, ALU.subtract))
        dpad()
        nc.scalar.wait_ge(dv.h, vv)
        apad()
        ac.inc(nc.scalar.activation(sv[5][:, :], sv[1][:, :], AF.Ln, bias=np.float32(EPS_LN)))
        apad()
        va = ac.inc(nc.scalar.activation(sv[2][:, :], sv[5][:, :], AF.Exp, scale=np.float32(-0.5)))
        nc.vector.wait_ge(ac.h, va)
        dpad()
        dv.inc(nc.vector.tensor_tensor(sv[3][:, :], sv[4][:, :], sv[2][:, :], ALU.mult))
        dpad()
        vb = dv.inc(nc.vector.engine_nop())
        apad()
        return vb

    def transpose_tok_to_feat(tok_bf, destT, tt, psbs, ready_sem, ready_val, state):
        for k4 in range(KC // 4):
            pb = psbs[k4 % 2]
            nc.tensor.wait_ge(ready_sem.h, ready_val)
            if state[k4 % 2]:
                nc.tensor.wait_ge(dv.h, state[k4 % 2])
            for j in range(4):
                kc = k4 * 4 + j
                i = nc.tensor.transpose(
                    pb[:, j * 128:(j + 1) * 128],
                    tok_bf[:, kc * 128:(kc + 1) * 128], identb[:, :])
            vp = pe.inc(i)
            nc.vector.wait_ge(pe.h, vp)
            dview = destT[:, :].rearrange("p (kc t) -> p kc t", t=1024)
            state[k4 % 2] = dv.inc(nc.vector.tensor_copy(
                dview[:, k4 * 4:k4 * 4 + 4, tt * 128:(tt + 1) * 128],
                pb[:, :].rearrange("p (j t) -> p j t", t=128)))

    attn_stack = ExitStack()
    xln1T = sb("xln1T", [128, KC * 1024], BF16, attn_stack)
    nc.tensor.wait_ge(ls.h, v_const_ls)
    with ExitStack() as p0b:
        xtt = [sb(f"xtt{i}", [128, DM], F32, p0b) for i in range(2)]
        sq_scr = sb("sq0", [128, DM], F32, p0b)
        tokbf = sb("tokbf0", [128, DM], BF16, p0b)
        psb = [psum(f"psb0{i}", [128, 512], BF16, p0b) for i in range(2)]
        tstate = [0, 0]
        v_x_ld = [0] * NTT
        v_x_free = [0] * NTT
        for tt in range(NTT):
            if tt >= 2:
                nc.sync.wait_ge(ac.h, v_x_free[tt - 2])
            else:
                nc.sync.wait_ge(lg.h, v_tables)
            v_x_ld[tt] = lx.dinc(nc.sync.dma_start(
                xtt[tt % 2][:, :], d_x[tt * 128:(tt + 1) * 128, :]))
            nc.vector.wait_ge(lx.h, v_x_ld[tt])
            nc.scalar.wait_ge(lx.h, v_x_ld[tt])
            ln_stats(xtt[tt % 2], sq_scr)
            nc.scalar.wait_ge(dv.h, dv.n)
            v_x_free[tt] = ac.inc(nc.scalar.activation(
                tokbf[:, :], xtt[tt % 2][:, :], AF.Identity,
                bias=sv[3][:, :], scale=sv[2][:, :]))
            if dbg and tt == 0:
                dbgt = nc.dram_tensor("dbg_tok0", [128, DM], BF16, kind="ExternalOutput").ap()
                dbgx = nc.dram_tensor("dbg_xtt0", [128, DM], F32, kind="ExternalOutput").ap()
                dbgs = nc.dram_tensor("dbg_sv0", [128, 8], F32, kind="ExternalOutput").ap()
                dbgs2 = nc.dram_tensor("dbg_sv0_late", [128, 8], F32, kind="ExternalOutput").ap()
                nc.gpsimd.wait_ge(ac.h, v_x_free[tt])
                lg.dinc(nc.gpsimd.dma_start(dbgt[:, :], tokbf[:, :]))
                lg.dinc(nc.gpsimd.dma_start(dbgx[:, :], xtt[0][:, :]))
                for i in range(6):
                    lg.dinc(nc.gpsimd.dma_start(dbgs[:, i:i + 1], sv[i][:, :], allow_slow_non_contiguous=True))
                nc.scalar.wait_ge(lg.h, lg.n)
                nc.vector.wait_ge(lg.h, lg.n)
            if dbg and tt == 1:
                nc.gpsimd.wait_ge(ac.h, ac.n)
                for i in range(6):
                    lg.dinc(nc.gpsimd.dma_start(dbgs2[:, i:i + 1], sv[i][:, :], allow_slow_non_contiguous=True))
                nc.vector.wait_ge(lg.h, lg.n)
                nc.scalar.wait_ge(lg.h, lg.n)
            transpose_tok_to_feat(tokbf, xln1T, tt, psb, ac, v_x_free[tt], tstate)
            if dbg and tt == 1:
                dbge = nc.dram_tensor("dbg_xln1T_early", [128, KC * 1024], BF16, kind="ExternalOutput").ap()
                nc.gpsimd.wait_ge(dv.h, dv.n)
                lg.dinc(nc.gpsimd.dma_start(dbge[:, :], xln1T[:, :]))
        v_xln1T = dv.n

    sig_q = sb("sig_q", [128, FCN * 1024], BF16, attn_stack)
    nc.vector.wait_ge(ls.h, v_const_ls)
    with ExitStack() as p1:
        wq_sb = [sb(f"wq{i}", [128, KC * 128], BF16, p1) for i in range(2)]
        scr_r = sb("scr_r", [128, 1024], F32, p1)
        scr_m = sb("scr_m", [128, 1024], F32, p1)
        v_wq_ld = [0] * FCN
        v_q_mm = [0] * FCN
        v_q_evac = [0] * FCN
        for it in range(FCN + 1):
            if it < FCN:
                if it >= 2:
                    nc.gpsimd.wait_ge(pe.h, v_q_mm[it - 2])
                else:
                    nc.gpsimd.wait_ge(dv.h, v_xln1T)
                v_wq_ld[it] = lg.dinc(nc.gpsimd.dma_start(
                    wq_sb[it % 2][:, :], d_wqsel[it].rearrange("p kc f -> p (kc f)")))
            fc = it - 1
            if fc < 0:
                continue
            pa, pb = (ps[0], ps[1]) if fc % 2 == 0 else (ps[2], ps[3])
            nc.tensor.wait_ge(lg.h, v_wq_ld[fc])
            if fc == 0:
                nc.tensor.wait_ge(dv.h, v_xln1T)
            if fc >= 2:
                nc.tensor.wait_ge(dv.h, v_q_evac[fc - 2])
            for half, pp in ((0, pa), (1, pb)):
                for kc in range(KC):
                    i = nc.tensor.matmul(
                        pp[:, :], wq_sb[fc % 2][:, kc * 128:(kc + 1) * 128],
                        xln1T[:, kc * 1024 + half * 512: kc * 1024 + (half + 1) * 512],
                        start=(kc == 0), stop=(kc == KC - 1))
            v_q_mm[fc] = pe.inc(i)
            nc.vector.wait_ge(pe.h, v_q_mm[fc])
            bqc = bqsel[:, fc:fc + 1]
            for half, pp in ((0, pa), (1, pb)):
                slh = slice(half * 512, (half + 1) * 512)
                dv.inc(nc.vector.tensor_scalar(scr_r[:, slh], pp[:, :], bqc, 0.0, ALU.add, ALU.max))
                dv.inc(nc.vector.tensor_scalar(scr_m[:, slh], pp[:, :], bqc, 0.0, ALU.add, ALU.min))
            nc.scalar.wait_ge(dv.h, dv.n)
            va = ac.inc(nc.scalar.activation(scr_m[:, :], scr_m[:, :], AF.Exp))
            nc.vector.wait_ge(ac.h, va)
            v_q_evac[fc] = dv.inc(nc.vector.tensor_tensor(
                sig_q[:, fc * 1024:(fc + 1) * 1024], scr_r[:, :], scr_m[:, :], ALU.add))

    if dbg:
        dbg1 = nc.dram_tensor("dbg_xln1T", [128, KC * 1024], BF16, kind="ExternalOutput").ap()
        dbg2 = nc.dram_tensor("dbg_sigq", [128, FCN * 1024], BF16, kind="ExternalOutput").ap()
        nc.gpsimd.wait_ge(dv.h, dv.n)
        lg.dinc(nc.gpsimd.dma_start(dbg1[:, :], xln1T[:, :]))
        lg.dinc(nc.gpsimd.dma_start(dbg2[:, :], sig_q[:, :]))

    ret = sb("ret", [128, HKV * 1024], BF16, attn_stack)
    with ExitStack() as p2:
        den_sb = sb("den_sb", [1, 1024], F32, p2)
        rden_bf = sb("rden_bf", [1, 1024], BF16, p2)
        rb_sb = sb("rb_sb", [128, 1024], BF16, p2)
        nc.tensor.wait_ge(lg.h, v_const_lg)
        nc.tensor.wait_ge(dv.h, dv.n)
        for h in range(HKV):
            sq = sig_q[:, h * 1024:(h + 1) * 1024]
            for half, pp in ((0, ps[0]), (1, ps[1])):
                i = nc.tensor.matmul(pp[:, :], m_sb[:, h * 128:(h + 1) * 128],
                                     sq[:, half * 512:(half + 1) * 512],
                                     start=True, stop=True)
            for half, pp in ((0, ps[2]), (1, ps[3])):
                i = nc.tensor.matmul(pp[0:1, :], z_sb[:, h:h + 1],
                                     sq[:, half * 512:(half + 1) * 512],
                                     start=True, stop=True)
            v_nd = pe.inc(i)
            nc.vector.wait_ge(pe.h, v_nd)
            dv.inc(nc.vector.tensor_scalar(den_sb[:, 0:512], ps[2][0:1, :], EPS_DEN, None, ALU.add))
            dv.inc(nc.vector.tensor_scalar(den_sb[:, 512:1024], ps[3][0:1, :], EPS_DEN, None, ALU.add))
            dv.inc(nc.vector.reciprocal(den_sb[:, :], den_sb[:, :]))
            v_rd = dv.inc(nc.vector.tensor_copy(rden_bf[:, :], den_sb[:, :]))
            nc.tensor.wait_ge(dv.h, v_rd)
            for half, pp in ((0, ps[4]), (1, ps[5])):
                i = nc.tensor.matmul(pp[:, :], ones1[0:1, :],
                                     rden_bf[0:1, half * 512:(half + 1) * 512],
                                     start=True, stop=True)
            v_bc = pe.inc(i)
            nc.vector.wait_ge(pe.h, v_bc)
            dv.inc(nc.vector.tensor_copy(rb_sb[:, 0:512], ps[4][:, :]))
            dv.inc(nc.vector.tensor_copy(rb_sb[:, 512:1024], ps[5][:, :]))
            dv.inc(nc.vector.tensor_tensor(ret[:, h * 1024:h * 1024 + 512],
                                           ps[0][:, :], rb_sb[:, 0:512], ALU.mult))
            v_ret = dv.inc(nc.vector.tensor_tensor(
                ret[:, h * 1024 + 512:(h + 1) * 1024],
                ps[1][:, :], rb_sb[:, 512:1024], ALU.mult))
            nc.tensor.wait_ge(dv.h, v_ret)
        v_p2_done = dv.n

    if dbg:
        dbg3 = nc.dram_tensor("dbg_ret", [128, HKV * 1024], BF16, kind="ExternalOutput").ap()
        nc.gpsimd.wait_ge(dv.h, dv.n)
        lg.dinc(nc.gpsimd.dma_start(dbg3[:, :], ret[:, :]))

    with ExitStack() as p3:
        wo_sb = [sb(f"wo{i}", [128, KC * 128], BF16, p3) for i in range(3)]
        woeff = [sb(f"woe{i}", [128, HKV * 128], BF16, p3) for i in range(2)]
        ao_sb = [sb(f"ao{i}", [128, 1024], F32, p3) for i in range(2)]
        v_wo_ld = [0] * OCN
        v_wo_add = [0] * OCN
        v_wo_mm = [0] * OCN
        v_wo_evac = [0] * OCN
        v_wo_spill = [0] * OCN
        for it in range(OCN + 2):
            if it < OCN:
                if it >= 3:
                    nc.gpsimd.wait_ge(dv.h, v_wo_add[it - 3])
                elif it == 0:
                    nc.gpsimd.wait_ge(dv.h, v_p2_done)

                v_wo_ld[it] = lg.dinc(nc.gpsimd.dma_start(
                    wo_sb[it % 3][:, :], d_wo[it].rearrange("p fc o -> p (fc o)")))
            oc = it - 2
            if oc < 0:
                continue
            w = wo_sb[oc % 3]
            we = woeff[oc % 2]
            nc.vector.wait_ge(lg.h, v_wo_ld[oc])
            if oc >= 2:
                nc.vector.wait_ge(pe.h, v_wo_mm[oc - 2])
            for hk in range(HKV):
                b = 4 * hk * 128
                dsl = slice(hk * 128, (hk + 1) * 128)
                dv.inc(nc.vector.tensor_tensor(we[:, dsl], w[:, b:b + 128],
                                               w[:, b + 128:b + 256], ALU.add))
                dv.inc(nc.vector.tensor_tensor(we[:, dsl], we[:, dsl],
                                               w[:, b + 256:b + 384], ALU.add))
                dv.inc(nc.vector.tensor_tensor(we[:, dsl], we[:, dsl],
                                               w[:, b + 384:b + 512], ALU.add))
            dpad()
            v_wo_add[oc] = dv.inc(nc.vector.engine_nop())
            pa, pb = (ps[0], ps[1]) if oc % 2 == 0 else (ps[2], ps[3])
            nc.tensor.wait_ge(dv.h, v_wo_add[oc])
            if oc >= 2:
                nc.tensor.wait_ge(dv.h, v_wo_evac[oc - 2])
            for half, pp in ((0, pa), (1, pb)):
                for hk in range(HKV):
                    i = nc.tensor.matmul(
                        pp[:, :], we[:, hk * 128:(hk + 1) * 128],
                        ret[:, hk * 1024 + half * 512: hk * 1024 + (half + 1) * 512],
                        start=(hk == 0), stop=(hk == HKV - 1))
            v_wo_mm[oc] = pe.inc(i)
            nc.vector.wait_ge(pe.h, v_wo_mm[oc])
            if oc >= 2:
                nc.vector.wait_ge(lg.h, v_wo_spill[oc - 2])
            dv.inc(nc.vector.tensor_copy(ao_sb[oc % 2][:, 0:512], pa[:, :]))
            v_wo_evac[oc] = dv.inc(nc.vector.tensor_copy(ao_sb[oc % 2][:, 512:1024], pb[:, :]))
            nc.gpsimd.wait_ge(dv.h, v_wo_evac[oc])
            v_wo_spill[oc] = lg.dinc(nc.gpsimd.dma_start(s_attn[oc], ao_sb[oc % 2][:, :]))
        v_attn_done = v_wo_spill[OCN - 1]

    attn_stack.close()
    ffn_stack = ExitStack()
    xln2T = sb("xln2T", [128, KC * 1024], BF16, ffn_stack)
    with ExitStack() as p4:
        ablk = [sb(f"ablk{i}", [128, DM], F32, p4) for i in range(2)]
        xtt = [sb(f"xtt4{i}", [128, DM], F32, p4) for i in range(2)]
        atok = sb("atok", [128, DM], F32, p4)
        sq_scr = sb("sq4", [128, DM], F32, p4)
        tokbf = sb("tokbf4", [128, DM], BF16, p4)
        x1sb = sb("x1sb", [128, DM], F32, p4)
        psb = [psum(f"psb4{i}", [128, 512], BF16, p4) for i in range(2)]
        tstate = [0, 0]
        pst = [ps[0], ps[1]]
        astate = [0, 0]
        v_a_ld = [0] * NTT
        v_a_free = [0] * NTT
        v_x1_sp = [0] * NTT
        nc.sync.wait_ge(lg.h, v_attn_done)
        for it in range(NTT + 1):
            if it < NTT:
                if it >= 2:
                    nc.sync.wait_ge(dv.h, v_a_free[it - 2])
                ls.dinc(nc.sync.dma_start(
                    ablk[it % 2][:, :].rearrange("p (oc t) -> p oc t", t=128),
                    s_attn[:, :, it * 128:(it + 1) * 128].transpose([1, 0, 2])))
                v_a_ld[it] = ls.dinc(nc.sync.dma_start(
                    xtt[it % 2][:, :], d_x[it * 128:(it + 1) * 128, :]))
            tt = it - 1
            if tt < 0:
                continue
            for k4 in range(KC // 4):
                pb = pst[k4 % 2]
                nc.tensor.wait_ge(ls.h, v_a_ld[tt])
                if astate[k4 % 2]:
                    nc.tensor.wait_ge(dv.h, astate[k4 % 2])
                for j in range(4):
                    oc = k4 * 4 + j
                    i = nc.tensor.transpose(
                        pb[:, j * 128:(j + 1) * 128],
                        ablk[tt % 2][:, oc * 128:(oc + 1) * 128], identf[:, :])
                vp = pe.inc(i)
                nc.vector.wait_ge(pe.h, vp)
                astate[k4 % 2] = dv.inc(nc.vector.tensor_copy(
                    atok[:, :].rearrange("p (k t) -> p k t", t=128)[:, k4 * 4:k4 * 4 + 4, :],
                    pb[:, :].rearrange("p (j t) -> p j t", t=128)))
            ln_stats(atok, sq_scr)
            nc.scalar.wait_ge(dv.h, dv.n)
            va = ac.inc(nc.scalar.activation(sq_scr[:, :], atok[:, :], AF.Identity,
                                             bias=sv[3][:, :], scale=sv[2][:, :]))
            nc.vector.wait_ge(ac.h, va)
            nc.vector.wait_ge(ls.h, v_a_ld[tt])
            vx1 = dv.inc(nc.vector.tensor_tensor(x1sb[:, :], xtt[tt % 2][:, :],
                                                 sq_scr[:, :], ALU.add))
            v_a_free[tt] = vx1
            nc.gpsimd.wait_ge(dv.h, vx1)
            v_x1_sp[tt] = lg.dinc(nc.gpsimd.dma_start(
                s_x1[tt * 128:(tt + 1) * 128, :], x1sb[:, :]))
            ln_stats(x1sb, sq_scr)
            nc.scalar.wait_ge(dv.h, dv.n)
            va = ac.inc(nc.scalar.activation(tokbf[:, :], x1sb[:, :], AF.Identity,
                                             bias=sv[3][:, :], scale=sv[2][:, :]))
            transpose_tok_to_feat(tokbf, xln2T, tt, psb, ac, va, tstate)
        v_x1_done = v_x1_sp[NTT - 1]

    if dbg:
        dbg4 = nc.dram_tensor("dbg_xln2T", [128, KC * 1024], BF16, kind="ExternalOutput").ap()
        nc.gpsimd.wait_ge(dv.h, dv.n)
        lg.dinc(nc.gpsimd.dma_start(dbg4[:, :], xln2T[:, :]))

    xgT = sb("xgT", [32, 1024], BF16, ffn_stack)
    nc.tensor.wait_ge(dv.h, dv.n)
    for half, pp in ((0, ps[0]), (1, ps[1])):
        for kc in range(KC):
            i = nc.tensor.matmul(
                pp[0:32, :], oneh32[:, kc * 32:(kc + 1) * 32],
                xln2T[:, kc * 1024 + half * 512: kc * 1024 + (half + 1) * 512],
                start=(kc == 0), stop=(kc == KC - 1))
    v5 = pe.inc(i)
    nc.vector.wait_ge(pe.h, v5)
    dv.inc(nc.vector.tensor_copy(xgT[:, 0:512], ps[0][0:32, :]))
    v_xg = dv.inc(nc.vector.tensor_copy(xgT[:, 512:1024], ps[1][0:32, :]))

    with ExitStack() as p6:
        ps6 = [psum(f"ps6{i}", [128, 512], F32, p6) for i in range(2)]
        pall = ps + ps6
        NS6 = 3
        gub = [[sb(f"byt{m}{i}", [128, 2048], U8, p6) for i in range(NS6)] for m in range(2)]
        zssl = [[sb(f"zssl{m}{i}", [32, 128], BF16, p6) for i in range(NS6)] for m in range(2)]
        sbc = [[sb(f"sbc{m}{i}", [128, DM], BF16, p6) for i in range(NS6)] for m in range(2)]
        wgu = [[sb(f"wgu{m}{i}", [128, DM], BF16, p6) for i in range(NS6)] for m in range(2)]
        nib = [sb(f"nib{m}", [128, DM], U8, p6) for m in range(2)]
        silu_sb = sb("silu_sb", [128, 1024], BF16, p6)
        h_sb = [sb(f"hsb{i}", [128, 1024], BF16, p6) for i in range(2)]
        nc.sync.wait_ge(lg.h, v_tables)
        v6_ld = [0] * CC
        v6_deq = [0] * CC
        v6_mm = [0] * CC
        v6_h = [0] * CC
        v6_hs = [0] * CC
        PF = 2
        srcs = [(d_gb, s_sgbblk[0], s_zs[0]), (d_ub, s_sgbblk[1], s_zs[1])]
        for it in range(CC + PF):
            c = it
            if c < CC:
                slc = c % NS6
                if c >= NS6:
                    nc.sync.wait_ge(dv.h, v6_deq[c - NS6])
                    nc.sync.wait_ge(pe.h, v6_mm[c - NS6])
                elif c == 0:
                    nc.sync.wait_ge(dv.h, v_xg)
                for m, (dby, dsgb, dzs) in enumerate(srcs):
                    ls.dinc(nc.sync.dma_start(
                        gub[m][slc][:, :], dby[c].rearrange("p kc b -> p (kc b)")))
                    ls.dinc(nc.sync.dma_start(
                        zssl[m][slc][:, :], dzs[:, c * 128:(c + 1) * 128]))
                    ls.dinc(nc.sync.dma_start(
                        sbc[m][slc][:, :],
                        dsgb[c].rearrange("g i -> (g i)").unsqueeze(0)
                        .partition_broadcast(128).squeeze(1)))
                v6_ld[c] = ls.n
            cd = it - 1
            if 0 <= cd < CC:
                slc = cd % NS6
                nc.vector.wait_ge(ls.h, v6_ld[cd])
                if cd >= NS6:
                    nc.vector.wait_ge(pe.h, v6_mm[cd - NS6])
                for m in range(2):
                    bt = gub[m][slc][:, :].rearrange("p (kc b) -> p kc b", kc=KC)
                    nv = nib[m][:, :].rearrange("p (kc b two) -> p kc b two", kc=KC, two=2)
                    dv.inc(nc.vector.tensor_scalar(nv[:, :, :, 0], bt, 15, None, ALU.bitwise_and))
                    dv.inc(nc.vector.tensor_scalar(nv[:, :, :, 1], bt, 4, None, ALU.logical_shift_right))
                    dv.inc(nc.vector.scalar_tensor_tensor(
                        wgu[m][slc][:, :], nib[m][:, :], QS, sbc[m][slc][:, :],
                        ALU.subtract, ALU.mult))
                v6_deq[cd] = dv.n
            cc = it - PF
            if cc < 0:
                continue
            slc = cc % NS6
            pbank = pall[0:4] if cc % 2 == 0 else pall[4:8]
            nc.tensor.wait_ge(dv.h, v6_deq[cc])
            if cc >= 2:
                nc.tensor.wait_ge(dv.h, v6_h[cc - 2])
            for m in range(2):
                for half in range(2):
                    pp = pbank[m * 2 + half]
                    i = nc.tensor.matmul(pp[:, :], zssl[m][slc][:, :],
                                         xgT[:, half * 512:(half + 1) * 512],
                                         start=True, stop=False)
                    for kc in range(KC):
                        i = nc.tensor.matmul(
                            pp[:, :], wgu[m][slc][:, kc * 128:(kc + 1) * 128],
                            xln2T[:, kc * 1024 + half * 512: kc * 1024 + (half + 1) * 512],
                            start=False, stop=(kc == KC - 1))
            v6_mm[cc] = pe.inc(i)
            nc.scalar.wait_ge(pe.h, v6_mm[cc])
            ac.inc(nc.scalar.activation(silu_sb[:, 0:512], pbank[0][:, :], AF.Silu))
            v_silu = ac.inc(nc.scalar.activation(silu_sb[:, 512:1024], pbank[1][:, :], AF.Silu))
            nc.vector.wait_ge(ac.h, v_silu)
            hsl = cc % 2
            if cc >= 2:
                nc.vector.wait_ge(lg.h, v6_hs[cc - 2])
            dv.inc(nc.vector.tensor_tensor(h_sb[hsl][:, 0:512], silu_sb[:, 0:512],
                                           pbank[2][:, :], ALU.mult))
            v6_h[cc] = dv.inc(nc.vector.tensor_tensor(
                h_sb[hsl][:, 512:1024], silu_sb[:, 512:1024], pbank[3][:, :], ALU.mult))
            nc.gpsimd.wait_ge(dv.h, v6_h[cc])
            v6_hs[cc] = lg.dinc(nc.gpsimd.dma_start(s_h[cc], h_sb[hsl][:, :]))
        v_h_done = v6_hs[CC - 1]

    ffn_stack.close()
    hgT_bf = sb("hgT_bf", [CC, 1024], BF16)
    with ExitStack() as p7:
        wd = [[sb(f"wd{b}_{j}", [128, 2048], BF16, p7) for j in range(DB)] for b in range(2)]
        wdby = [sb(f"wdby{j}", [128, 1024], U8, p7) for j in range(DB)]
        h7 = [[sb(f"h7{b}_{j}", [128, 1024], BF16, p7) for j in range(DB)] for b in range(2)]
        sdbc = [sb(f"sdbc{j}", [128, 2048], BF16, p7) for j in range(DB)]
        nibd = sb("nibd", [128, 2048], U8, p7)
        oneh88 = sb("oneh88", [128, DB * CC], BF16, p7)
        zsd_sb = sb("zsd_sb", [CC, 2048], BF16, p7)
        stg = [sb(f"stg{i}", [128, 1024], F32, p7) for i in range(2)]
        nc.sync.wait_ge(lg.h, v_h_done)

        pd = ps[0:4]
        phg = (ps[4][0:CC, :], ps[5][0:CC, :])
        NG = 2 * NR

        v_acc = {}
        v_stg = {}
        v_evac = {}
        last_deq = 0
        v_band_ld = {}
        v_deq = {}
        v_mmband = {}
        v_hg_pe = 0
        v_hgT = 0
        v_zsd_ld = [0, 0]

        def issue_band_loads(g):
            nonlocal last_deq
            ph, r = g // NR, g % NR
            buf = g % 2
            if last_deq:
                nc.sync.wait_ge(dv.h, last_deq)
            if g >= 2:
                nc.sync.wait_ge(pe.h, v_mmband[g - 2])
            for j in range(DB):
                c = r * DB + j
                ls.dinc(nc.sync.dma_start(
                    wdby[j][:, :], d_db[c][:, ph * 1024:(ph + 1) * 1024]))
                ls.dinc(nc.sync.dma_start(h7[buf][j][:, :], s_h[c]))
                ls.dinc(nc.sync.dma_start(
                    sdbc[j][:, :],
                    s_dsb[c:c + 1, ph * 2048:(ph + 1) * 2048]
                    .partition_broadcast(128).squeeze(1)))
            v_band_ld[g] = ls.n

        def deq_band(g):
            nonlocal last_deq
            buf = g % 2
            nc.vector.wait_ge(ls.h, v_band_ld[g])
            if g >= 2:
                nc.vector.wait_ge(pe.h, v_mmband[g - 2])
            for j in range(DB):
                nv = nibd[:, :].rearrange("p (b two) -> p b two", two=2)
                dv.inc(nc.vector.tensor_scalar(nv[:, :, 0], wdby[j][:, :], 15, None, ALU.bitwise_and))
                dv.inc(nc.vector.tensor_scalar(nv[:, :, 1], wdby[j][:, :], 4, None, ALU.logical_shift_right))
                dv.inc(nc.vector.scalar_tensor_tensor(
                    wd[buf][j][:, :], nibd[:, :], QS, sdbc[j][:, :],
                    ALU.subtract, ALU.mult))
            last_deq = dv.n
            v_deq[g] = dv.n

        def mm_round(g):
            nonlocal v_hg_pe, v_hgT
            ph, r = g // NR, g % NR
            buf = g % 2
            if ph == 0:
                nc.vector.wait_ge(pe.h, v_hg_pe)
                dv.inc(nc.vector.memset(oneh88[:, :], 0.0))
                for j in range(DB):
                    c = r * DB + j
                    dv.inc(nc.vector.memset(oneh88[:, j * CC + c:j * CC + c + 1], 1.0))
                dpad(16)
                v_oneh = dv.inc(nc.vector.engine_nop())
                nc.tensor.wait_ge(dv.h, v_oneh)
                nc.tensor.wait_ge(ls.h, v_band_ld[g])
                for j in range(DB):
                    for half, pp in ((0, phg[0]), (1, phg[1])):
                        i = nc.tensor.matmul(
                            pp, oneh88[:, j * CC:(j + 1) * CC],
                            h7[buf][j][:, half * 512:(half + 1) * 512],
                            start=(r == 0 and j == 0),
                            stop=(r == NR - 1 and j == DB - 1))
                v_hg_pe = pe.inc(i)
                if r == NR - 1:
                    nc.vector.wait_ge(pe.h, v_hg_pe)
                    dv.inc(nc.vector.tensor_copy(hgT_bf[:, 0:512], phg[0]))
                    v_hgT = dv.inc(nc.vector.tensor_copy(hgT_bf[:, 512:1024], phg[1]))
            nc.tensor.wait_ge(dv.h, v_deq[g])
            nc.tensor.wait_ge(ls.h, v_band_ld[g])
            if g % NR == NR - 1:
                nc.tensor.wait_ge(dv.h, v_hgT)
                nc.tensor.wait_ge(ls.h, v_zsd_ld[ph])
            for ol in range(16):
                oc = ph * 16 + ol
                pair = ol % 2
                if pair in v_evac:
                    nc.tensor.wait_ge(dv.h, v_evac[pair])
                ppa, ppb = pd[pair * 2], pd[pair * 2 + 1]
                for half, pp in ((0, ppa), (1, ppb)):
                    for j in range(DB):
                        i = nc.tensor.matmul(
                            pp[:, :], wd[buf][j][:, ol * 128:(ol + 1) * 128],
                            h7[buf][j][:, half * 512:(half + 1) * 512],
                            start=(j == 0), stop=(j == DB - 1 and r != NR - 1))
                    if r == NR - 1:
                        i = nc.tensor.matmul(
                            pp[:, :], zsd_sb[:, ol * 128:(ol + 1) * 128],
                            hgT_bf[:, half * 512:(half + 1) * 512],
                            start=False, stop=True)
                v_mmoc = pe.inc(i)
                nc.vector.wait_ge(pe.h, v_mmoc)
                slot = ol % 2
                if slot in v_stg:
                    nc.vector.wait_ge(lg.h, v_stg[slot])
                sg = stg[slot]
                dv.inc(nc.vector.tensor_copy(sg[:, 0:512], ppa[:, :]))
                v_ev = dv.inc(nc.vector.tensor_copy(sg[:, 512:1024], ppb[:, :]))
                v_evac[pair] = v_ev
                nc.gpsimd.wait_ge(dv.h, v_ev)
                if oc in v_acc:
                    nc.gpsimd.wait_ge(lg.h, v_acc[oc])
                op = ALU.bypass if r == 0 else ALU.add
                v_acc[oc] = lg.dinc(nc.gpsimd.dma_start(
                    s_oacc[oc], sg[:, :], accum_op=op))
                v_stg[slot] = v_acc[oc]
            v_mmband[g] = pe.n

        v_zsd_ld[0] = ls.dinc(nc.sync.dma_start(
            zsd_sb[:, :], s_zsd[:, 0:2048]))
        issue_band_loads(0)
        deq_band(0)
        issue_band_loads(1)
        for g in range(NG):
            if g + 1 < NG:
                deq_band(g + 1)
            mm_round(g)
            if g == NR - 1:
                nc.sync.wait_ge(pe.h, pe.n)
                v_zsd_ld[1] = ls.dinc(nc.sync.dma_start(
                    zsd_sb[:, :], s_zsd[:, 2048:4096]))
            if g + 2 < NG:
                issue_band_loads(g + 2)
        v_oacc_done = max(v_acc.values())

    with ExitStack() as p8:
        oblk = [sb(f"oblk{i}", [128, DM], F32, p8) for i in range(2)]
        x1tt = [sb(f"x1tt{i}", [128, DM], F32, p8) for i in range(2)]
        ftok = sb("ftok", [128, DM], F32, p8)
        sq_scr = sb("sq8", [128, DM], F32, p8)
        out_sb = [sb(f"outsb{i}", [128, DM], F32, p8) for i in range(2)]
        pst = [ps[0], ps[1]]
        ostate = [0, 0]
        v_o_ld = [0] * NTT
        v_o_free = [0] * NTT
        v_out_sp = [0] * NTT
        nc.sync.wait_ge(lg.h, v_oacc_done)
        for it in range(NTT + 1):
            if it < NTT:
                if it >= 2:
                    nc.sync.wait_ge(dv.h, v_o_free[it - 2])
                ls.dinc(nc.sync.dma_start(
                    oblk[it % 2][:, :].rearrange("p (oc t) -> p oc t", t=128),
                    s_oacc[:, :, it * 128:(it + 1) * 128].transpose([1, 0, 2])))
                v_o_ld[it] = ls.dinc(nc.sync.dma_start(
                    x1tt[it % 2][:, :], s_x1[it * 128:(it + 1) * 128, :]))
            tt = it - 1
            if tt < 0:
                continue
            for k4 in range(KC // 4):
                pb = pst[k4 % 2]
                nc.tensor.wait_ge(ls.h, v_o_ld[tt])
                if ostate[k4 % 2]:
                    nc.tensor.wait_ge(dv.h, ostate[k4 % 2])
                for j in range(4):
                    oc = k4 * 4 + j
                    i = nc.tensor.transpose(
                        pb[:, j * 128:(j + 1) * 128],
                        oblk[tt % 2][:, oc * 128:(oc + 1) * 128], identf[:, :])
                vp = pe.inc(i)
                nc.vector.wait_ge(pe.h, vp)
                ostate[k4 % 2] = dv.inc(nc.vector.tensor_copy(
                    ftok[:, :].rearrange("p (k t) -> p k t", t=128)[:, k4 * 4:k4 * 4 + 4, :],
                    pb[:, :].rearrange("p (j t) -> p j t", t=128)))
            ln_stats(ftok, sq_scr)
            nc.scalar.wait_ge(dv.h, dv.n)
            va = ac.inc(nc.scalar.activation(sq_scr[:, :], ftok[:, :], AF.Identity,
                                             bias=sv[3][:, :], scale=sv[2][:, :]))
            nc.vector.wait_ge(ac.h, va)
            nc.vector.wait_ge(ls.h, v_o_ld[tt])
            if tt >= 2:
                nc.vector.wait_ge(st.h, v_out_sp[tt - 2])
            v_o_free[tt] = dv.inc(nc.vector.tensor_tensor(
                out_sb[tt % 2][:, :], x1tt[tt % 2][:, :], sq_scr[:, :], ALU.add))
            nc.gpsimd.wait_ge(dv.h, v_o_free[tt])
            v_out_sp[tt] = st.dinc(nc.gpsimd.dma_start(
                d_out[tt * 128:(tt + 1) * 128, :], out_sb[tt % 2][:, :]))
        nc.gpsimd.wait_ge(st.h, v_out_sp[NTT - 1])

    top.close()
    return nc


_NC_CACHE = None


def kernel(**inputs):
    global _NC_CACHE
    x_shards, common = _host_prep(inputs)
    if _NC_CACHE is None:
        _NC_CACHE = build_program()
    nc = _NC_CACHE
    in_maps = [dict(common, x=x_shards[c]) for c in range(NCORE)]
    res = run_bass_kernel_spmd(nc, in_maps, core_ids=list(range(NCORE)))
    out = np.concatenate([np.asarray(r["out"]) for r in res.results], axis=0)
    return out.reshape(4, 2048, DM).astype(np.float32)


# version 12
# speedup vs baseline: 1.0325x; 1.0317x over previous
import os
import sys

if os.path.isdir("/opt/trn_rl_repo") and "/opt/trn_rl_repo" not in sys.path:
    sys.path.insert(0, "/opt/trn_rl_repo")

from contextlib import ExitStack

import numpy as np

import concourse.bass as bass
import concourse.mybir as mybir
from concourse.bass_utils import run_bass_kernel_spmd

F32 = mybir.dt.float32
BF16 = mybir.dt.bfloat16
U8 = mybir.dt.uint8
AF = mybir.ActivationFunctionType
ALU = mybir.AluOpType
AX = mybir.AxisListType

NCORE = 8
T = 1024
NTT = 8
DM = 4096
KC = 32
FCN = 8
OCN = 32
CC = 88
IPAD = CC * 128
HKV = 8
DB = 8
NR = CC // DB
EPS_LN = 1e-5
EPS_DEN = 1e-6
QS = 7.5


def _host_prep(inputs):
    f32 = np.float32
    X = np.asarray(inputs["L0_attn_hidden_states"], dtype=f32).reshape(NCORE * T, DM)
    x_shards = [np.ascontiguousarray(X[c * T:(c + 1) * T]) for c in range(NCORE)]

    Wq = np.asarray(inputs["L0_attn_Wq"], dtype=f32)
    wq3 = Wq.reshape(32, 128, DM)[0::4]
    wqsel = np.ascontiguousarray(
        wq3.transpose(0, 2, 1).reshape(FCN, KC, 128, 128).transpose(0, 2, 1, 3))
    bq = np.asarray(inputs["L0_attn_bq"], dtype=f32)
    bqsel = np.ascontiguousarray(bq.reshape(32, 128)[0::4].T)

    Wo = np.asarray(inputs["L0_attn_Wo"], dtype=f32)
    wo_blk = np.ascontiguousarray(Wo.reshape(OCN, 128, KC, 128).transpose(0, 3, 2, 1))

    M = np.asarray(inputs["L0_attn_M"], dtype=f32)
    m_host = np.ascontiguousarray(M.transpose(1, 0, 2).reshape(128, HKV * 128))
    z_host = np.ascontiguousarray(np.asarray(inputs["L0_attn_z"], dtype=f32).T)

    def _gu_bytes(qw):
        qv = np.ascontiguousarray(np.asarray(qw)).view(np.uint8)
        qv = np.concatenate(
            [qv, np.zeros((DM, IPAD // 2 - qv.shape[1]), np.uint8)], axis=1)
        return np.ascontiguousarray(
            qv.reshape(KC, 128, CC, 64).transpose(2, 1, 0, 3))

    g_bytes = _gu_bytes(inputs["L0_ffn_gate_qweight"])
    u_bytes = _gu_bytes(inputs["L0_ffn_up_qweight"])

    dqv = np.ascontiguousarray(np.asarray(inputs["L0_ffn_down_qweight"])).view(np.uint8)
    dqv = np.concatenate([dqv, np.zeros((IPAD - dqv.shape[0], 2048), np.uint8)], axis=0)
    d_bytes = np.ascontiguousarray(dqv.reshape(CC, 128, 2048))

    g_sc = np.zeros((32, IPAD), f32)
    g_sc[:, :11008] = np.asarray(inputs["L0_ffn_gate_scales"], dtype=f32)
    u_sc = np.zeros((32, IPAD), f32)
    u_sc[:, :11008] = np.asarray(inputs["L0_ffn_up_scales"], dtype=f32)
    d_sc = np.zeros((CC, DM), f32)
    d_sc[:86] = np.asarray(inputs["L0_ffn_down_scales"], dtype=f32)

    def _zb(qz, rows, cols):
        v = np.ascontiguousarray(np.asarray(qz)).view(np.uint8)
        out = np.zeros((rows, cols), np.uint8)
        out[:v.shape[0], :v.shape[1]] = v
        return out

    g_zb = _zb(inputs["L0_ffn_gate_qzeros"], 32, IPAD // 2)
    u_zb = _zb(inputs["L0_ffn_up_qzeros"], 32, IPAD // 2)
    d_zb = _zb(inputs["L0_ffn_down_qzeros"], CC, 2048)

    import ml_dtypes
    ident_bf = np.eye(128, dtype=ml_dtypes.bfloat16)
    ident_f32 = np.eye(128, dtype=np.float32)

    common = dict(
        wqsel=wqsel, bqsel=bqsel, wo_blk=wo_blk, m_h=m_host, z_h=z_host,
        g_bytes=g_bytes, u_bytes=u_bytes, d_bytes=d_bytes,
        g_sc=g_sc, u_sc=u_sc, d_sc=d_sc, g_zb=g_zb, u_zb=u_zb, d_zb=d_zb,
        ident_bf=ident_bf, ident_f32=ident_f32)
    return x_shards, common


class Sem:
    def __init__(self, nc, stack, name):
        self.h = stack.enter_context(nc.semaphore(name))
        self.n = 0

    def inc(self, instr, k=1):
        self.n += k
        instr.then_inc(self.h, k)
        return self.n

    def dinc(self, instr):
        return self.inc(instr, 16)


def build_program(dbg=False):
    nc = bass.Bass("TRN2", target_bir_lowering=False)

    def din(name, shape, dt):
        return nc.dram_tensor(name, shape, dt, kind="ExternalInput").ap()

    d_x = din("x", [T, DM], F32)
    d_wqsel = din("wqsel", [FCN, 128, KC, 128], F32)
    d_bqsel = din("bqsel", [128, FCN], F32)
    d_wo = din("wo_blk", [OCN, 128, KC, 128], F32)
    d_m = din("m_h", [128, HKV * 128], F32)
    d_z = din("z_h", [128, HKV], F32)
    d_gb = din("g_bytes", [CC, 128, KC, 64], U8)
    d_ub = din("u_bytes", [CC, 128, KC, 64], U8)
    d_db = din("d_bytes", [CC, 128, 2048], U8)
    d_gsc = din("g_sc", [32, IPAD], F32)
    d_usc = din("u_sc", [32, IPAD], F32)
    d_dsc = din("d_sc", [CC, DM], F32)
    d_gzb = din("g_zb", [32, IPAD // 2], U8)
    d_uzb = din("u_zb", [32, IPAD // 2], U8)
    d_dzb = din("d_zb", [CC, 2048], U8)
    d_identb = din("ident_bf", [128, 128], BF16)
    d_identf = din("ident_f32", [128, 128], F32)
    d_out = nc.dram_tensor("out", [T, DM], F32, kind="ExternalOutput").ap()

    def dscr(name, shape, dt):
        kind = "ExternalOutput" if dbg else "Internal"
        return nc.dram_tensor(name, shape, dt, kind=kind).ap()

    s_x1 = dscr("x1_scr", [T, DM], F32)
    s_attn = dscr("attn_scr", [OCN, 128, T], F32)
    s_h = dscr("h_scr", [CC, 128, T], BF16)
    s_oacc = dscr("oacc_scr", [OCN, 128, T], F32)
    s_sgb = dscr("sgb_scr", [2, 32, IPAD], BF16)
    s_sgbblk = dscr("sgbblk_scr", [2, CC, 32, 128], BF16)
    s_zs = dscr("zs_scr", [2, 32, IPAD], BF16)
    s_dsb = dscr("dsb_scr", [CC, DM], BF16)
    s_zsd = dscr("zsd_scr", [CC, DM], BF16)

    top = ExitStack()

    def sb(name, shape, dt, stk=None):
        return (stk or top).enter_context(nc.sbuf_tensor(name, shape, dt))

    def psum(name, shape, dt, stk=None):
        return (stk or top).enter_context(nc.psum_tensor(name, shape, dt))

    ls = Sem(nc, top, "ls")
    lg = Sem(nc, top, "lg")
    pe = Sem(nc, top, "pe")
    dv = Sem(nc, top, "dv")
    ac = Sem(nc, top, "ac")
    st = Sem(nc, top, "st")
    lx = Sem(nc, top, "lx")

    ps = [psum(f"ps{i}", [128, 512], F32) for i in range(6)]

    identb = sb("identb", [128, 128], BF16)
    identf = sb("identf", [128, 128], F32)
    bqsel = sb("bqsel_sb", [128, FCN], F32)
    m_sb = sb("m_sb", [128, HKV * 128], BF16)
    z_sb = sb("z_sb", [128, HKV], BF16)
    ones1 = sb("ones1", [1, 128], BF16)
    oneh32 = sb("oneh32", [128, KC * 32], BF16)
    sv = [sb(f"sv{i}", [128, 1], F32) for i in range(6)]
    svp = sb("svp", [128, 1], F32)

    ls.dinc(nc.sync.dma_start(identb[:, :], d_identb[:, :]))
    ls.dinc(nc.sync.dma_start(identf[:, :], d_identf[:, :]))
    ls.dinc(nc.sync.dma_start(bqsel[:, :], d_bqsel[:, :]))
    lg.dinc(nc.gpsimd.dma_start(m_sb[:, :], d_m[:, :]))
    v_const_lg = lg.dinc(nc.gpsimd.dma_start(z_sb[:, :], d_z[:, :]))
    v_const_ls = ls.n

    dv.inc(nc.vector.memset(ones1[:, :], 1.0))
    dv.inc(nc.vector.memset(oneh32[:, :], 0.0))
    for kc in range(KC):
        dv.inc(nc.vector.memset(oneh32[:, kc * 32 + kc:kc * 32 + kc + 1], 1.0))
    for _ in range(16):
        nc.vector.engine_nop()
    dv.inc(nc.vector.engine_nop())

    with ExitStack() as p0a:
        sgbuf = sb("sgbuf", [32, IPAD], BF16, p0a)
        zbuf = sb("zbuf", [32, IPAD // 2], U8, p0a)
        znib = sb("znib", [32, IPAD], U8, p0a)
        zsbuf = sb("zsbuf", [32, IPAD], BF16, p0a)
        for mat, (dsc_, dzb_) in enumerate([(d_gsc, d_gzb), (d_usc, d_uzb)]):
            if mat > 0:
                nc.sync.wait_ge(dv.h, dv.n)
            vl = lg.dinc(nc.gpsimd.dma_start(sgbuf[:, :], dsc_[:, :]))
            vz = ls.dinc(nc.sync.dma_start(zbuf[:, :], dzb_[:, :]))
            nc.vector.wait_ge(lg.h, vl)
            nc.vector.wait_ge(ls.h, vz)
            znv = znib[:, :].rearrange("p (b two) -> p b two", two=2)
            dv.inc(nc.vector.tensor_scalar(znv[:, :, 0], zbuf[:, :], 15, None, ALU.bitwise_and))
            dv.inc(nc.vector.tensor_scalar(znv[:, :, 1], zbuf[:, :], 4, None, ALU.logical_shift_right))
            dv.inc(nc.vector.scalar_tensor_tensor(
                zsbuf[:, :], znib[:, :], QS, sgbuf[:, :], ALU.subtract, ALU.mult))
            vneg = dv.inc(nc.vector.tensor_scalar(zsbuf[:, :], zsbuf[:, :], -1.0, None, ALU.mult))
            nc.gpsimd.wait_ge(dv.h, vneg)
            lg.dinc(nc.gpsimd.dma_start(s_sgb[mat], sgbuf[:, :]))
            lg.dinc(nc.gpsimd.dma_start(
                s_sgbblk[mat].transpose([1, 0, 2]),
                sgbuf[:, :].rearrange("g (c i) -> g c i", c=CC)))
            vwb = lg.dinc(nc.gpsimd.dma_start(s_zs[mat], zsbuf[:, :]))
            nc.gpsimd.wait_ge(lg.h, vwb)
        dsbuf = sb("dsbuf", [CC, DM], BF16, p0a)
        dzbuf = sb("dzbuf", [CC, 2048], U8, p0a)
        dznib = sb("dznib", [CC, DM], U8, p0a)
        dzsb = sb("dzsb", [CC, DM], BF16, p0a)
        vl = lg.dinc(nc.gpsimd.dma_start(dsbuf[:, :], d_dsc[:, :]))
        vz = ls.dinc(nc.sync.dma_start(dzbuf[:, :], d_dzb[:, :]))
        nc.vector.wait_ge(lg.h, vl)
        nc.vector.wait_ge(ls.h, vz)
        dznv = dznib[:, :].rearrange("p (b two) -> p b two", two=2)
        dv.inc(nc.vector.tensor_scalar(dznv[:, :, 0], dzbuf[:, :], 15, None, ALU.bitwise_and))
        dv.inc(nc.vector.tensor_scalar(dznv[:, :, 1], dzbuf[:, :], 4, None, ALU.logical_shift_right))
        dv.inc(nc.vector.scalar_tensor_tensor(
            dzsb[:, :], dznib[:, :], QS, dsbuf[:, :], ALU.subtract, ALU.mult))
        vneg = dv.inc(nc.vector.tensor_scalar(dzsb[:, :], dzsb[:, :], -1.0, None, ALU.mult))
        nc.gpsimd.wait_ge(dv.h, vneg)
        lg.dinc(nc.gpsimd.dma_start(s_dsb[:, :], dsbuf[:, :]))
        v_tables = lg.dinc(nc.gpsimd.dma_start(s_zsd[:, :], dzsb[:, :]))

    def dpad(k=12):
        for _ in range(k):
            nc.vector.engine_nop()

    def apad(k=2):
        for _ in range(k):
            ac.inc(nc.scalar.activation(svp[:, :], svp[:, :], AF.Identity))

    def ln_stats(x_tile, sq_scr):
        nc.vector.wait_ge(ac.h, ac.n)
        dv.inc(nc.vector.reduce_sum(sv[0][:, :], x_tile[:, :], AX.X, negate=True))
        nc.scalar.wait_ge(dv.h, dv.n)
        va = ac.inc(nc.scalar.activation(sq_scr[:, :], x_tile[:, :], AF.Square))
        nc.vector.wait_ge(ac.h, va)
        dv.inc(nc.vector.reduce_sum(sv[1][:, :], sq_scr[:, :], AX.X))
        dv.inc(nc.vector.tensor_scalar(sv[4][:, :], sv[0][:, :], 1.0 / DM, None, ALU.mult))
        dv.inc(nc.vector.scalar_tensor_tensor(
            sv[5][:, :], sv[0][:, :], 1.0 / DM ** 2, sv[0][:, :], ALU.mult, ALU.mult))
        dpad()
        vv = dv.inc(nc.vector.scalar_tensor_tensor(
            sv[1][:, :], sv[1][:, :], 1.0 / DM, sv[5][:, :], ALU.mult, ALU.subtract))
        dpad()
        nc.scalar.wait_ge(dv.h, vv)
        apad()
        ac.inc(nc.scalar.activation(sv[5][:, :], sv[1][:, :], AF.Ln, bias=np.float32(EPS_LN)))
        apad()
        va = ac.inc(nc.scalar.activation(sv[2][:, :], sv[5][:, :], AF.Exp, scale=np.float32(-0.5)))
        nc.vector.wait_ge(ac.h, va)
        dpad()
        dv.inc(nc.vector.tensor_tensor(sv[3][:, :], sv[4][:, :], sv[2][:, :], ALU.mult))
        dpad()
        vb = dv.inc(nc.vector.engine_nop())
        apad()
        return vb

    def transpose_tok_to_feat(tok_bf, destT, tt, psbs, ready_sem, ready_val, state):
        for k4 in range(KC // 4):
            pb = psbs[k4 % 2]
            nc.tensor.wait_ge(ready_sem.h, ready_val)
            if state[k4 % 2]:
                nc.tensor.wait_ge(dv.h, state[k4 % 2])
            for j in range(4):
                kc = k4 * 4 + j
                i = nc.tensor.transpose(
                    pb[:, j * 128:(j + 1) * 128],
                    tok_bf[:, kc * 128:(kc + 1) * 128], identb[:, :])
            vp = pe.inc(i)
            nc.vector.wait_ge(pe.h, vp)
            dview = destT[:, :].rearrange("p (kc t) -> p kc t", t=1024)
            state[k4 % 2] = dv.inc(nc.vector.tensor_copy(
                dview[:, k4 * 4:k4 * 4 + 4, tt * 128:(tt + 1) * 128],
                pb[:, :].rearrange("p (j t) -> p j t", t=128)))

    attn_stack = ExitStack()
    xln1T = sb("xln1T", [128, KC * 1024], BF16, attn_stack)
    nc.tensor.wait_ge(ls.h, v_const_ls)
    with ExitStack() as p0b:
        xtt = [sb(f"xtt{i}", [128, DM], F32, p0b) for i in range(2)]
        sq_scr = sb("sq0", [128, DM], F32, p0b)
        tokbf = sb("tokbf0", [128, DM], BF16, p0b)
        psb = [psum(f"psb0{i}", [128, 512], BF16, p0b) for i in range(2)]
        tstate = [0, 0]
        v_x_ld = [0] * NTT
        v_x_free = [0] * NTT
        for tt in range(NTT):
            if tt >= 2:
                nc.sync.wait_ge(ac.h, v_x_free[tt - 2])
            else:
                nc.sync.wait_ge(lg.h, v_tables)
            v_x_ld[tt] = lx.dinc(nc.sync.dma_start(
                xtt[tt % 2][:, :], d_x[tt * 128:(tt + 1) * 128, :]))
            nc.vector.wait_ge(lx.h, v_x_ld[tt])
            nc.scalar.wait_ge(lx.h, v_x_ld[tt])
            ln_stats(xtt[tt % 2], sq_scr)
            nc.scalar.wait_ge(dv.h, dv.n)
            v_x_free[tt] = ac.inc(nc.scalar.activation(
                tokbf[:, :], xtt[tt % 2][:, :], AF.Identity,
                bias=sv[3][:, :], scale=sv[2][:, :]))
            if dbg and tt == 0:
                dbgt = nc.dram_tensor("dbg_tok0", [128, DM], BF16, kind="ExternalOutput").ap()
                dbgx = nc.dram_tensor("dbg_xtt0", [128, DM], F32, kind="ExternalOutput").ap()
                dbgs = nc.dram_tensor("dbg_sv0", [128, 8], F32, kind="ExternalOutput").ap()
                dbgs2 = nc.dram_tensor("dbg_sv0_late", [128, 8], F32, kind="ExternalOutput").ap()
                nc.gpsimd.wait_ge(ac.h, v_x_free[tt])
                lg.dinc(nc.gpsimd.dma_start(dbgt[:, :], tokbf[:, :]))
                lg.dinc(nc.gpsimd.dma_start(dbgx[:, :], xtt[0][:, :]))
                for i in range(6):
                    lg.dinc(nc.gpsimd.dma_start(dbgs[:, i:i + 1], sv[i][:, :], allow_slow_non_contiguous=True))
                nc.scalar.wait_ge(lg.h, lg.n)
                nc.vector.wait_ge(lg.h, lg.n)
            if dbg and tt == 1:
                nc.gpsimd.wait_ge(ac.h, ac.n)
                for i in range(6):
                    lg.dinc(nc.gpsimd.dma_start(dbgs2[:, i:i + 1], sv[i][:, :], allow_slow_non_contiguous=True))
                nc.vector.wait_ge(lg.h, lg.n)
                nc.scalar.wait_ge(lg.h, lg.n)
            transpose_tok_to_feat(tokbf, xln1T, tt, psb, ac, v_x_free[tt], tstate)
            if dbg and tt == 1:
                dbge = nc.dram_tensor("dbg_xln1T_early", [128, KC * 1024], BF16, kind="ExternalOutput").ap()
                nc.gpsimd.wait_ge(dv.h, dv.n)
                lg.dinc(nc.gpsimd.dma_start(dbge[:, :], xln1T[:, :]))
        v_xln1T = dv.n

    sig_q = sb("sig_q", [128, FCN * 1024], BF16, attn_stack)
    nc.vector.wait_ge(ls.h, v_const_ls)
    with ExitStack() as p1:
        wq_sb = [sb(f"wq{i}", [128, KC * 128], BF16, p1) for i in range(2)]
        scr_r = sb("scr_r", [128, 1024], F32, p1)
        scr_m = sb("scr_m", [128, 1024], F32, p1)
        v_wq_ld = [0] * FCN
        v_q_mm = [0] * FCN
        v_q_evac = [0] * FCN
        for it in range(FCN + 1):
            if it < FCN:
                if it >= 2:
                    nc.gpsimd.wait_ge(pe.h, v_q_mm[it - 2])
                else:
                    nc.gpsimd.wait_ge(dv.h, v_xln1T)
                v_wq_ld[it] = lg.dinc(nc.gpsimd.dma_start(
                    wq_sb[it % 2][:, :], d_wqsel[it].rearrange("p kc f -> p (kc f)")))
            fc = it - 1
            if fc < 0:
                continue
            pa, pb = (ps[0], ps[1]) if fc % 2 == 0 else (ps[2], ps[3])
            nc.tensor.wait_ge(lg.h, v_wq_ld[fc])
            if fc == 0:
                nc.tensor.wait_ge(dv.h, v_xln1T)
            if fc >= 2:
                nc.tensor.wait_ge(dv.h, v_q_evac[fc - 2])
            for half, pp in ((0, pa), (1, pb)):
                for kc in range(KC):
                    i = nc.tensor.matmul(
                        pp[:, :], wq_sb[fc % 2][:, kc * 128:(kc + 1) * 128],
                        xln1T[:, kc * 1024 + half * 512: kc * 1024 + (half + 1) * 512],
                        start=(kc == 0), stop=(kc == KC - 1))
            v_q_mm[fc] = pe.inc(i)
            nc.vector.wait_ge(pe.h, v_q_mm[fc])
            bqc = bqsel[:, fc:fc + 1]
            for half, pp in ((0, pa), (1, pb)):
                slh = slice(half * 512, (half + 1) * 512)
                dv.inc(nc.vector.tensor_scalar(scr_r[:, slh], pp[:, :], bqc, 0.0, ALU.add, ALU.max))
                dv.inc(nc.vector.tensor_scalar(scr_m[:, slh], pp[:, :], bqc, 0.0, ALU.add, ALU.min))
            nc.scalar.wait_ge(dv.h, dv.n)
            va = ac.inc(nc.scalar.activation(scr_m[:, :], scr_m[:, :], AF.Exp))
            nc.vector.wait_ge(ac.h, va)
            v_q_evac[fc] = dv.inc(nc.vector.tensor_tensor(
                sig_q[:, fc * 1024:(fc + 1) * 1024], scr_r[:, :], scr_m[:, :], ALU.add))

    if dbg:
        dbg1 = nc.dram_tensor("dbg_xln1T", [128, KC * 1024], BF16, kind="ExternalOutput").ap()
        dbg2 = nc.dram_tensor("dbg_sigq", [128, FCN * 1024], BF16, kind="ExternalOutput").ap()
        nc.gpsimd.wait_ge(dv.h, dv.n)
        lg.dinc(nc.gpsimd.dma_start(dbg1[:, :], xln1T[:, :]))
        lg.dinc(nc.gpsimd.dma_start(dbg2[:, :], sig_q[:, :]))

    ret = sb("ret", [128, HKV * 1024], BF16, attn_stack)
    with ExitStack() as p2:
        den_sb = sb("den_sb", [1, 1024], F32, p2)
        rden_bf = sb("rden_bf", [1, 1024], BF16, p2)
        rb_sb = sb("rb_sb", [128, 1024], BF16, p2)
        nc.tensor.wait_ge(lg.h, v_const_lg)
        nc.tensor.wait_ge(dv.h, dv.n)
        for h in range(HKV):
            sq = sig_q[:, h * 1024:(h + 1) * 1024]
            for half, pp in ((0, ps[0]), (1, ps[1])):
                i = nc.tensor.matmul(pp[:, :], m_sb[:, h * 128:(h + 1) * 128],
                                     sq[:, half * 512:(half + 1) * 512],
                                     start=True, stop=True)
            for half, pp in ((0, ps[2]), (1, ps[3])):
                i = nc.tensor.matmul(pp[0:1, :], z_sb[:, h:h + 1],
                                     sq[:, half * 512:(half + 1) * 512],
                                     start=True, stop=True)
            v_nd = pe.inc(i)
            nc.vector.wait_ge(pe.h, v_nd)
            dv.inc(nc.vector.tensor_scalar(den_sb[:, 0:512], ps[2][0:1, :], EPS_DEN, None, ALU.add))
            dv.inc(nc.vector.tensor_scalar(den_sb[:, 512:1024], ps[3][0:1, :], EPS_DEN, None, ALU.add))
            dv.inc(nc.vector.reciprocal(den_sb[:, :], den_sb[:, :]))
            v_rd = dv.inc(nc.vector.tensor_copy(rden_bf[:, :], den_sb[:, :]))
            nc.tensor.wait_ge(dv.h, v_rd)
            for half, pp in ((0, ps[4]), (1, ps[5])):
                i = nc.tensor.matmul(pp[:, :], ones1[0:1, :],
                                     rden_bf[0:1, half * 512:(half + 1) * 512],
                                     start=True, stop=True)
            v_bc = pe.inc(i)
            nc.vector.wait_ge(pe.h, v_bc)
            dv.inc(nc.vector.tensor_copy(rb_sb[:, 0:512], ps[4][:, :]))
            dv.inc(nc.vector.tensor_copy(rb_sb[:, 512:1024], ps[5][:, :]))
            dv.inc(nc.vector.tensor_tensor(ret[:, h * 1024:h * 1024 + 512],
                                           ps[0][:, :], rb_sb[:, 0:512], ALU.mult))
            v_ret = dv.inc(nc.vector.tensor_tensor(
                ret[:, h * 1024 + 512:(h + 1) * 1024],
                ps[1][:, :], rb_sb[:, 512:1024], ALU.mult))
            nc.tensor.wait_ge(dv.h, v_ret)
        v_p2_done = dv.n

    if dbg:
        dbg3 = nc.dram_tensor("dbg_ret", [128, HKV * 1024], BF16, kind="ExternalOutput").ap()
        nc.gpsimd.wait_ge(dv.h, dv.n)
        lg.dinc(nc.gpsimd.dma_start(dbg3[:, :], ret[:, :]))

    with ExitStack() as p3:
        wo_sb = [sb(f"wo{i}", [128, KC * 128], BF16, p3) for i in range(3)]
        woeff = [sb(f"woe{i}", [128, HKV * 128], BF16, p3) for i in range(2)]
        ao_sb = [sb(f"ao{i}", [128, 1024], F32, p3) for i in range(2)]
        v_wo_ld = [0] * OCN
        v_wo_add = [0] * OCN
        v_wo_mm = [0] * OCN
        v_wo_evac = [0] * OCN
        v_wo_spill = [0] * OCN
        for it in range(OCN + 2):
            if it < OCN:
                if it >= 3:
                    nc.gpsimd.wait_ge(dv.h, v_wo_add[it - 3])
                elif it == 0:
                    nc.gpsimd.wait_ge(dv.h, v_p2_done)

                v_wo_ld[it] = lg.dinc(nc.gpsimd.dma_start(
                    wo_sb[it % 3][:, :], d_wo[it].rearrange("p fc o -> p (fc o)")))
            oc = it - 2
            if oc < 0:
                continue
            w = wo_sb[oc % 3]
            we = woeff[oc % 2]
            nc.vector.wait_ge(lg.h, v_wo_ld[oc])
            if oc >= 2:
                nc.vector.wait_ge(pe.h, v_wo_mm[oc - 2])
            for hk in range(HKV):
                b = 4 * hk * 128
                dsl = slice(hk * 128, (hk + 1) * 128)
                dv.inc(nc.vector.tensor_tensor(we[:, dsl], w[:, b:b + 128],
                                               w[:, b + 128:b + 256], ALU.add))
                dv.inc(nc.vector.tensor_tensor(we[:, dsl], we[:, dsl],
                                               w[:, b + 256:b + 384], ALU.add))
                dv.inc(nc.vector.tensor_tensor(we[:, dsl], we[:, dsl],
                                               w[:, b + 384:b + 512], ALU.add))
            dpad()
            v_wo_add[oc] = dv.inc(nc.vector.engine_nop())
            pa, pb = (ps[0], ps[1]) if oc % 2 == 0 else (ps[2], ps[3])
            nc.tensor.wait_ge(dv.h, v_wo_add[oc])
            if oc >= 2:
                nc.tensor.wait_ge(dv.h, v_wo_evac[oc - 2])
            for half, pp in ((0, pa), (1, pb)):
                for hk in range(HKV):
                    i = nc.tensor.matmul(
                        pp[:, :], we[:, hk * 128:(hk + 1) * 128],
                        ret[:, hk * 1024 + half * 512: hk * 1024 + (half + 1) * 512],
                        start=(hk == 0), stop=(hk == HKV - 1))
            v_wo_mm[oc] = pe.inc(i)
            nc.vector.wait_ge(pe.h, v_wo_mm[oc])
            if oc >= 2:
                nc.vector.wait_ge(lg.h, v_wo_spill[oc - 2])
            dv.inc(nc.vector.tensor_copy(ao_sb[oc % 2][:, 0:512], pa[:, :]))
            v_wo_evac[oc] = dv.inc(nc.vector.tensor_copy(ao_sb[oc % 2][:, 512:1024], pb[:, :]))
            nc.gpsimd.wait_ge(dv.h, v_wo_evac[oc])
            v_wo_spill[oc] = lg.dinc(nc.gpsimd.dma_start(s_attn[oc], ao_sb[oc % 2][:, :]))
        v_attn_done = v_wo_spill[OCN - 1]

    attn_stack.close()
    ffn_stack = ExitStack()
    xln2T = sb("xln2T", [128, KC * 1024], BF16, ffn_stack)
    with ExitStack() as p4:
        ablk = [sb(f"ablk{i}", [128, DM], F32, p4) for i in range(2)]
        xtt = [sb(f"xtt4{i}", [128, DM], F32, p4) for i in range(2)]
        atok = sb("atok", [128, DM], F32, p4)
        sq_scr = sb("sq4", [128, DM], F32, p4)
        tokbf = sb("tokbf4", [128, DM], BF16, p4)
        x1sb = sb("x1sb", [128, DM], F32, p4)
        psb = [psum(f"psb4{i}", [128, 512], BF16, p4) for i in range(2)]
        tstate = [0, 0]
        pst = [ps[0], ps[1]]
        astate = [0, 0]
        v_a_ld = [0] * NTT
        v_a_free = [0] * NTT
        v_x1_sp = [0] * NTT
        nc.sync.wait_ge(lg.h, v_attn_done)
        for it in range(NTT + 1):
            if it < NTT:
                if it >= 2:
                    nc.sync.wait_ge(dv.h, v_a_free[it - 2])
                ls.dinc(nc.sync.dma_start(
                    ablk[it % 2][:, :].rearrange("p (oc t) -> p oc t", t=128),
                    s_attn[:, :, it * 128:(it + 1) * 128].transpose([1, 0, 2])))
                v_a_ld[it] = ls.dinc(nc.sync.dma_start(
                    xtt[it % 2][:, :], d_x[it * 128:(it + 1) * 128, :]))
            tt = it - 1
            if tt < 0:
                continue
            for k4 in range(KC // 4):
                pb = pst[k4 % 2]
                nc.tensor.wait_ge(ls.h, v_a_ld[tt])
                if astate[k4 % 2]:
                    nc.tensor.wait_ge(dv.h, astate[k4 % 2])
                for j in range(4):
                    oc = k4 * 4 + j
                    i = nc.tensor.transpose(
                        pb[:, j * 128:(j + 1) * 128],
                        ablk[tt % 2][:, oc * 128:(oc + 1) * 128], identf[:, :])
                vp = pe.inc(i)
                nc.vector.wait_ge(pe.h, vp)
                astate[k4 % 2] = dv.inc(nc.vector.tensor_copy(
                    atok[:, :].rearrange("p (k t) -> p k t", t=128)[:, k4 * 4:k4 * 4 + 4, :],
                    pb[:, :].rearrange("p (j t) -> p j t", t=128)))
            ln_stats(atok, sq_scr)
            nc.scalar.wait_ge(dv.h, dv.n)
            va = ac.inc(nc.scalar.activation(sq_scr[:, :], atok[:, :], AF.Identity,
                                             bias=sv[3][:, :], scale=sv[2][:, :]))
            nc.vector.wait_ge(ac.h, va)
            nc.vector.wait_ge(ls.h, v_a_ld[tt])
            vx1 = dv.inc(nc.vector.tensor_tensor(x1sb[:, :], xtt[tt % 2][:, :],
                                                 sq_scr[:, :], ALU.add))
            v_a_free[tt] = vx1
            nc.gpsimd.wait_ge(dv.h, vx1)
            v_x1_sp[tt] = lg.dinc(nc.gpsimd.dma_start(
                s_x1[tt * 128:(tt + 1) * 128, :], x1sb[:, :]))
            ln_stats(x1sb, sq_scr)
            nc.scalar.wait_ge(dv.h, dv.n)
            va = ac.inc(nc.scalar.activation(tokbf[:, :], x1sb[:, :], AF.Identity,
                                             bias=sv[3][:, :], scale=sv[2][:, :]))
            transpose_tok_to_feat(tokbf, xln2T, tt, psb, ac, va, tstate)
        v_x1_done = v_x1_sp[NTT - 1]

    if dbg:
        dbg4 = nc.dram_tensor("dbg_xln2T", [128, KC * 1024], BF16, kind="ExternalOutput").ap()
        nc.gpsimd.wait_ge(dv.h, dv.n)
        lg.dinc(nc.gpsimd.dma_start(dbg4[:, :], xln2T[:, :]))

    xgT = sb("xgT", [32, 1024], BF16, ffn_stack)
    nc.tensor.wait_ge(dv.h, dv.n)
    for half, pp in ((0, ps[0]), (1, ps[1])):
        for kc in range(KC):
            i = nc.tensor.matmul(
                pp[0:32, :], oneh32[:, kc * 32:(kc + 1) * 32],
                xln2T[:, kc * 1024 + half * 512: kc * 1024 + (half + 1) * 512],
                start=(kc == 0), stop=(kc == KC - 1))
    v5 = pe.inc(i)
    nc.vector.wait_ge(pe.h, v5)
    dv.inc(nc.vector.tensor_copy(xgT[:, 0:512], ps[0][0:32, :]))
    v_xg = dv.inc(nc.vector.tensor_copy(xgT[:, 512:1024], ps[1][0:32, :]))

    with ExitStack() as p6:
        ps6 = [psum(f"ps6{i}", [128, 512], F32, p6) for i in range(2)]
        pall = ps + ps6
        NS6 = 3
        gub = [[sb(f"byt{m}{i}", [128, 2048], U8, p6) for i in range(NS6)] for m in range(2)]
        zssl = [[sb(f"zssl{m}{i}", [32, 128], BF16, p6) for i in range(NS6)] for m in range(2)]
        sbc = [[sb(f"sbc{m}{i}", [128, DM], BF16, p6) for i in range(NS6)] for m in range(2)]
        wgu = [[sb(f"wgu{m}{i}", [128, DM], BF16, p6) for i in range(NS6)] for m in range(2)]
        nib = [sb(f"nib{m}", [128, DM], U8, p6) for m in range(2)]
        silu_sb = sb("silu_sb", [128, 1024], BF16, p6)
        h_sb = [sb(f"hsb{i}", [128, 1024], BF16, p6) for i in range(2)]
        nc.sync.wait_ge(lg.h, v_tables)
        v6_ld = [0] * CC
        v6_deq = [0] * CC
        v6_mm = [0] * CC
        v6_h = [0] * CC
        v6_hs = [0] * CC
        PF = 2
        srcs = [(d_gb, s_sgbblk[0], s_zs[0]), (d_ub, s_sgbblk[1], s_zs[1])]
        for it in range(CC + PF):
            c = it
            if c < CC:
                slc = c % NS6
                if c >= NS6:
                    nc.sync.wait_ge(dv.h, v6_deq[c - NS6])
                    nc.sync.wait_ge(pe.h, v6_mm[c - NS6])
                elif c == 0:
                    nc.sync.wait_ge(dv.h, v_xg)
                for m, (dby, dsgb, dzs) in enumerate(srcs):
                    ls.dinc(nc.sync.dma_start(
                        gub[m][slc][:, :], dby[c].rearrange("p kc b -> p (kc b)")))
                    ls.dinc(nc.sync.dma_start(
                        zssl[m][slc][:, :], dzs[:, c * 128:(c + 1) * 128]))
                    ls.dinc(nc.sync.dma_start(
                        sbc[m][slc][:, :],
                        dsgb[c].rearrange("g i -> (g i)").unsqueeze(0)
                        .partition_broadcast(128).squeeze(1)))
                v6_ld[c] = ls.n
            cd = it - 1
            if 0 <= cd < CC:
                slc = cd % NS6
                nc.vector.wait_ge(ls.h, v6_ld[cd])
                if cd >= NS6:
                    nc.vector.wait_ge(pe.h, v6_mm[cd - NS6])
                for m in range(2):
                    bt = gub[m][slc][:, :].rearrange("p (kc b) -> p kc b", kc=KC)
                    nv = nib[m][:, :].rearrange("p (kc b two) -> p kc b two", kc=KC, two=2)
                    dv.inc(nc.vector.tensor_scalar(nv[:, :, :, 0], bt, 15, None, ALU.bitwise_and))
                    dv.inc(nc.vector.tensor_scalar(nv[:, :, :, 1], bt, 4, None, ALU.logical_shift_right))
                    dv.inc(nc.vector.scalar_tensor_tensor(
                        wgu[m][slc][:, :], nib[m][:, :], QS, sbc[m][slc][:, :],
                        ALU.subtract, ALU.mult))
                v6_deq[cd] = dv.n
            cc = it - PF
            if cc < 0:
                continue
            slc = cc % NS6
            pbank = pall[0:4] if cc % 2 == 0 else pall[4:8]
            nc.tensor.wait_ge(dv.h, v6_deq[cc])
            if cc >= 2:
                nc.tensor.wait_ge(dv.h, v6_h[cc - 2])
            for m in range(2):
                for half in range(2):
                    pp = pbank[m * 2 + half]
                    i = nc.tensor.matmul(pp[:, :], zssl[m][slc][:, :],
                                         xgT[:, half * 512:(half + 1) * 512],
                                         start=True, stop=False)
                    for kc in range(KC):
                        i = nc.tensor.matmul(
                            pp[:, :], wgu[m][slc][:, kc * 128:(kc + 1) * 128],
                            xln2T[:, kc * 1024 + half * 512: kc * 1024 + (half + 1) * 512],
                            start=False, stop=(kc == KC - 1))
            v6_mm[cc] = pe.inc(i)
            nc.scalar.wait_ge(pe.h, v6_mm[cc])
            ac.inc(nc.scalar.activation(silu_sb[:, 0:512], pbank[0][:, :], AF.Silu))
            v_silu = ac.inc(nc.scalar.activation(silu_sb[:, 512:1024], pbank[1][:, :], AF.Silu))
            nc.vector.wait_ge(ac.h, v_silu)
            hsl = cc % 2
            if cc >= 2:
                nc.vector.wait_ge(lg.h, v6_hs[cc - 2])
            dv.inc(nc.vector.tensor_tensor(h_sb[hsl][:, 0:512], silu_sb[:, 0:512],
                                           pbank[2][:, :], ALU.mult))
            v6_h[cc] = dv.inc(nc.vector.tensor_tensor(
                h_sb[hsl][:, 512:1024], silu_sb[:, 512:1024], pbank[3][:, :], ALU.mult))
            nc.gpsimd.wait_ge(dv.h, v6_h[cc])
            v6_hs[cc] = lg.dinc(nc.gpsimd.dma_start(s_h[cc], h_sb[hsl][:, :]))
        v_h_done = v6_hs[CC - 1]

    ffn_stack.close()
    hgT_bf = sb("hgT_bf", [CC, 1024], BF16)
    with ExitStack() as p7:
        wd = [[sb(f"wd{b}_{j}", [128, 2048], BF16, p7) for j in range(DB)] for b in range(2)]
        wdby = [sb(f"wdby{j}", [128, 1024], U8, p7) for j in range(DB)]
        h7 = [[sb(f"h7{b}_{j}", [128, 1024], BF16, p7) for j in range(DB)] for b in range(2)]
        sdbc = [sb(f"sdbc{j}", [128, 2048], BF16, p7) for j in range(DB)]
        nibd = sb("nibd", [128, 2048], U8, p7)
        oneh88 = sb("oneh88", [128, DB * CC], BF16, p7)
        zsd_sb = sb("zsd_sb", [CC, 2048], BF16, p7)
        stg = [sb(f"stg{i}", [128, 1024], F32, p7) for i in range(2)]
        nc.sync.wait_ge(lg.h, v_h_done)

        pd = ps[0:4]
        phg = (ps[4][0:CC, :], ps[5][0:CC, :])
        NG = 2 * NR

        v_acc = {}
        v_stg = {}
        v_evac = {}
        last_deq = 0
        v_band_ld = {}
        v_deq = {}
        v_mmband = {}
        v_hg_pe = 0
        v_hgT = 0
        v_oneh = 0
        v_zsd_ld = [0, 0]

        def issue_band_loads(g):
            nonlocal last_deq
            ph, r = g // NR, g % NR
            buf = g % 2
            if last_deq:
                nc.sync.wait_ge(dv.h, last_deq)
            if g >= 2:
                nc.sync.wait_ge(pe.h, v_mmband[g - 2])
            for j in range(DB):
                c = r * DB + j
                ls.dinc(nc.sync.dma_start(
                    wdby[j][:, :], d_db[c][:, ph * 1024:(ph + 1) * 1024]))
                ls.dinc(nc.sync.dma_start(h7[buf][j][:, :], s_h[c]))
                ls.dinc(nc.sync.dma_start(
                    sdbc[j][:, :],
                    s_dsb[c:c + 1, ph * 2048:(ph + 1) * 2048]
                    .partition_broadcast(128).squeeze(1)))
            v_band_ld[g] = ls.n

        def deq_band(g):
            nonlocal last_deq
            buf = g % 2
            nc.vector.wait_ge(ls.h, v_band_ld[g])
            if g >= 2:
                nc.vector.wait_ge(pe.h, v_mmband[g - 2])
            for j in range(DB):
                nv = nibd[:, :].rearrange("p (b two) -> p b two", two=2)
                dv.inc(nc.vector.tensor_scalar(nv[:, :, 0], wdby[j][:, :], 15, None, ALU.bitwise_and))
                dv.inc(nc.vector.tensor_scalar(nv[:, :, 1], wdby[j][:, :], 4, None, ALU.logical_shift_right))
                dv.inc(nc.vector.scalar_tensor_tensor(
                    wd[buf][j][:, :], nibd[:, :], QS, sdbc[j][:, :],
                    ALU.subtract, ALU.mult))
            dpad()
            last_deq = dv.inc(nc.vector.engine_nop())
            v_deq[g] = last_deq

        def oneh_build(g):
            nonlocal v_oneh
            r = g % NR
            nc.vector.wait_ge(pe.h, v_hg_pe)
            dv.inc(nc.vector.memset(oneh88[:, :], 0.0))
            for j in range(DB):
                c = r * DB + j
                dv.inc(nc.vector.memset(oneh88[:, j * CC + c:j * CC + c + 1], 1.0))
            dpad(16)
            v_oneh = dv.inc(nc.vector.engine_nop())

        def mm_round(g):
            nonlocal v_hg_pe, v_hgT
            ph, r = g // NR, g % NR
            buf = g % 2
            if ph == 0:
                nc.tensor.wait_ge(dv.h, v_oneh)
                nc.tensor.wait_ge(ls.h, v_band_ld[g])
                for j in range(DB):
                    for half, pp in ((0, phg[0]), (1, phg[1])):
                        i = nc.tensor.matmul(
                            pp, oneh88[:, j * CC:(j + 1) * CC],
                            h7[buf][j][:, half * 512:(half + 1) * 512],
                            start=(r == 0 and j == 0),
                            stop=(r == NR - 1 and j == DB - 1))
                v_hg_pe = pe.inc(i)
                if r == NR - 1:
                    nc.vector.wait_ge(pe.h, v_hg_pe)
                    dv.inc(nc.vector.tensor_copy(hgT_bf[:, 0:512], phg[0]))
                    dv.inc(nc.vector.tensor_copy(hgT_bf[:, 512:1024], phg[1]))
                    dpad()
                    v_hgT = dv.inc(nc.vector.engine_nop())
            nc.tensor.wait_ge(dv.h, v_deq[g])
            nc.tensor.wait_ge(ls.h, v_band_ld[g])
            if g % NR == NR - 1:
                nc.tensor.wait_ge(dv.h, v_hgT)
                nc.tensor.wait_ge(ls.h, v_zsd_ld[ph])
            for ol in range(16):
                oc = ph * 16 + ol
                pair = ol % 2
                if pair in v_evac:
                    nc.tensor.wait_ge(dv.h, v_evac[pair])
                ppa, ppb = pd[pair * 2], pd[pair * 2 + 1]
                for half, pp in ((0, ppa), (1, ppb)):
                    for j in range(DB):
                        i = nc.tensor.matmul(
                            pp[:, :], wd[buf][j][:, ol * 128:(ol + 1) * 128],
                            h7[buf][j][:, half * 512:(half + 1) * 512],
                            start=(j == 0), stop=(j == DB - 1 and r != NR - 1))
                    if r == NR - 1:
                        i = nc.tensor.matmul(
                            pp[:, :], zsd_sb[:, ol * 128:(ol + 1) * 128],
                            hgT_bf[:, half * 512:(half + 1) * 512],
                            start=False, stop=True)
                v_mmoc = pe.inc(i)
                nc.vector.wait_ge(pe.h, v_mmoc)
                slot = ol % 2
                if slot in v_stg:
                    nc.vector.wait_ge(lg.h, v_stg[slot])
                sg = stg[slot]
                dv.inc(nc.vector.tensor_copy(sg[:, 0:512], ppa[:, :]))
                v_ev = dv.inc(nc.vector.tensor_copy(sg[:, 512:1024], ppb[:, :]))
                v_evac[pair] = v_ev
                nc.gpsimd.wait_ge(dv.h, v_ev)
                if oc in v_acc:
                    nc.gpsimd.wait_ge(lg.h, v_acc[oc])
                op = ALU.bypass if r == 0 else ALU.add
                v_acc[oc] = lg.dinc(nc.gpsimd.dma_start(
                    s_oacc[oc], sg[:, :], accum_op=op))
                v_stg[slot] = v_acc[oc]
            v_mmband[g] = pe.n

        v_zsd_ld[0] = ls.dinc(nc.sync.dma_start(
            zsd_sb[:, :], s_zsd[:, 0:2048]))
        issue_band_loads(0)
        deq_band(0)
        issue_band_loads(1)
        for g in range(NG):
            if g < NR:
                oneh_build(g)
            if g + 1 < NG:
                deq_band(g + 1)
            mm_round(g)
            if g == NR - 1:
                nc.sync.wait_ge(pe.h, pe.n)
                v_zsd_ld[1] = ls.dinc(nc.sync.dma_start(
                    zsd_sb[:, :], s_zsd[:, 2048:4096]))
            if g + 2 < NG:
                issue_band_loads(g + 2)
        v_oacc_done = max(v_acc.values())

    with ExitStack() as p8:
        oblk = [sb(f"oblk{i}", [128, DM], F32, p8) for i in range(2)]
        x1tt = [sb(f"x1tt{i}", [128, DM], F32, p8) for i in range(2)]
        ftok = sb("ftok", [128, DM], F32, p8)
        sq_scr = sb("sq8", [128, DM], F32, p8)
        out_sb = [sb(f"outsb{i}", [128, DM], F32, p8) for i in range(2)]
        pst = [ps[0], ps[1]]
        ostate = [0, 0]
        v_o_ld = [0] * NTT
        v_o_free = [0] * NTT
        v_out_sp = [0] * NTT
        nc.sync.wait_ge(lg.h, v_oacc_done)
        for it in range(NTT + 1):
            if it < NTT:
                if it >= 2:
                    nc.sync.wait_ge(dv.h, v_o_free[it - 2])
                ls.dinc(nc.sync.dma_start(
                    oblk[it % 2][:, :].rearrange("p (oc t) -> p oc t", t=128),
                    s_oacc[:, :, it * 128:(it + 1) * 128].transpose([1, 0, 2])))
                v_o_ld[it] = ls.dinc(nc.sync.dma_start(
                    x1tt[it % 2][:, :], s_x1[it * 128:(it + 1) * 128, :]))
            tt = it - 1
            if tt < 0:
                continue
            for k4 in range(KC // 4):
                pb = pst[k4 % 2]
                nc.tensor.wait_ge(ls.h, v_o_ld[tt])
                if ostate[k4 % 2]:
                    nc.tensor.wait_ge(dv.h, ostate[k4 % 2])
                for j in range(4):
                    oc = k4 * 4 + j
                    i = nc.tensor.transpose(
                        pb[:, j * 128:(j + 1) * 128],
                        oblk[tt % 2][:, oc * 128:(oc + 1) * 128], identf[:, :])
                vp = pe.inc(i)
                nc.vector.wait_ge(pe.h, vp)
                ostate[k4 % 2] = dv.inc(nc.vector.tensor_copy(
                    ftok[:, :].rearrange("p (k t) -> p k t", t=128)[:, k4 * 4:k4 * 4 + 4, :],
                    pb[:, :].rearrange("p (j t) -> p j t", t=128)))
            ln_stats(ftok, sq_scr)
            nc.scalar.wait_ge(dv.h, dv.n)
            va = ac.inc(nc.scalar.activation(sq_scr[:, :], ftok[:, :], AF.Identity,
                                             bias=sv[3][:, :], scale=sv[2][:, :]))
            nc.vector.wait_ge(ac.h, va)
            nc.vector.wait_ge(ls.h, v_o_ld[tt])
            if tt >= 2:
                nc.vector.wait_ge(st.h, v_out_sp[tt - 2])
            v_o_free[tt] = dv.inc(nc.vector.tensor_tensor(
                out_sb[tt % 2][:, :], x1tt[tt % 2][:, :], sq_scr[:, :], ALU.add))
            nc.gpsimd.wait_ge(dv.h, v_o_free[tt])
            v_out_sp[tt] = st.dinc(nc.gpsimd.dma_start(
                d_out[tt * 128:(tt + 1) * 128, :], out_sb[tt % 2][:, :]))
        nc.gpsimd.wait_ge(st.h, v_out_sp[NTT - 1])

    top.close()
    return nc


_NC_CACHE = None


def kernel(**inputs):
    global _NC_CACHE
    x_shards, common = _host_prep(inputs)
    if _NC_CACHE is None:
        _NC_CACHE = build_program()
    nc = _NC_CACHE
    in_maps = [dict(common, x=x_shards[c]) for c in range(NCORE)]
    res = run_bass_kernel_spmd(nc, in_maps, core_ids=list(range(NCORE)))
    out = np.concatenate([np.asarray(r["out"]) for r in res.results], axis=0)
    return out.reshape(4, 2048, DM).astype(np.float32)


# version 13
# speedup vs baseline: 1.1303x; 1.0948x over previous
import os
import sys

if os.path.isdir("/opt/trn_rl_repo") and "/opt/trn_rl_repo" not in sys.path:
    sys.path.insert(0, "/opt/trn_rl_repo")

from contextlib import ExitStack

import numpy as np

import concourse.bass as bass
import concourse.mybir as mybir
from concourse.bass_utils import run_bass_kernel_spmd

F32 = mybir.dt.float32
BF16 = mybir.dt.bfloat16
U8 = mybir.dt.uint8
AF = mybir.ActivationFunctionType
ALU = mybir.AluOpType
AX = mybir.AxisListType

NCORE = 8
T = 1024
NTT = 8
DM = 4096
KC = 32
FCN = 8
OCN = 32
CC = 88
IPAD = CC * 128
HKV = 8
DB = 8
NR = CC // DB
EPS_LN = 1e-5
EPS_DEN = 1e-6
QS = 7.5


def _host_prep(inputs):
    f32 = np.float32
    X = np.asarray(inputs["L0_attn_hidden_states"], dtype=f32).reshape(NCORE * T, DM)
    x_shards = [np.ascontiguousarray(X[c * T:(c + 1) * T]) for c in range(NCORE)]

    Wq = np.asarray(inputs["L0_attn_Wq"], dtype=f32)
    wq3 = Wq.reshape(32, 128, DM)[0::4]
    wqsel = np.ascontiguousarray(
        wq3.transpose(0, 2, 1).reshape(FCN, KC, 128, 128).transpose(0, 2, 1, 3))
    bq = np.asarray(inputs["L0_attn_bq"], dtype=f32)
    bqsel = np.ascontiguousarray(bq.reshape(32, 128)[0::4].T)

    Wo = np.asarray(inputs["L0_attn_Wo"], dtype=f32)
    wo_blk = np.ascontiguousarray(Wo.reshape(OCN, 128, KC, 128).transpose(0, 3, 2, 1))

    M = np.asarray(inputs["L0_attn_M"], dtype=f32)
    m_host = np.ascontiguousarray(M.transpose(1, 0, 2).reshape(128, HKV * 128))
    z_host = np.ascontiguousarray(np.asarray(inputs["L0_attn_z"], dtype=f32).T)

    def _gu_bytes(qw):
        qv = np.ascontiguousarray(np.asarray(qw)).view(np.uint8)
        qv = np.concatenate(
            [qv, np.zeros((DM, IPAD // 2 - qv.shape[1]), np.uint8)], axis=1)
        return np.ascontiguousarray(
            qv.reshape(KC, 128, CC, 64).transpose(2, 1, 0, 3))

    g_bytes = _gu_bytes(inputs["L0_ffn_gate_qweight"])
    u_bytes = _gu_bytes(inputs["L0_ffn_up_qweight"])

    dqv = np.ascontiguousarray(np.asarray(inputs["L0_ffn_down_qweight"])).view(np.uint8)
    dqv = np.concatenate([dqv, np.zeros((IPAD - dqv.shape[0], 2048), np.uint8)], axis=0)
    d_bytes = np.ascontiguousarray(dqv.reshape(CC, 128, 2048))

    g_sc = np.zeros((32, IPAD), f32)
    g_sc[:, :11008] = np.asarray(inputs["L0_ffn_gate_scales"], dtype=f32)
    u_sc = np.zeros((32, IPAD), f32)
    u_sc[:, :11008] = np.asarray(inputs["L0_ffn_up_scales"], dtype=f32)
    d_sc = np.zeros((CC, DM), f32)
    d_sc[:86] = np.asarray(inputs["L0_ffn_down_scales"], dtype=f32)

    def _zb(qz, rows, cols):
        v = np.ascontiguousarray(np.asarray(qz)).view(np.uint8)
        out = np.zeros((rows, cols), np.uint8)
        out[:v.shape[0], :v.shape[1]] = v
        return out

    g_zb = _zb(inputs["L0_ffn_gate_qzeros"], 32, IPAD // 2)
    u_zb = _zb(inputs["L0_ffn_up_qzeros"], 32, IPAD // 2)
    d_zb = _zb(inputs["L0_ffn_down_qzeros"], CC, 2048)

    import ml_dtypes
    ident_bf = np.eye(128, dtype=ml_dtypes.bfloat16)
    ident_f32 = np.eye(128, dtype=np.float32)

    common = dict(
        wqsel=wqsel, bqsel=bqsel, wo_blk=wo_blk, m_h=m_host, z_h=z_host,
        g_bytes=g_bytes, u_bytes=u_bytes, d_bytes=d_bytes,
        g_sc=g_sc, u_sc=u_sc, d_sc=d_sc, g_zb=g_zb, u_zb=u_zb, d_zb=d_zb,
        ident_bf=ident_bf, ident_f32=ident_f32)
    return x_shards, common


class Sem:
    def __init__(self, nc, stack, name):
        self.h = stack.enter_context(nc.semaphore(name))
        self.n = 0

    def inc(self, instr, k=1):
        self.n += k
        instr.then_inc(self.h, k)
        return self.n

    def dinc(self, instr):
        return self.inc(instr, 16)


def build_program(dbg=False):
    nc = bass.Bass("TRN2", target_bir_lowering=False)

    def din(name, shape, dt):
        return nc.dram_tensor(name, shape, dt, kind="ExternalInput").ap()

    d_x = din("x", [T, DM], F32)
    d_wqsel = din("wqsel", [FCN, 128, KC, 128], F32)
    d_bqsel = din("bqsel", [128, FCN], F32)
    d_wo = din("wo_blk", [OCN, 128, KC, 128], F32)
    d_m = din("m_h", [128, HKV * 128], F32)
    d_z = din("z_h", [128, HKV], F32)
    d_gb = din("g_bytes", [CC, 128, KC, 64], U8)
    d_ub = din("u_bytes", [CC, 128, KC, 64], U8)
    d_db = din("d_bytes", [CC, 128, 2048], U8)
    d_gsc = din("g_sc", [32, IPAD], F32)
    d_usc = din("u_sc", [32, IPAD], F32)
    d_dsc = din("d_sc", [CC, DM], F32)
    d_gzb = din("g_zb", [32, IPAD // 2], U8)
    d_uzb = din("u_zb", [32, IPAD // 2], U8)
    d_dzb = din("d_zb", [CC, 2048], U8)
    d_identb = din("ident_bf", [128, 128], BF16)
    d_identf = din("ident_f32", [128, 128], F32)
    d_out = nc.dram_tensor("out", [T, DM], F32, kind="ExternalOutput").ap()

    def dscr(name, shape, dt):
        kind = "ExternalOutput" if dbg else "Internal"
        return nc.dram_tensor(name, shape, dt, kind=kind).ap()

    s_x1 = dscr("x1_scr", [T, DM], F32)
    s_attn = dscr("attn_scr", [OCN, 128, T], F32)
    s_h = dscr("h_scr", [CC, 128, T], BF16)
    s_oacc = dscr("oacc_scr", [OCN, 128, T], F32)
    s_sgb = dscr("sgb_scr", [2, 32, IPAD], BF16)
    s_sgbblk = dscr("sgbblk_scr", [2, CC, 32, 128], BF16)
    s_zs = dscr("zs_scr", [2, 32, IPAD], BF16)
    s_dsb = dscr("dsb_scr", [CC, DM], BF16)
    s_zsd = dscr("zsd_scr", [CC, DM], BF16)

    top = ExitStack()

    def sb(name, shape, dt, stk=None):
        return (stk or top).enter_context(nc.sbuf_tensor(name, shape, dt))

    def psum(name, shape, dt, stk=None):
        return (stk or top).enter_context(nc.psum_tensor(name, shape, dt))

    ls = Sem(nc, top, "ls")
    lg = Sem(nc, top, "lg")
    pe = Sem(nc, top, "pe")
    dv = Sem(nc, top, "dv")
    ac = Sem(nc, top, "ac")
    st = Sem(nc, top, "st")
    lx = Sem(nc, top, "lx")

    ps = [psum(f"ps{i}", [128, 512], F32) for i in range(6)]

    identb = sb("identb", [128, 128], BF16)
    identf = sb("identf", [128, 128], F32)
    bqsel = sb("bqsel_sb", [128, FCN], F32)
    m_sb = sb("m_sb", [128, HKV * 128], BF16)
    z_sb = sb("z_sb", [128, HKV], BF16)
    ones1 = sb("ones1", [1, 128], BF16)
    oneh32 = sb("oneh32", [128, KC * 32], BF16)
    sv = [sb(f"sv{i}", [128, 1], F32) for i in range(6)]
    svp = sb("svp", [128, 1], F32)

    ls.dinc(nc.sync.dma_start(identb[:, :], d_identb[:, :]))
    ls.dinc(nc.sync.dma_start(identf[:, :], d_identf[:, :]))
    ls.dinc(nc.sync.dma_start(bqsel[:, :], d_bqsel[:, :]))
    lg.dinc(nc.gpsimd.dma_start(m_sb[:, :], d_m[:, :]))
    v_const_lg = lg.dinc(nc.gpsimd.dma_start(z_sb[:, :], d_z[:, :]))
    v_const_ls = ls.n

    dv.inc(nc.vector.memset(ones1[:, :], 1.0))
    dv.inc(nc.vector.memset(oneh32[:, :], 0.0))
    for kc in range(KC):
        dv.inc(nc.vector.memset(oneh32[:, kc * 32 + kc:kc * 32 + kc + 1], 1.0))
    for _ in range(16):
        nc.vector.engine_nop()
    dv.inc(nc.vector.engine_nop())

    with ExitStack() as p0a:
        sgbuf = sb("sgbuf", [32, IPAD], BF16, p0a)
        zbuf = sb("zbuf", [32, IPAD // 2], U8, p0a)
        znib = sb("znib", [32, IPAD], U8, p0a)
        zsbuf = sb("zsbuf", [32, IPAD], BF16, p0a)
        for mat, (dsc_, dzb_) in enumerate([(d_gsc, d_gzb), (d_usc, d_uzb)]):
            if mat > 0:
                nc.sync.wait_ge(dv.h, dv.n)
            vl = lg.dinc(nc.gpsimd.dma_start(sgbuf[:, :], dsc_[:, :]))
            vz = ls.dinc(nc.sync.dma_start(zbuf[:, :], dzb_[:, :]))
            nc.vector.wait_ge(lg.h, vl)
            nc.vector.wait_ge(ls.h, vz)
            znv = znib[:, :].rearrange("p (b two) -> p b two", two=2)
            dv.inc(nc.vector.tensor_scalar(znv[:, :, 0], zbuf[:, :], 15, None, ALU.bitwise_and))
            dv.inc(nc.vector.tensor_scalar(znv[:, :, 1], zbuf[:, :], 4, None, ALU.logical_shift_right))
            dv.inc(nc.vector.scalar_tensor_tensor(
                zsbuf[:, :], znib[:, :], QS, sgbuf[:, :], ALU.subtract, ALU.mult))
            vneg = dv.inc(nc.vector.tensor_scalar(zsbuf[:, :], zsbuf[:, :], -1.0, None, ALU.mult))
            nc.gpsimd.wait_ge(dv.h, vneg)
            lg.dinc(nc.gpsimd.dma_start(s_sgb[mat], sgbuf[:, :]))
            lg.dinc(nc.gpsimd.dma_start(
                s_sgbblk[mat].transpose([1, 0, 2]),
                sgbuf[:, :].rearrange("g (c i) -> g c i", c=CC)))
            vwb = lg.dinc(nc.gpsimd.dma_start(s_zs[mat], zsbuf[:, :]))
            nc.gpsimd.wait_ge(lg.h, vwb)
        dsbuf = sb("dsbuf", [CC, DM], BF16, p0a)
        dzbuf = sb("dzbuf", [CC, 2048], U8, p0a)
        dznib = sb("dznib", [CC, DM], U8, p0a)
        dzsb = sb("dzsb", [CC, DM], BF16, p0a)
        vl = lg.dinc(nc.gpsimd.dma_start(dsbuf[:, :], d_dsc[:, :]))
        vz = ls.dinc(nc.sync.dma_start(dzbuf[:, :], d_dzb[:, :]))
        nc.vector.wait_ge(lg.h, vl)
        nc.vector.wait_ge(ls.h, vz)
        dznv = dznib[:, :].rearrange("p (b two) -> p b two", two=2)
        dv.inc(nc.vector.tensor_scalar(dznv[:, :, 0], dzbuf[:, :], 15, None, ALU.bitwise_and))
        dv.inc(nc.vector.tensor_scalar(dznv[:, :, 1], dzbuf[:, :], 4, None, ALU.logical_shift_right))
        dv.inc(nc.vector.scalar_tensor_tensor(
            dzsb[:, :], dznib[:, :], QS, dsbuf[:, :], ALU.subtract, ALU.mult))
        vneg = dv.inc(nc.vector.tensor_scalar(dzsb[:, :], dzsb[:, :], -1.0, None, ALU.mult))
        nc.gpsimd.wait_ge(dv.h, vneg)
        lg.dinc(nc.gpsimd.dma_start(s_dsb[:, :], dsbuf[:, :]))
        v_tables = lg.dinc(nc.gpsimd.dma_start(s_zsd[:, :], dzsb[:, :]))

    def dpad(k=12):
        for _ in range(k):
            nc.vector.engine_nop()

    def apad(k=2):
        for _ in range(k):
            ac.inc(nc.scalar.activation(svp[:, :], svp[:, :], AF.Identity))

    def ln_stats(x_tile, sq_scr):
        nc.vector.wait_ge(ac.h, ac.n)
        dv.inc(nc.vector.reduce_sum(sv[0][:, :], x_tile[:, :], AX.X, negate=True))
        nc.scalar.wait_ge(dv.h, dv.n)
        va = ac.inc(nc.scalar.activation(sq_scr[:, :], x_tile[:, :], AF.Square))
        nc.vector.wait_ge(ac.h, va)
        dv.inc(nc.vector.reduce_sum(sv[1][:, :], sq_scr[:, :], AX.X))
        dv.inc(nc.vector.tensor_scalar(sv[4][:, :], sv[0][:, :], 1.0 / DM, None, ALU.mult))
        dv.inc(nc.vector.scalar_tensor_tensor(
            sv[5][:, :], sv[0][:, :], 1.0 / DM ** 2, sv[0][:, :], ALU.mult, ALU.mult))
        dpad()
        vv = dv.inc(nc.vector.scalar_tensor_tensor(
            sv[1][:, :], sv[1][:, :], 1.0 / DM, sv[5][:, :], ALU.mult, ALU.subtract))
        dpad()
        nc.scalar.wait_ge(dv.h, vv)
        apad()
        ac.inc(nc.scalar.activation(sv[5][:, :], sv[1][:, :], AF.Ln, bias=np.float32(EPS_LN)))
        apad()
        va = ac.inc(nc.scalar.activation(sv[2][:, :], sv[5][:, :], AF.Exp, scale=np.float32(-0.5)))
        nc.vector.wait_ge(ac.h, va)
        dpad()
        dv.inc(nc.vector.tensor_tensor(sv[3][:, :], sv[4][:, :], sv[2][:, :], ALU.mult))
        dpad()
        vb = dv.inc(nc.vector.engine_nop())
        apad()
        return vb

    def transpose_tok_to_feat(tok_bf, destT, tt, psbs, ready_sem, ready_val, state):
        for k4 in range(KC // 4):
            pb = psbs[k4 % 2]
            nc.tensor.wait_ge(ready_sem.h, ready_val)
            if state[k4 % 2]:
                nc.tensor.wait_ge(dv.h, state[k4 % 2])
            for j in range(4):
                kc = k4 * 4 + j
                i = nc.tensor.transpose(
                    pb[:, j * 128:(j + 1) * 128],
                    tok_bf[:, kc * 128:(kc + 1) * 128], identb[:, :])
            vp = pe.inc(i)
            nc.vector.wait_ge(pe.h, vp)
            dview = destT[:, :].rearrange("p (kc t) -> p kc t", t=1024)
            state[k4 % 2] = dv.inc(nc.vector.tensor_copy(
                dview[:, k4 * 4:k4 * 4 + 4, tt * 128:(tt + 1) * 128],
                pb[:, :].rearrange("p (j t) -> p j t", t=128)))

    attn_stack = ExitStack()
    xln1T = sb("xln1T", [128, KC * 1024], BF16, attn_stack)
    nc.tensor.wait_ge(ls.h, v_const_ls)
    with ExitStack() as p0b:
        xtt = [sb(f"xtt{i}", [128, DM], F32, p0b) for i in range(2)]
        sq_scr = sb("sq0", [128, DM], F32, p0b)
        tokbf = sb("tokbf0", [128, DM], BF16, p0b)
        psb = [psum(f"psb0{i}", [128, 512], BF16, p0b) for i in range(2)]
        tstate = [0, 0]
        v_x_ld = [0] * NTT
        v_x_free = [0] * NTT
        for tt in range(NTT):
            if tt >= 2:
                nc.sync.wait_ge(ac.h, v_x_free[tt - 2])
            else:
                nc.sync.wait_ge(lg.h, v_tables)
            v_x_ld[tt] = lx.dinc(nc.sync.dma_start(
                xtt[tt % 2][:, :], d_x[tt * 128:(tt + 1) * 128, :]))
            nc.vector.wait_ge(lx.h, v_x_ld[tt])
            nc.scalar.wait_ge(lx.h, v_x_ld[tt])
            ln_stats(xtt[tt % 2], sq_scr)
            nc.scalar.wait_ge(dv.h, dv.n)
            v_x_free[tt] = ac.inc(nc.scalar.activation(
                tokbf[:, :], xtt[tt % 2][:, :], AF.Identity,
                bias=sv[3][:, :], scale=sv[2][:, :]))
            if dbg and tt == 0:
                dbgt = nc.dram_tensor("dbg_tok0", [128, DM], BF16, kind="ExternalOutput").ap()
                dbgx = nc.dram_tensor("dbg_xtt0", [128, DM], F32, kind="ExternalOutput").ap()
                dbgs = nc.dram_tensor("dbg_sv0", [128, 8], F32, kind="ExternalOutput").ap()
                dbgs2 = nc.dram_tensor("dbg_sv0_late", [128, 8], F32, kind="ExternalOutput").ap()
                nc.gpsimd.wait_ge(ac.h, v_x_free[tt])
                lg.dinc(nc.gpsimd.dma_start(dbgt[:, :], tokbf[:, :]))
                lg.dinc(nc.gpsimd.dma_start(dbgx[:, :], xtt[0][:, :]))
                for i in range(6):
                    lg.dinc(nc.gpsimd.dma_start(dbgs[:, i:i + 1], sv[i][:, :], allow_slow_non_contiguous=True))
                nc.scalar.wait_ge(lg.h, lg.n)
                nc.vector.wait_ge(lg.h, lg.n)
            if dbg and tt == 1:
                nc.gpsimd.wait_ge(ac.h, ac.n)
                for i in range(6):
                    lg.dinc(nc.gpsimd.dma_start(dbgs2[:, i:i + 1], sv[i][:, :], allow_slow_non_contiguous=True))
                nc.vector.wait_ge(lg.h, lg.n)
                nc.scalar.wait_ge(lg.h, lg.n)
            transpose_tok_to_feat(tokbf, xln1T, tt, psb, ac, v_x_free[tt], tstate)
            if dbg and tt == 1:
                dbge = nc.dram_tensor("dbg_xln1T_early", [128, KC * 1024], BF16, kind="ExternalOutput").ap()
                nc.gpsimd.wait_ge(dv.h, dv.n)
                lg.dinc(nc.gpsimd.dma_start(dbge[:, :], xln1T[:, :]))
        v_xln1T = dv.n

    sig_q = sb("sig_q", [128, FCN * 1024], BF16, attn_stack)
    nc.vector.wait_ge(ls.h, v_const_ls)
    with ExitStack() as p1:
        wq_sb = [sb(f"wq{i}", [128, KC * 128], BF16, p1) for i in range(2)]
        scr_r = sb("scr_r", [128, 1024], F32, p1)
        scr_m = sb("scr_m", [128, 1024], F32, p1)
        v_wq_ld = [0] * FCN
        v_q_mm = [0] * FCN
        v_q_evac = [0] * FCN
        for it in range(FCN + 1):
            if it < FCN:
                if it >= 2:
                    nc.gpsimd.wait_ge(pe.h, v_q_mm[it - 2])
                else:
                    nc.gpsimd.wait_ge(dv.h, v_xln1T)
                v_wq_ld[it] = lg.dinc(nc.gpsimd.dma_start(
                    wq_sb[it % 2][:, :], d_wqsel[it].rearrange("p kc f -> p (kc f)")))
            fc = it - 1
            if fc < 0:
                continue
            pa, pb = (ps[0], ps[1]) if fc % 2 == 0 else (ps[2], ps[3])
            nc.tensor.wait_ge(lg.h, v_wq_ld[fc])
            if fc == 0:
                nc.tensor.wait_ge(dv.h, v_xln1T)
            if fc >= 2:
                nc.tensor.wait_ge(dv.h, v_q_evac[fc - 2])
            for half, pp in ((0, pa), (1, pb)):
                for kc in range(KC):
                    i = nc.tensor.matmul(
                        pp[:, :], wq_sb[fc % 2][:, kc * 128:(kc + 1) * 128],
                        xln1T[:, kc * 1024 + half * 512: kc * 1024 + (half + 1) * 512],
                        start=(kc == 0), stop=(kc == KC - 1))
            v_q_mm[fc] = pe.inc(i)
            nc.vector.wait_ge(pe.h, v_q_mm[fc])
            bqc = bqsel[:, fc:fc + 1]
            for half, pp in ((0, pa), (1, pb)):
                slh = slice(half * 512, (half + 1) * 512)
                dv.inc(nc.vector.tensor_scalar(scr_r[:, slh], pp[:, :], bqc, 0.0, ALU.add, ALU.max))
                dv.inc(nc.vector.tensor_scalar(scr_m[:, slh], pp[:, :], bqc, 0.0, ALU.add, ALU.min))
            nc.scalar.wait_ge(dv.h, dv.n)
            va = ac.inc(nc.scalar.activation(scr_m[:, :], scr_m[:, :], AF.Exp))
            nc.vector.wait_ge(ac.h, va)
            v_q_evac[fc] = dv.inc(nc.vector.tensor_tensor(
                sig_q[:, fc * 1024:(fc + 1) * 1024], scr_r[:, :], scr_m[:, :], ALU.add))

    if dbg:
        dbg1 = nc.dram_tensor("dbg_xln1T", [128, KC * 1024], BF16, kind="ExternalOutput").ap()
        dbg2 = nc.dram_tensor("dbg_sigq", [128, FCN * 1024], BF16, kind="ExternalOutput").ap()
        nc.gpsimd.wait_ge(dv.h, dv.n)
        lg.dinc(nc.gpsimd.dma_start(dbg1[:, :], xln1T[:, :]))
        lg.dinc(nc.gpsimd.dma_start(dbg2[:, :], sig_q[:, :]))

    ret = sb("ret", [128, HKV * 1024], BF16, attn_stack)
    with ExitStack() as p2:
        den_sb = sb("den_sb", [1, 1024], F32, p2)
        rden_bf = sb("rden_bf", [1, 1024], BF16, p2)
        rb_sb = sb("rb_sb", [128, 1024], BF16, p2)
        nc.tensor.wait_ge(lg.h, v_const_lg)
        nc.tensor.wait_ge(dv.h, dv.n)
        for h in range(HKV):
            sq = sig_q[:, h * 1024:(h + 1) * 1024]
            for half, pp in ((0, ps[0]), (1, ps[1])):
                i = nc.tensor.matmul(pp[:, :], m_sb[:, h * 128:(h + 1) * 128],
                                     sq[:, half * 512:(half + 1) * 512],
                                     start=True, stop=True)
            for half, pp in ((0, ps[2]), (1, ps[3])):
                i = nc.tensor.matmul(pp[0:1, :], z_sb[:, h:h + 1],
                                     sq[:, half * 512:(half + 1) * 512],
                                     start=True, stop=True)
            v_nd = pe.inc(i)
            nc.vector.wait_ge(pe.h, v_nd)
            dv.inc(nc.vector.tensor_scalar(den_sb[:, 0:512], ps[2][0:1, :], EPS_DEN, None, ALU.add))
            dv.inc(nc.vector.tensor_scalar(den_sb[:, 512:1024], ps[3][0:1, :], EPS_DEN, None, ALU.add))
            dv.inc(nc.vector.reciprocal(den_sb[:, :], den_sb[:, :]))
            v_rd = dv.inc(nc.vector.tensor_copy(rden_bf[:, :], den_sb[:, :]))
            nc.tensor.wait_ge(dv.h, v_rd)
            for half, pp in ((0, ps[4]), (1, ps[5])):
                i = nc.tensor.matmul(pp[:, :], ones1[0:1, :],
                                     rden_bf[0:1, half * 512:(half + 1) * 512],
                                     start=True, stop=True)
            v_bc = pe.inc(i)
            nc.vector.wait_ge(pe.h, v_bc)
            dv.inc(nc.vector.tensor_copy(rb_sb[:, 0:512], ps[4][:, :]))
            dv.inc(nc.vector.tensor_copy(rb_sb[:, 512:1024], ps[5][:, :]))
            dv.inc(nc.vector.tensor_tensor(ret[:, h * 1024:h * 1024 + 512],
                                           ps[0][:, :], rb_sb[:, 0:512], ALU.mult))
            v_ret = dv.inc(nc.vector.tensor_tensor(
                ret[:, h * 1024 + 512:(h + 1) * 1024],
                ps[1][:, :], rb_sb[:, 512:1024], ALU.mult))
            nc.tensor.wait_ge(dv.h, v_ret)
        v_p2_done = dv.n

    if dbg:
        dbg3 = nc.dram_tensor("dbg_ret", [128, HKV * 1024], BF16, kind="ExternalOutput").ap()
        nc.gpsimd.wait_ge(dv.h, dv.n)
        lg.dinc(nc.gpsimd.dma_start(dbg3[:, :], ret[:, :]))

    with ExitStack() as p3:
        wo_sb = [sb(f"wo{i}", [128, KC * 128], BF16, p3) for i in range(3)]
        woeff = [sb(f"woe{i}", [128, HKV * 128], BF16, p3) for i in range(2)]
        ao_sb = [sb(f"ao{i}", [128, 1024], F32, p3) for i in range(2)]
        v_wo_ld = [0] * OCN
        v_wo_add = [0] * OCN
        v_wo_mm = [0] * OCN
        v_wo_evac = [0] * OCN
        v_wo_spill = [0] * OCN
        for it in range(OCN + 2):
            if it < OCN:
                if it >= 3:
                    nc.gpsimd.wait_ge(dv.h, v_wo_add[it - 3])
                elif it == 0:
                    nc.gpsimd.wait_ge(dv.h, v_p2_done)

                v_wo_ld[it] = lg.dinc(nc.gpsimd.dma_start(
                    wo_sb[it % 3][:, :], d_wo[it].rearrange("p fc o -> p (fc o)")))
            oc = it - 2
            if oc < 0:
                continue
            w = wo_sb[oc % 3]
            we = woeff[oc % 2]
            nc.vector.wait_ge(lg.h, v_wo_ld[oc])
            if oc >= 2:
                nc.vector.wait_ge(pe.h, v_wo_mm[oc - 2])
            for hk in range(HKV):
                b = 4 * hk * 128
                dsl = slice(hk * 128, (hk + 1) * 128)
                dv.inc(nc.vector.tensor_tensor(we[:, dsl], w[:, b:b + 128],
                                               w[:, b + 128:b + 256], ALU.add))
                dv.inc(nc.vector.tensor_tensor(we[:, dsl], we[:, dsl],
                                               w[:, b + 256:b + 384], ALU.add))
                dv.inc(nc.vector.tensor_tensor(we[:, dsl], we[:, dsl],
                                               w[:, b + 384:b + 512], ALU.add))
            dpad()
            v_wo_add[oc] = dv.inc(nc.vector.engine_nop())
            pa, pb = (ps[0], ps[1]) if oc % 2 == 0 else (ps[2], ps[3])
            nc.tensor.wait_ge(dv.h, v_wo_add[oc])
            if oc >= 2:
                nc.tensor.wait_ge(dv.h, v_wo_evac[oc - 2])
            for half, pp in ((0, pa), (1, pb)):
                for hk in range(HKV):
                    i = nc.tensor.matmul(
                        pp[:, :], we[:, hk * 128:(hk + 1) * 128],
                        ret[:, hk * 1024 + half * 512: hk * 1024 + (half + 1) * 512],
                        start=(hk == 0), stop=(hk == HKV - 1))
            v_wo_mm[oc] = pe.inc(i)
            nc.vector.wait_ge(pe.h, v_wo_mm[oc])
            if oc >= 2:
                nc.vector.wait_ge(lg.h, v_wo_spill[oc - 2])
            dv.inc(nc.vector.tensor_copy(ao_sb[oc % 2][:, 0:512], pa[:, :]))
            v_wo_evac[oc] = dv.inc(nc.vector.tensor_copy(ao_sb[oc % 2][:, 512:1024], pb[:, :]))
            nc.gpsimd.wait_ge(dv.h, v_wo_evac[oc])
            v_wo_spill[oc] = lg.dinc(nc.gpsimd.dma_start(s_attn[oc], ao_sb[oc % 2][:, :]))
        v_attn_done = v_wo_spill[OCN - 1]

    attn_stack.close()
    ffn_stack = ExitStack()
    xln2T = sb("xln2T", [128, KC * 1024], BF16, ffn_stack)
    with ExitStack() as p4:
        ablk = [sb(f"ablk{i}", [128, DM], F32, p4) for i in range(2)]
        xtt = [sb(f"xtt4{i}", [128, DM], F32, p4) for i in range(2)]
        atok = sb("atok", [128, DM], F32, p4)
        sq_scr = sb("sq4", [128, DM], F32, p4)
        tokbf = sb("tokbf4", [128, DM], BF16, p4)
        x1sb = sb("x1sb", [128, DM], F32, p4)
        psb = [psum(f"psb4{i}", [128, 512], BF16, p4) for i in range(2)]
        tstate = [0, 0]
        pst = [ps[0], ps[1]]
        astate = [0, 0]
        v_a_ld = [0] * NTT
        v_a_free = [0] * NTT
        v_x1_sp = [0] * NTT
        nc.sync.wait_ge(lg.h, v_attn_done)
        for it in range(NTT + 1):
            if it < NTT:
                if it >= 2:
                    nc.sync.wait_ge(dv.h, v_a_free[it - 2])
                ls.dinc(nc.sync.dma_start(
                    ablk[it % 2][:, :].rearrange("p (oc t) -> p oc t", t=128),
                    s_attn[:, :, it * 128:(it + 1) * 128].transpose([1, 0, 2])))
                v_a_ld[it] = ls.dinc(nc.sync.dma_start(
                    xtt[it % 2][:, :], d_x[it * 128:(it + 1) * 128, :]))
            tt = it - 1
            if tt < 0:
                continue
            for k4 in range(KC // 4):
                pb = pst[k4 % 2]
                nc.tensor.wait_ge(ls.h, v_a_ld[tt])
                if astate[k4 % 2]:
                    nc.tensor.wait_ge(dv.h, astate[k4 % 2])
                for j in range(4):
                    oc = k4 * 4 + j
                    i = nc.tensor.transpose(
                        pb[:, j * 128:(j + 1) * 128],
                        ablk[tt % 2][:, oc * 128:(oc + 1) * 128], identf[:, :])
                vp = pe.inc(i)
                nc.vector.wait_ge(pe.h, vp)
                astate[k4 % 2] = dv.inc(nc.vector.tensor_copy(
                    atok[:, :].rearrange("p (k t) -> p k t", t=128)[:, k4 * 4:k4 * 4 + 4, :],
                    pb[:, :].rearrange("p (j t) -> p j t", t=128)))
            ln_stats(atok, sq_scr)
            nc.scalar.wait_ge(dv.h, dv.n)
            va = ac.inc(nc.scalar.activation(sq_scr[:, :], atok[:, :], AF.Identity,
                                             bias=sv[3][:, :], scale=sv[2][:, :]))
            nc.vector.wait_ge(ac.h, va)
            nc.vector.wait_ge(ls.h, v_a_ld[tt])
            vx1 = dv.inc(nc.vector.tensor_tensor(x1sb[:, :], xtt[tt % 2][:, :],
                                                 sq_scr[:, :], ALU.add))
            v_a_free[tt] = vx1
            nc.gpsimd.wait_ge(dv.h, vx1)
            v_x1_sp[tt] = lg.dinc(nc.gpsimd.dma_start(
                s_x1[tt * 128:(tt + 1) * 128, :], x1sb[:, :]))
            ln_stats(x1sb, sq_scr)
            nc.scalar.wait_ge(dv.h, dv.n)
            va = ac.inc(nc.scalar.activation(tokbf[:, :], x1sb[:, :], AF.Identity,
                                             bias=sv[3][:, :], scale=sv[2][:, :]))
            transpose_tok_to_feat(tokbf, xln2T, tt, psb, ac, va, tstate)
        v_x1_done = v_x1_sp[NTT - 1]

    if dbg:
        dbg4 = nc.dram_tensor("dbg_xln2T", [128, KC * 1024], BF16, kind="ExternalOutput").ap()
        nc.gpsimd.wait_ge(dv.h, dv.n)
        lg.dinc(nc.gpsimd.dma_start(dbg4[:, :], xln2T[:, :]))

    xgT = sb("xgT", [32, 1024], BF16, ffn_stack)
    nc.tensor.wait_ge(dv.h, dv.n)
    for half, pp in ((0, ps[0]), (1, ps[1])):
        for kc in range(KC):
            i = nc.tensor.matmul(
                pp[0:32, :], oneh32[:, kc * 32:(kc + 1) * 32],
                xln2T[:, kc * 1024 + half * 512: kc * 1024 + (half + 1) * 512],
                start=(kc == 0), stop=(kc == KC - 1))
    v5 = pe.inc(i)
    nc.vector.wait_ge(pe.h, v5)
    dv.inc(nc.vector.tensor_copy(xgT[:, 0:512], ps[0][0:32, :]))
    v_xg = dv.inc(nc.vector.tensor_copy(xgT[:, 512:1024], ps[1][0:32, :]))

    with ExitStack() as p6:
        ps6 = [psum(f"ps6{i}", [128, 512], F32, p6) for i in range(2)]
        pall = ps + ps6
        NS6 = 3
        gub = [[sb(f"byt{m}{i}", [128, 2048], U8, p6) for i in range(NS6)] for m in range(2)]
        zssl = [[sb(f"zssl{m}{i}", [32, 128], BF16, p6) for i in range(NS6)] for m in range(2)]
        sbc = [[sb(f"sbc{m}{i}", [128, DM], BF16, p6) for i in range(NS6)] for m in range(2)]
        wgu = [[sb(f"wgu{m}{i}", [128, DM], BF16, p6) for i in range(NS6)] for m in range(2)]
        nib = [sb(f"nib{m}", [128, DM], U8, p6) for m in range(2)]
        silu_sb = sb("silu_sb", [128, 1024], BF16, p6)
        h_sb = [sb(f"hsb{i}", [128, 1024], BF16, p6) for i in range(2)]
        nc.sync.wait_ge(lg.h, v_tables)
        v6_ld = [0] * CC
        v6_deq = [0] * CC
        v6_mm = [0] * CC
        v6_h = [0] * CC
        v6_hs = [0] * CC
        PF = 2
        srcs = [(d_gb, s_sgbblk[0], s_zs[0]), (d_ub, s_sgbblk[1], s_zs[1])]
        for it in range(CC + PF):
            c = it
            if c < CC:
                slc = c % NS6
                if c >= NS6:
                    nc.sync.wait_ge(dv.h, v6_deq[c - NS6])
                    nc.sync.wait_ge(pe.h, v6_mm[c - NS6])
                elif c == 0:
                    nc.sync.wait_ge(dv.h, v_xg)
                for m, (dby, dsgb, dzs) in enumerate(srcs):
                    ls.dinc(nc.sync.dma_start(
                        gub[m][slc][:, :], dby[c].rearrange("p kc b -> p (kc b)")))
                    ls.dinc(nc.sync.dma_start(
                        zssl[m][slc][:, :], dzs[:, c * 128:(c + 1) * 128]))
                    ls.dinc(nc.sync.dma_start(
                        sbc[m][slc][:, :],
                        dsgb[c].rearrange("g i -> (g i)").unsqueeze(0)
                        .partition_broadcast(128).squeeze(1)))
                v6_ld[c] = ls.n
            cd = it - 1
            if 0 <= cd < CC:
                slc = cd % NS6
                nc.vector.wait_ge(ls.h, v6_ld[cd])
                if cd >= NS6:
                    nc.vector.wait_ge(pe.h, v6_mm[cd - NS6])
                for m in range(2):
                    bt = gub[m][slc][:, :].rearrange("p (kc b) -> p kc b", kc=KC)
                    nv = nib[m][:, :].rearrange("p (kc b two) -> p kc b two", kc=KC, two=2)
                    dv.inc(nc.vector.tensor_scalar(nv[:, :, :, 0], bt, 15, None, ALU.bitwise_and))
                    dv.inc(nc.vector.tensor_scalar(nv[:, :, :, 1], bt, 4, None, ALU.logical_shift_right))
                    dv.inc(nc.vector.scalar_tensor_tensor(
                        wgu[m][slc][:, :], nib[m][:, :], QS, sbc[m][slc][:, :],
                        ALU.subtract, ALU.mult))
                v6_deq[cd] = dv.n
            cc = it - PF
            if cc < 0:
                continue
            slc = cc % NS6
            pbank = pall[0:4] if cc % 2 == 0 else pall[4:8]
            nc.tensor.wait_ge(dv.h, v6_deq[cc])
            if cc >= 2:
                nc.tensor.wait_ge(dv.h, v6_h[cc - 2])
            for m in range(2):
                for half in range(2):
                    pp = pbank[m * 2 + half]
                    i = nc.tensor.matmul(pp[:, :], zssl[m][slc][:, :],
                                         xgT[:, half * 512:(half + 1) * 512],
                                         start=True, stop=False)
                    for kc in range(KC):
                        i = nc.tensor.matmul(
                            pp[:, :], wgu[m][slc][:, kc * 128:(kc + 1) * 128],
                            xln2T[:, kc * 1024 + half * 512: kc * 1024 + (half + 1) * 512],
                            start=False, stop=(kc == KC - 1))
            v6_mm[cc] = pe.inc(i)
            nc.scalar.wait_ge(pe.h, v6_mm[cc])
            ac.inc(nc.scalar.activation(silu_sb[:, 0:512], pbank[0][:, :], AF.Silu))
            v_silu = ac.inc(nc.scalar.activation(silu_sb[:, 512:1024], pbank[1][:, :], AF.Silu))
            nc.vector.wait_ge(ac.h, v_silu)
            hsl = cc % 2
            if cc >= 2:
                nc.vector.wait_ge(lg.h, v6_hs[cc - 2])
            dv.inc(nc.vector.tensor_tensor(h_sb[hsl][:, 0:512], silu_sb[:, 0:512],
                                           pbank[2][:, :], ALU.mult))
            v6_h[cc] = dv.inc(nc.vector.tensor_tensor(
                h_sb[hsl][:, 512:1024], silu_sb[:, 512:1024], pbank[3][:, :], ALU.mult))
            nc.gpsimd.wait_ge(dv.h, v6_h[cc])
            v6_hs[cc] = lg.dinc(nc.gpsimd.dma_start(s_h[cc], h_sb[hsl][:, :]))
        v_h_done = v6_hs[CC - 1]

    ffn_stack.close()
    hgT_bf = sb("hgT_bf", [CC, 1024], BF16)
    with ExitStack() as p7:
        wd = [[sb(f"wd{b}_{j}", [128, 2048], BF16, p7) for j in range(DB)] for b in range(2)]
        wdby = [sb(f"wdby{j}", [128, 1024], U8, p7) for j in range(DB)]
        h7 = [[sb(f"h7{b}_{j}", [128, 1024], BF16, p7) for j in range(DB)] for b in range(2)]
        sdbc = [sb(f"sdbc{j}", [128, 2048], BF16, p7) for j in range(DB)]
        nibd = sb("nibd", [128, 2048], U8, p7)
        oneh88 = sb("oneh88", [128, DB * CC], BF16, p7)
        zsd_sb = sb("zsd_sb", [CC, 2048], BF16, p7)
        stg = [sb(f"stg{i}", [128, 1024], F32, p7) for i in range(2)]
        nc.sync.wait_ge(lg.h, v_h_done)

        pd = ps[0:4]
        phg = (ps[4][0:CC, :], ps[5][0:CC, :])
        NG = 2 * NR

        v_acc = {}
        v_stg = {}
        v_evac = {}
        last_deq = 0
        v_band_ld = {}
        v_deq = {}
        v_mmband = {}
        v_hg_pe = 0
        v_hgT = 0
        v_oneh = 0
        v_zsd_ld = [0, 0]

        def issue_band_loads(g):
            nonlocal last_deq
            ph, r = g // NR, g % NR
            buf = g % 2
            if last_deq:
                nc.sync.wait_ge(dv.h, last_deq)
            if g >= 2:
                nc.sync.wait_ge(pe.h, v_mmband[g - 2])
            for j in range(DB):
                c = r * DB + j
                ls.dinc(nc.sync.dma_start(
                    wdby[j][:, :], d_db[c][:, ph * 1024:(ph + 1) * 1024]))
                ls.dinc(nc.sync.dma_start(h7[buf][j][:, :], s_h[c]))
                ls.dinc(nc.sync.dma_start(
                    sdbc[j][:, :],
                    s_dsb[c:c + 1, ph * 2048:(ph + 1) * 2048]
                    .partition_broadcast(128).squeeze(1)))
            v_band_ld[g] = ls.n

        def deq_band(g):
            nonlocal last_deq
            buf = g % 2
            nc.vector.wait_ge(ls.h, v_band_ld[g])
            if g >= 2:
                nc.vector.wait_ge(pe.h, v_mmband[g - 2])
            for j in range(DB):
                nv = nibd[:, :].rearrange("p (b two) -> p b two", two=2)
                dv.inc(nc.vector.tensor_scalar(nv[:, :, 0], wdby[j][:, :], 15, None, ALU.bitwise_and))
                dv.inc(nc.vector.tensor_scalar(nv[:, :, 1], wdby[j][:, :], 4, None, ALU.logical_shift_right))
                dv.inc(nc.vector.scalar_tensor_tensor(
                    wd[buf][j][:, :], nibd[:, :], QS, sdbc[j][:, :],
                    ALU.subtract, ALU.mult))
            dpad()
            last_deq = dv.inc(nc.vector.engine_nop())
            v_deq[g] = last_deq

        def oneh_build(g):
            nonlocal v_oneh
            r = g % NR
            nc.vector.wait_ge(pe.h, v_hg_pe)
            dv.inc(nc.vector.memset(oneh88[:, :], 0.0))
            for j in range(DB):
                c = r * DB + j
                dv.inc(nc.vector.memset(oneh88[:, j * CC + c:j * CC + c + 1], 1.0))
            dpad(16)
            v_oneh = dv.inc(nc.vector.engine_nop())

        def mm_round(g):
            nonlocal v_hg_pe, v_hgT
            ph, r = g // NR, g % NR
            buf = g % 2
            if ph == 0:
                nc.tensor.wait_ge(dv.h, v_oneh)
                nc.tensor.wait_ge(ls.h, v_band_ld[g])
                for j in range(DB):
                    for half, pp in ((0, phg[0]), (1, phg[1])):
                        i = nc.tensor.matmul(
                            pp, oneh88[:, j * CC:(j + 1) * CC],
                            h7[buf][j][:, half * 512:(half + 1) * 512],
                            start=(r == 0 and j == 0),
                            stop=(r == NR - 1 and j == DB - 1))
                v_hg_pe = pe.inc(i)
                if r == NR - 1:
                    nc.vector.wait_ge(pe.h, v_hg_pe)
                    dv.inc(nc.vector.tensor_copy(hgT_bf[:, 0:512], phg[0]))
                    dv.inc(nc.vector.tensor_copy(hgT_bf[:, 512:1024], phg[1]))
                    dpad()
                    v_hgT = dv.inc(nc.vector.engine_nop())
            nc.tensor.wait_ge(dv.h, v_deq[g])
            nc.tensor.wait_ge(ls.h, v_band_ld[g])
            if g % NR == NR - 1:
                nc.tensor.wait_ge(dv.h, v_hgT)
                nc.tensor.wait_ge(ls.h, v_zsd_ld[ph])
            for ol in range(16):
                oc = ph * 16 + ol
                pair = ol % 2
                if pair in v_evac:
                    nc.tensor.wait_ge(ac.h, v_evac[pair])
                ppa, ppb = pd[pair * 2], pd[pair * 2 + 1]
                for half, pp in ((0, ppa), (1, ppb)):
                    for j in range(DB):
                        i = nc.tensor.matmul(
                            pp[:, :], wd[buf][j][:, ol * 128:(ol + 1) * 128],
                            h7[buf][j][:, half * 512:(half + 1) * 512],
                            start=(j == 0), stop=(j == DB - 1 and r != NR - 1))
                    if r == NR - 1:
                        i = nc.tensor.matmul(
                            pp[:, :], zsd_sb[:, ol * 128:(ol + 1) * 128],
                            hgT_bf[:, half * 512:(half + 1) * 512],
                            start=False, stop=True)
                v_mmoc = pe.inc(i)
                nc.scalar.wait_ge(pe.h, v_mmoc)
                slot = ol % 2
                if slot in v_stg:
                    nc.scalar.wait_ge(lg.h, v_stg[slot])
                sg = stg[slot]
                ac.inc(nc.scalar.activation(sg[:, 0:512], ppa[:, :], AF.Identity))
                ac.inc(nc.scalar.activation(sg[:, 512:1024], ppb[:, :], AF.Identity))
                apad(2)
                v_ev = ac.inc(nc.scalar.activation(svp[:, :], svp[:, :], AF.Identity))
                v_evac[pair] = v_ev
                nc.gpsimd.wait_ge(ac.h, v_ev)
                if oc in v_acc:
                    nc.gpsimd.wait_ge(lg.h, v_acc[oc])
                op = ALU.bypass if r == 0 else ALU.add
                v_acc[oc] = lg.dinc(nc.gpsimd.dma_start(
                    s_oacc[oc], sg[:, :], accum_op=op))
                v_stg[slot] = v_acc[oc]
            v_mmband[g] = pe.n

        v_zsd_ld[0] = ls.dinc(nc.sync.dma_start(
            zsd_sb[:, :], s_zsd[:, 0:2048]))
        issue_band_loads(0)
        deq_band(0)
        issue_band_loads(1)
        for g in range(NG):
            if g < NR:
                oneh_build(g)
            if g + 1 < NG:
                deq_band(g + 1)
            mm_round(g)
            if g == NR - 1:
                nc.sync.wait_ge(pe.h, pe.n)
                v_zsd_ld[1] = ls.dinc(nc.sync.dma_start(
                    zsd_sb[:, :], s_zsd[:, 2048:4096]))
            if g + 2 < NG:
                issue_band_loads(g + 2)
        v_oacc_done = max(v_acc.values())

    with ExitStack() as p8:
        oblk = [sb(f"oblk{i}", [128, DM], F32, p8) for i in range(2)]
        x1tt = [sb(f"x1tt{i}", [128, DM], F32, p8) for i in range(2)]
        ftok = sb("ftok", [128, DM], F32, p8)
        sq_scr = sb("sq8", [128, DM], F32, p8)
        out_sb = [sb(f"outsb{i}", [128, DM], F32, p8) for i in range(2)]
        pst = [ps[0], ps[1]]
        ostate = [0, 0]
        v_o_ld = [0] * NTT
        v_o_free = [0] * NTT
        v_out_sp = [0] * NTT
        nc.sync.wait_ge(lg.h, v_oacc_done)
        for it in range(NTT + 1):
            if it < NTT:
                if it >= 2:
                    nc.sync.wait_ge(dv.h, v_o_free[it - 2])
                ls.dinc(nc.sync.dma_start(
                    oblk[it % 2][:, :].rearrange("p (oc t) -> p oc t", t=128),
                    s_oacc[:, :, it * 128:(it + 1) * 128].transpose([1, 0, 2])))
                v_o_ld[it] = ls.dinc(nc.sync.dma_start(
                    x1tt[it % 2][:, :], s_x1[it * 128:(it + 1) * 128, :]))
            tt = it - 1
            if tt < 0:
                continue
            for k4 in range(KC // 4):
                pb = pst[k4 % 2]
                nc.tensor.wait_ge(ls.h, v_o_ld[tt])
                if ostate[k4 % 2]:
                    nc.tensor.wait_ge(dv.h, ostate[k4 % 2])
                for j in range(4):
                    oc = k4 * 4 + j
                    i = nc.tensor.transpose(
                        pb[:, j * 128:(j + 1) * 128],
                        oblk[tt % 2][:, oc * 128:(oc + 1) * 128], identf[:, :])
                vp = pe.inc(i)
                nc.vector.wait_ge(pe.h, vp)
                ostate[k4 % 2] = dv.inc(nc.vector.tensor_copy(
                    ftok[:, :].rearrange("p (k t) -> p k t", t=128)[:, k4 * 4:k4 * 4 + 4, :],
                    pb[:, :].rearrange("p (j t) -> p j t", t=128)))
            ln_stats(ftok, sq_scr)
            nc.scalar.wait_ge(dv.h, dv.n)
            va = ac.inc(nc.scalar.activation(sq_scr[:, :], ftok[:, :], AF.Identity,
                                             bias=sv[3][:, :], scale=sv[2][:, :]))
            nc.vector.wait_ge(ac.h, va)
            nc.vector.wait_ge(ls.h, v_o_ld[tt])
            if tt >= 2:
                nc.vector.wait_ge(st.h, v_out_sp[tt - 2])
            v_o_free[tt] = dv.inc(nc.vector.tensor_tensor(
                out_sb[tt % 2][:, :], x1tt[tt % 2][:, :], sq_scr[:, :], ALU.add))
            nc.gpsimd.wait_ge(dv.h, v_o_free[tt])
            v_out_sp[tt] = st.dinc(nc.gpsimd.dma_start(
                d_out[tt * 128:(tt + 1) * 128, :], out_sb[tt % 2][:, :]))
        nc.gpsimd.wait_ge(st.h, v_out_sp[NTT - 1])

    top.close()
    return nc


_NC_CACHE = None


def kernel(**inputs):
    global _NC_CACHE
    x_shards, common = _host_prep(inputs)
    if _NC_CACHE is None:
        _NC_CACHE = build_program()
    nc = _NC_CACHE
    in_maps = [dict(common, x=x_shards[c]) for c in range(NCORE)]
    res = run_bass_kernel_spmd(nc, in_maps, core_ids=list(range(NCORE)))
    out = np.concatenate([np.asarray(r["out"]) for r in res.results], axis=0)
    return out.reshape(4, 2048, DM).astype(np.float32)


# version 14
# speedup vs baseline: 1.1376x; 1.0064x over previous
import os
import sys

if os.path.isdir("/opt/trn_rl_repo") and "/opt/trn_rl_repo" not in sys.path:
    sys.path.insert(0, "/opt/trn_rl_repo")

from contextlib import ExitStack

import numpy as np

import concourse.bass as bass
import concourse.mybir as mybir
from concourse.bass_utils import run_bass_kernel_spmd

F32 = mybir.dt.float32
BF16 = mybir.dt.bfloat16
U8 = mybir.dt.uint8
AF = mybir.ActivationFunctionType
ALU = mybir.AluOpType
AX = mybir.AxisListType

NCORE = 8
T = 1024
NTT = 8
DM = 4096
KC = 32
FCN = 8
OCN = 32
CC = 88
IPAD = CC * 128
HKV = 8
DB = 8
NR = CC // DB
EPS_LN = 1e-5
EPS_DEN = 1e-6
QS = 7.5


def _host_prep(inputs):
    f32 = np.float32
    X = np.asarray(inputs["L0_attn_hidden_states"], dtype=f32).reshape(NCORE * T, DM)
    x_shards = [np.ascontiguousarray(X[c * T:(c + 1) * T]) for c in range(NCORE)]

    Wq = np.asarray(inputs["L0_attn_Wq"], dtype=f32)
    wq3 = Wq.reshape(32, 128, DM)[0::4]
    wqsel = np.ascontiguousarray(
        wq3.transpose(0, 2, 1).reshape(FCN, KC, 128, 128).transpose(0, 2, 1, 3))
    bq = np.asarray(inputs["L0_attn_bq"], dtype=f32)
    bqsel = np.ascontiguousarray(bq.reshape(32, 128)[0::4].T)

    Wo = np.asarray(inputs["L0_attn_Wo"], dtype=f32)
    wo_blk = np.ascontiguousarray(Wo.reshape(OCN, 128, KC, 128).transpose(0, 3, 2, 1))

    M = np.asarray(inputs["L0_attn_M"], dtype=f32)
    m_host = np.ascontiguousarray(M.transpose(1, 0, 2).reshape(128, HKV * 128))
    z_host = np.ascontiguousarray(np.asarray(inputs["L0_attn_z"], dtype=f32).T)

    def _gu_bytes(qw):
        qv = np.ascontiguousarray(np.asarray(qw)).view(np.uint8)
        qv = np.concatenate(
            [qv, np.zeros((DM, IPAD // 2 - qv.shape[1]), np.uint8)], axis=1)
        return np.ascontiguousarray(
            qv.reshape(KC, 128, CC, 64).transpose(2, 1, 0, 3))

    g_bytes = _gu_bytes(inputs["L0_ffn_gate_qweight"])
    u_bytes = _gu_bytes(inputs["L0_ffn_up_qweight"])

    dqv = np.ascontiguousarray(np.asarray(inputs["L0_ffn_down_qweight"])).view(np.uint8)
    dqv = np.concatenate([dqv, np.zeros((IPAD - dqv.shape[0], 2048), np.uint8)], axis=0)
    d_bytes = np.ascontiguousarray(dqv.reshape(CC, 128, 2048))

    g_sc = np.zeros((32, IPAD), f32)
    g_sc[:, :11008] = np.asarray(inputs["L0_ffn_gate_scales"], dtype=f32)
    u_sc = np.zeros((32, IPAD), f32)
    u_sc[:, :11008] = np.asarray(inputs["L0_ffn_up_scales"], dtype=f32)
    d_sc = np.zeros((CC, DM), f32)
    d_sc[:86] = np.asarray(inputs["L0_ffn_down_scales"], dtype=f32)

    def _zb(qz, rows, cols):
        v = np.ascontiguousarray(np.asarray(qz)).view(np.uint8)
        out = np.zeros((rows, cols), np.uint8)
        out[:v.shape[0], :v.shape[1]] = v
        return out

    g_zb = _zb(inputs["L0_ffn_gate_qzeros"], 32, IPAD // 2)
    u_zb = _zb(inputs["L0_ffn_up_qzeros"], 32, IPAD // 2)
    d_zb = _zb(inputs["L0_ffn_down_qzeros"], CC, 2048)

    import ml_dtypes
    ident_bf = np.eye(128, dtype=ml_dtypes.bfloat16)
    ident_f32 = np.eye(128, dtype=np.float32)

    common = dict(
        wqsel=wqsel, bqsel=bqsel, wo_blk=wo_blk, m_h=m_host, z_h=z_host,
        g_bytes=g_bytes, u_bytes=u_bytes, d_bytes=d_bytes,
        g_sc=g_sc, u_sc=u_sc, d_sc=d_sc, g_zb=g_zb, u_zb=u_zb, d_zb=d_zb,
        ident_bf=ident_bf, ident_f32=ident_f32)
    return x_shards, common


class Sem:
    def __init__(self, nc, stack, name):
        self.h = stack.enter_context(nc.semaphore(name))
        self.n = 0

    def inc(self, instr, k=1):
        self.n += k
        instr.then_inc(self.h, k)
        return self.n

    def dinc(self, instr):
        return self.inc(instr, 16)


def build_program(dbg=False):
    nc = bass.Bass("TRN2", target_bir_lowering=False)

    def din(name, shape, dt):
        return nc.dram_tensor(name, shape, dt, kind="ExternalInput").ap()

    d_x = din("x", [T, DM], F32)
    d_wqsel = din("wqsel", [FCN, 128, KC, 128], F32)
    d_bqsel = din("bqsel", [128, FCN], F32)
    d_wo = din("wo_blk", [OCN, 128, KC, 128], F32)
    d_m = din("m_h", [128, HKV * 128], F32)
    d_z = din("z_h", [128, HKV], F32)
    d_gb = din("g_bytes", [CC, 128, KC, 64], U8)
    d_ub = din("u_bytes", [CC, 128, KC, 64], U8)
    d_db = din("d_bytes", [CC, 128, 2048], U8)
    d_gsc = din("g_sc", [32, IPAD], F32)
    d_usc = din("u_sc", [32, IPAD], F32)
    d_dsc = din("d_sc", [CC, DM], F32)
    d_gzb = din("g_zb", [32, IPAD // 2], U8)
    d_uzb = din("u_zb", [32, IPAD // 2], U8)
    d_dzb = din("d_zb", [CC, 2048], U8)
    d_identb = din("ident_bf", [128, 128], BF16)
    d_identf = din("ident_f32", [128, 128], F32)
    d_out = nc.dram_tensor("out", [T, DM], F32, kind="ExternalOutput").ap()

    def dscr(name, shape, dt):
        kind = "ExternalOutput" if dbg else "Internal"
        return nc.dram_tensor(name, shape, dt, kind=kind).ap()

    s_x1 = dscr("x1_scr", [T, DM], F32)
    s_atok = dscr("atok_scr", [NTT, 128, DM], BF16)
    s_h = dscr("h_scr", [CC, 128, T], BF16)
    s_oacc = dscr("oacc_scr", [OCN, 128, T], F32)
    s_sgb = dscr("sgb_scr", [2, 32, IPAD], BF16)
    s_sgbblk = dscr("sgbblk_scr", [2, CC, 32, 128], BF16)
    s_zs = dscr("zs_scr", [2, 32, IPAD], BF16)
    s_dsb = dscr("dsb_scr", [CC, DM], BF16)
    s_zsd = dscr("zsd_scr", [CC, DM], BF16)

    top = ExitStack()

    def sb(name, shape, dt, stk=None):
        return (stk or top).enter_context(nc.sbuf_tensor(name, shape, dt))

    def psum(name, shape, dt, stk=None):
        return (stk or top).enter_context(nc.psum_tensor(name, shape, dt))

    ls = Sem(nc, top, "ls")
    lg = Sem(nc, top, "lg")
    pe = Sem(nc, top, "pe")
    dv = Sem(nc, top, "dv")
    ac = Sem(nc, top, "ac")
    st = Sem(nc, top, "st")
    lx = Sem(nc, top, "lx")

    ps = [psum(f"ps{i}", [128, 512], F32) for i in range(6)]

    identb = sb("identb", [128, 128], BF16)
    identf = sb("identf", [128, 128], F32)
    bqsel = sb("bqsel_sb", [128, FCN], F32)
    m_sb = sb("m_sb", [128, HKV * 128], BF16)
    z_sb = sb("z_sb", [128, HKV], BF16)
    ones1 = sb("ones1", [1, 128], BF16)
    oneh32 = sb("oneh32", [128, KC * 32], BF16)
    sv = [sb(f"sv{i}", [128, 1], F32) for i in range(6)]
    svp = sb("svp", [128, 1], F32)

    ls.dinc(nc.sync.dma_start(identb[:, :], d_identb[:, :]))
    ls.dinc(nc.sync.dma_start(identf[:, :], d_identf[:, :]))
    ls.dinc(nc.sync.dma_start(bqsel[:, :], d_bqsel[:, :]))
    lg.dinc(nc.gpsimd.dma_start(m_sb[:, :], d_m[:, :]))
    v_const_lg = lg.dinc(nc.gpsimd.dma_start(z_sb[:, :], d_z[:, :]))
    v_const_ls = ls.n

    dv.inc(nc.vector.memset(ones1[:, :], 1.0))
    dv.inc(nc.vector.memset(oneh32[:, :], 0.0))
    for kc in range(KC):
        dv.inc(nc.vector.memset(oneh32[:, kc * 32 + kc:kc * 32 + kc + 1], 1.0))
    for _ in range(16):
        nc.vector.engine_nop()
    dv.inc(nc.vector.engine_nop())

    with ExitStack() as p0a:
        sgbuf = sb("sgbuf", [32, IPAD], BF16, p0a)
        zbuf = sb("zbuf", [32, IPAD // 2], U8, p0a)
        znib = sb("znib", [32, IPAD], U8, p0a)
        zsbuf = sb("zsbuf", [32, IPAD], BF16, p0a)
        for mat, (dsc_, dzb_) in enumerate([(d_gsc, d_gzb), (d_usc, d_uzb)]):
            if mat > 0:
                nc.sync.wait_ge(dv.h, dv.n)
            vl = lg.dinc(nc.gpsimd.dma_start(sgbuf[:, :], dsc_[:, :]))
            vz = ls.dinc(nc.sync.dma_start(zbuf[:, :], dzb_[:, :]))
            nc.vector.wait_ge(lg.h, vl)
            nc.vector.wait_ge(ls.h, vz)
            znv = znib[:, :].rearrange("p (b two) -> p b two", two=2)
            dv.inc(nc.vector.tensor_scalar(znv[:, :, 0], zbuf[:, :], 15, None, ALU.bitwise_and))
            dv.inc(nc.vector.tensor_scalar(znv[:, :, 1], zbuf[:, :], 4, None, ALU.logical_shift_right))
            dv.inc(nc.vector.scalar_tensor_tensor(
                zsbuf[:, :], znib[:, :], QS, sgbuf[:, :], ALU.subtract, ALU.mult))
            vneg = dv.inc(nc.vector.tensor_scalar(zsbuf[:, :], zsbuf[:, :], -1.0, None, ALU.mult))
            nc.gpsimd.wait_ge(dv.h, vneg)
            lg.dinc(nc.gpsimd.dma_start(s_sgb[mat], sgbuf[:, :]))
            lg.dinc(nc.gpsimd.dma_start(
                s_sgbblk[mat].transpose([1, 0, 2]),
                sgbuf[:, :].rearrange("g (c i) -> g c i", c=CC)))
            vwb = lg.dinc(nc.gpsimd.dma_start(s_zs[mat], zsbuf[:, :]))
            nc.gpsimd.wait_ge(lg.h, vwb)
        dsbuf = sb("dsbuf", [CC, DM], BF16, p0a)
        dzbuf = sb("dzbuf", [CC, 2048], U8, p0a)
        dznib = sb("dznib", [CC, DM], U8, p0a)
        dzsb = sb("dzsb", [CC, DM], BF16, p0a)
        vl = lg.dinc(nc.gpsimd.dma_start(dsbuf[:, :], d_dsc[:, :]))
        vz = ls.dinc(nc.sync.dma_start(dzbuf[:, :], d_dzb[:, :]))
        nc.vector.wait_ge(lg.h, vl)
        nc.vector.wait_ge(ls.h, vz)
        dznv = dznib[:, :].rearrange("p (b two) -> p b two", two=2)
        dv.inc(nc.vector.tensor_scalar(dznv[:, :, 0], dzbuf[:, :], 15, None, ALU.bitwise_and))
        dv.inc(nc.vector.tensor_scalar(dznv[:, :, 1], dzbuf[:, :], 4, None, ALU.logical_shift_right))
        dv.inc(nc.vector.scalar_tensor_tensor(
            dzsb[:, :], dznib[:, :], QS, dsbuf[:, :], ALU.subtract, ALU.mult))
        vneg = dv.inc(nc.vector.tensor_scalar(dzsb[:, :], dzsb[:, :], -1.0, None, ALU.mult))
        nc.gpsimd.wait_ge(dv.h, vneg)
        lg.dinc(nc.gpsimd.dma_start(s_dsb[:, :], dsbuf[:, :]))
        v_tables = lg.dinc(nc.gpsimd.dma_start(s_zsd[:, :], dzsb[:, :]))

    def dpad(k=12):
        for _ in range(k):
            nc.vector.engine_nop()

    def apad(k=2):
        for _ in range(k):
            ac.inc(nc.scalar.activation(svp[:, :], svp[:, :], AF.Identity))

    def ln_stats(x_tile, sq_scr):
        nc.vector.wait_ge(ac.h, ac.n)
        dv.inc(nc.vector.reduce_sum(sv[0][:, :], x_tile[:, :], AX.X, negate=True))
        nc.scalar.wait_ge(dv.h, dv.n)
        va = ac.inc(nc.scalar.activation(sq_scr[:, :], x_tile[:, :], AF.Square))
        nc.vector.wait_ge(ac.h, va)
        dv.inc(nc.vector.reduce_sum(sv[1][:, :], sq_scr[:, :], AX.X))
        dv.inc(nc.vector.tensor_scalar(sv[4][:, :], sv[0][:, :], 1.0 / DM, None, ALU.mult))
        dv.inc(nc.vector.scalar_tensor_tensor(
            sv[5][:, :], sv[0][:, :], 1.0 / DM ** 2, sv[0][:, :], ALU.mult, ALU.mult))
        dpad()
        vv = dv.inc(nc.vector.scalar_tensor_tensor(
            sv[1][:, :], sv[1][:, :], 1.0 / DM, sv[5][:, :], ALU.mult, ALU.subtract))
        dpad()
        nc.scalar.wait_ge(dv.h, vv)
        apad()
        ac.inc(nc.scalar.activation(sv[5][:, :], sv[1][:, :], AF.Ln, bias=np.float32(EPS_LN)))
        apad()
        va = ac.inc(nc.scalar.activation(sv[2][:, :], sv[5][:, :], AF.Exp, scale=np.float32(-0.5)))
        nc.vector.wait_ge(ac.h, va)
        dpad()
        dv.inc(nc.vector.tensor_tensor(sv[3][:, :], sv[4][:, :], sv[2][:, :], ALU.mult))
        dpad()
        vb = dv.inc(nc.vector.engine_nop())
        apad()
        return vb

    def transpose_tok_to_feat(tok_bf, destT, tt, psbs, ready_sem, ready_val, state):
        for k4 in range(KC // 4):
            pb = psbs[k4 % 2]
            nc.tensor.wait_ge(ready_sem.h, ready_val)
            if state[k4 % 2]:
                nc.tensor.wait_ge(dv.h, state[k4 % 2])
            for j in range(4):
                kc = k4 * 4 + j
                i = nc.tensor.transpose(
                    pb[:, j * 128:(j + 1) * 128],
                    tok_bf[:, kc * 128:(kc + 1) * 128], identb[:, :])
            vp = pe.inc(i)
            nc.vector.wait_ge(pe.h, vp)
            dview = destT[:, :].rearrange("p (kc t) -> p kc t", t=1024)
            state[k4 % 2] = dv.inc(nc.vector.tensor_copy(
                dview[:, k4 * 4:k4 * 4 + 4, tt * 128:(tt + 1) * 128],
                pb[:, :].rearrange("p (j t) -> p j t", t=128)))

    attn_stack = ExitStack()
    ret = sb("ret", [128, HKV * 1024], BF16, attn_stack)
    sigq_stack = ExitStack()
    sig_q = sb("sig_q", [128, FCN * 1024], BF16, sigq_stack)
    xln1_stack = ExitStack()
    xln1T = sb("xln1T", [128, KC * 1024], BF16, xln1_stack)
    nc.tensor.wait_ge(ls.h, v_const_ls)
    with ExitStack() as p0b:
        xtt = [sb(f"xtt{i}", [128, DM], F32, p0b) for i in range(2)]
        sq_scr = sb("sq0", [128, DM], BF16, p0b)
        tokbf = sq_scr
        psb = [psum(f"psb0{i}", [128, 512], BF16, p0b) for i in range(2)]
        tstate = [0, 0]
        v_x_ld = [0] * NTT
        v_x_free = [0] * NTT
        for tt in range(NTT):
            if tt >= 2:
                nc.sync.wait_ge(ac.h, v_x_free[tt - 2])
            else:
                nc.sync.wait_ge(lg.h, v_tables)
            v_x_ld[tt] = lx.dinc(nc.sync.dma_start(
                xtt[tt % 2][:, :], d_x[tt * 128:(tt + 1) * 128, :]))
            nc.vector.wait_ge(lx.h, v_x_ld[tt])
            nc.scalar.wait_ge(lx.h, v_x_ld[tt])
            ln_stats(xtt[tt % 2], sq_scr)
            nc.scalar.wait_ge(dv.h, dv.n)
            v_x_free[tt] = ac.inc(nc.scalar.activation(
                tokbf[:, :], xtt[tt % 2][:, :], AF.Identity,
                bias=sv[3][:, :], scale=sv[2][:, :]))
            if dbg and tt == 0:
                dbgt = nc.dram_tensor("dbg_tok0", [128, DM], BF16, kind="ExternalOutput").ap()
                dbgx = nc.dram_tensor("dbg_xtt0", [128, DM], F32, kind="ExternalOutput").ap()
                dbgs = nc.dram_tensor("dbg_sv0", [128, 8], F32, kind="ExternalOutput").ap()
                dbgs2 = nc.dram_tensor("dbg_sv0_late", [128, 8], F32, kind="ExternalOutput").ap()
                nc.gpsimd.wait_ge(ac.h, v_x_free[tt])
                lg.dinc(nc.gpsimd.dma_start(dbgt[:, :], tokbf[:, :]))
                lg.dinc(nc.gpsimd.dma_start(dbgx[:, :], xtt[0][:, :]))
                for i in range(6):
                    lg.dinc(nc.gpsimd.dma_start(dbgs[:, i:i + 1], sv[i][:, :], allow_slow_non_contiguous=True))
                nc.scalar.wait_ge(lg.h, lg.n)
                nc.vector.wait_ge(lg.h, lg.n)
            if dbg and tt == 1:
                nc.gpsimd.wait_ge(ac.h, ac.n)
                for i in range(6):
                    lg.dinc(nc.gpsimd.dma_start(dbgs2[:, i:i + 1], sv[i][:, :], allow_slow_non_contiguous=True))
                nc.vector.wait_ge(lg.h, lg.n)
                nc.scalar.wait_ge(lg.h, lg.n)
            transpose_tok_to_feat(tokbf, xln1T, tt, psb, ac, v_x_free[tt], tstate)
            if dbg and tt == 1:
                dbge = nc.dram_tensor("dbg_xln1T_early", [128, KC * 1024], BF16, kind="ExternalOutput").ap()
                nc.gpsimd.wait_ge(dv.h, dv.n)
                lg.dinc(nc.gpsimd.dma_start(dbge[:, :], xln1T[:, :]))
        v_xln1T = dv.n

    nc.vector.wait_ge(ls.h, v_const_ls)
    with ExitStack() as p1:
        wq_sb = [sb(f"wq{i}", [128, KC * 128], BF16, p1) for i in range(2)]
        scr_r = sb("scr_r", [128, 1024], F32, p1)
        scr_m = sb("scr_m", [128, 1024], F32, p1)
        v_wq_ld = [0] * FCN
        v_q_mm = [0] * FCN
        v_q_evac = [0] * FCN
        for it in range(FCN + 1):
            if it < FCN:
                if it >= 2:
                    nc.gpsimd.wait_ge(pe.h, v_q_mm[it - 2])
                else:
                    nc.gpsimd.wait_ge(dv.h, v_xln1T)
                v_wq_ld[it] = lg.dinc(nc.gpsimd.dma_start(
                    wq_sb[it % 2][:, :], d_wqsel[it].rearrange("p kc f -> p (kc f)")))
            fc = it - 1
            if fc < 0:
                continue
            pa, pb = (ps[0], ps[1]) if fc % 2 == 0 else (ps[2], ps[3])
            nc.tensor.wait_ge(lg.h, v_wq_ld[fc])
            if fc == 0:
                nc.tensor.wait_ge(dv.h, v_xln1T)
            if fc >= 2:
                nc.tensor.wait_ge(dv.h, v_q_evac[fc - 2])
            for half, pp in ((0, pa), (1, pb)):
                for kc in range(KC):
                    i = nc.tensor.matmul(
                        pp[:, :], wq_sb[fc % 2][:, kc * 128:(kc + 1) * 128],
                        xln1T[:, kc * 1024 + half * 512: kc * 1024 + (half + 1) * 512],
                        start=(kc == 0), stop=(kc == KC - 1))
            v_q_mm[fc] = pe.inc(i)
            nc.vector.wait_ge(pe.h, v_q_mm[fc])
            bqc = bqsel[:, fc:fc + 1]
            for half, pp in ((0, pa), (1, pb)):
                slh = slice(half * 512, (half + 1) * 512)
                dv.inc(nc.vector.tensor_scalar(scr_r[:, slh], pp[:, :], bqc, 0.0, ALU.add, ALU.max))
                dv.inc(nc.vector.tensor_scalar(scr_m[:, slh], pp[:, :], bqc, 0.0, ALU.add, ALU.min))
            nc.scalar.wait_ge(dv.h, dv.n)
            va = ac.inc(nc.scalar.activation(scr_m[:, :], scr_m[:, :], AF.Exp))
            nc.vector.wait_ge(ac.h, va)
            v_q_evac[fc] = dv.inc(nc.vector.tensor_tensor(
                sig_q[:, fc * 1024:(fc + 1) * 1024], scr_r[:, :], scr_m[:, :], ALU.add))

    if dbg:
        dbg1 = nc.dram_tensor("dbg_xln1T", [128, KC * 1024], BF16, kind="ExternalOutput").ap()
        dbg2 = nc.dram_tensor("dbg_sigq", [128, FCN * 1024], BF16, kind="ExternalOutput").ap()
        nc.gpsimd.wait_ge(dv.h, dv.n)
        lg.dinc(nc.gpsimd.dma_start(dbg1[:, :], xln1T[:, :]))
        lg.dinc(nc.gpsimd.dma_start(dbg2[:, :], sig_q[:, :]))

    xln1_stack.close()
    with ExitStack() as p2:
        den_sb = sb("den_sb", [1, 1024], F32, p2)
        rden_bf = sb("rden_bf", [1, 1024], BF16, p2)
        rb_sb = sb("rb_sb", [128, 1024], BF16, p2)
        nc.tensor.wait_ge(lg.h, v_const_lg)
        nc.tensor.wait_ge(dv.h, dv.n)
        for h in range(HKV):
            sq = sig_q[:, h * 1024:(h + 1) * 1024]
            for half, pp in ((0, ps[0]), (1, ps[1])):
                i = nc.tensor.matmul(pp[:, :], m_sb[:, h * 128:(h + 1) * 128],
                                     sq[:, half * 512:(half + 1) * 512],
                                     start=True, stop=True)
            for half, pp in ((0, ps[2]), (1, ps[3])):
                i = nc.tensor.matmul(pp[0:1, :], z_sb[:, h:h + 1],
                                     sq[:, half * 512:(half + 1) * 512],
                                     start=True, stop=True)
            v_nd = pe.inc(i)
            nc.vector.wait_ge(pe.h, v_nd)
            dv.inc(nc.vector.tensor_scalar(den_sb[:, 0:512], ps[2][0:1, :], EPS_DEN, None, ALU.add))
            dv.inc(nc.vector.tensor_scalar(den_sb[:, 512:1024], ps[3][0:1, :], EPS_DEN, None, ALU.add))
            dv.inc(nc.vector.reciprocal(den_sb[:, :], den_sb[:, :]))
            v_rd = dv.inc(nc.vector.tensor_copy(rden_bf[:, :], den_sb[:, :]))
            nc.tensor.wait_ge(dv.h, v_rd)
            for half, pp in ((0, ps[4]), (1, ps[5])):
                i = nc.tensor.matmul(pp[:, :], ones1[0:1, :],
                                     rden_bf[0:1, half * 512:(half + 1) * 512],
                                     start=True, stop=True)
            v_bc = pe.inc(i)
            nc.vector.wait_ge(pe.h, v_bc)
            dv.inc(nc.vector.tensor_copy(rb_sb[:, 0:512], ps[4][:, :]))
            dv.inc(nc.vector.tensor_copy(rb_sb[:, 512:1024], ps[5][:, :]))
            dv.inc(nc.vector.tensor_tensor(ret[:, h * 1024:h * 1024 + 512],
                                           ps[0][:, :], rb_sb[:, 0:512], ALU.mult))
            v_ret = dv.inc(nc.vector.tensor_tensor(
                ret[:, h * 1024 + 512:(h + 1) * 1024],
                ps[1][:, :], rb_sb[:, 512:1024], ALU.mult))
            nc.tensor.wait_ge(dv.h, v_ret)
        v_p2_done = dv.n

    sigq_stack.close()
    if dbg:
        dbg3 = nc.dram_tensor("dbg_ret", [128, HKV * 1024], BF16, kind="ExternalOutput").ap()
        nc.gpsimd.wait_ge(dv.h, dv.n)
        lg.dinc(nc.gpsimd.dma_start(dbg3[:, :], ret[:, :]))

    with ExitStack() as p3:
        wo_sb = [sb(f"wo{i}", [128, KC * 128], BF16, p3) for i in range(4)]
        weq = [sb(f"weq{i}", [128, HKV * 512], BF16, p3) for i in range(2)]
        ao_tok = [sb(f"aotok{i}", [128, 512], BF16, p3) for i in range(2)]
        v_ao = {}
        v_wo_ld = [0] * OCN
        NQ = OCN // 4
        v_weq = [0] * NQ
        v_mmq = [0] * NQ
        ev_bank = {}
        for q in range(NQ + 1):
            if q < NQ:
                if q == 0:
                    nc.gpsimd.wait_ge(dv.h, v_p2_done)
                else:
                    nc.gpsimd.wait_ge(dv.h, v_weq[q - 1])
                for k in range(4):
                    oc = q * 4 + k
                    v_wo_ld[oc] = lg.dinc(nc.gpsimd.dma_start(
                        wo_sb[k][:, :], d_wo[oc].rearrange("p fc o -> p (fc o)")))
                wq = weq[q % 2]
                if q > 0:
                    nc.vector.wait_ge(pe.h, v_mmq[q - 1])
                for k in range(4):
                    oc = q * 4 + k
                    nc.vector.wait_ge(lg.h, v_wo_ld[oc])
                    w = wo_sb[k]
                    for hk in range(HKV):
                        b = 4 * hk * 128
                        dsl = slice(hk * 512 + k * 128, hk * 512 + (k + 1) * 128)
                        dv.inc(nc.vector.tensor_tensor(wq[:, dsl], w[:, b:b + 128],
                                                       w[:, b + 128:b + 256], ALU.add))
                        dv.inc(nc.vector.tensor_tensor(wq[:, dsl], wq[:, dsl],
                                                       w[:, b + 256:b + 384], ALU.add))
                        dv.inc(nc.vector.tensor_tensor(wq[:, dsl], wq[:, dsl],
                                                       w[:, b + 384:b + 512], ALU.add))
                dpad()
                v_weq[q] = dv.inc(nc.vector.engine_nop())
                if dbg and q == 0:
                    dbgwq = nc.dram_tensor("dbg_weq0", [128, HKV * 512], BF16, kind="ExternalOutput").ap()
                    nc.gpsimd.wait_ge(dv.h, v_weq[q])
                    lg.dinc(nc.gpsimd.dma_start(dbgwq[:, :], wq[:, :]))
                    nc.tensor.wait_ge(lg.h, lg.n)
            qq = q if q < NQ else q - 1
            if q < NQ:
                qq = q
            else:
                break
            wq = weq[qq % 2]
            nc.tensor.wait_ge(dv.h, v_weq[qq])
            for tt in range(NTT):
                bank = tt % 2
                pp = ps[bank]
                if bank in ev_bank:
                    nc.tensor.wait_ge(ac.h, ev_bank[bank])
                for hk in range(HKV):
                    i = nc.tensor.matmul(
                        pp[:, :], ret[:, hk * 1024 + tt * 128: hk * 1024 + (tt + 1) * 128],
                        wq[:, hk * 512:(hk + 1) * 512],
                        start=(hk == 0), stop=(hk == HKV - 1))
                vm = pe.inc(i)
                nc.scalar.wait_ge(pe.h, vm)
                aslot = tt % 2
                if aslot in v_ao:
                    nc.scalar.wait_ge(lg.h, v_ao[aslot])
                ev_bank[bank] = ac.inc(nc.scalar.activation(
                    ao_tok[aslot][:, :], pp[:, :], AF.Identity))
                nc.gpsimd.wait_ge(ac.h, ev_bank[bank])
                v_ao[aslot] = lg.dinc(nc.gpsimd.dma_start(
                    s_atok[tt, :, qq * 512:(qq + 1) * 512], ao_tok[aslot][:, :]))
            v_mmq[qq] = pe.n
        nc.sync.wait_ge(lg.h, lg.n)
        nc.sync.wait_ge(dv.h, dv.n)
        v_attn_done = dv.n


    attn_stack.close()
    ffn_stack = ExitStack()
    xln2T = sb("xln2T", [128, KC * 1024], BF16, ffn_stack)
    with ExitStack() as p4:
        xtt = [sb(f"xtt4{i}", [128, DM], F32, p4) for i in range(2)]
        atk = [sb(f"atk{i}", [128, DM], BF16, p4) for i in range(2)]
        sq_scr = sb("sq4b", [128, DM], BF16, p4)
        tmpbf = sq_scr
        x1sb = sb("x1sb", [128, DM], F32, p4)
        psb = [psum(f"psb4{i}", [128, 512], BF16, p4) for i in range(2)]
        tstate = [0, 0]
        v_a_ld = [0] * NTT
        v_a_free = [0] * NTT
        v_at_used = [0] * NTT
        v_x1_sp = [0] * NTT
        for it in range(NTT + 1):
            if it < NTT:
                if it >= 2:
                    nc.sync.wait_ge(dv.h, v_a_free[it - 2])
                    nc.sync.wait_ge(ac.h, v_at_used[it - 2])
                else:
                    nc.sync.wait_ge(dv.h, v_attn_done)
                ls.dinc(nc.sync.dma_start(
                    atk[it % 2][:, :], s_atok[it].rearrange("p f -> p f")))
                v_a_ld[it] = ls.dinc(nc.sync.dma_start(
                    xtt[it % 2][:, :], d_x[it * 128:(it + 1) * 128, :]))
            tt = it - 1
            if tt < 0:
                continue
            at = atk[tt % 2][:, :]
            nc.vector.wait_ge(ls.h, v_a_ld[tt])
            nc.scalar.wait_ge(ls.h, v_a_ld[tt])
            ln_stats(at, sq_scr)
            nc.scalar.wait_ge(dv.h, dv.n)
            va = ac.inc(nc.scalar.activation(tmpbf[:, :], at, AF.Identity,
                                             bias=sv[3][:, :], scale=sv[2][:, :]))
            v_at_used[tt] = va
            nc.vector.wait_ge(ac.h, va)
            nc.vector.wait_ge(ls.h, v_a_ld[tt])
            vx1 = dv.inc(nc.vector.tensor_tensor(x1sb[:, :], xtt[tt % 2][:, :],
                                                 tmpbf[:, :], ALU.add))
            v_a_free[tt] = vx1
            nc.gpsimd.wait_ge(dv.h, vx1)
            v_x1_sp[tt] = lg.dinc(nc.gpsimd.dma_start(
                s_x1[tt * 128:(tt + 1) * 128, :], x1sb[:, :]))
            ln_stats(x1sb, sq_scr)
            nc.scalar.wait_ge(dv.h, dv.n)
            va = ac.inc(nc.scalar.activation(tmpbf[:, :], x1sb[:, :], AF.Identity,
                                             bias=sv[3][:, :], scale=sv[2][:, :]))
            transpose_tok_to_feat(tmpbf, xln2T, tt, psb, ac, va, tstate)
        v_x1_done = v_x1_sp[NTT - 1]

    if dbg:
        dbg4 = nc.dram_tensor("dbg_xln2T", [128, KC * 1024], BF16, kind="ExternalOutput").ap()
        nc.gpsimd.wait_ge(dv.h, dv.n)
        lg.dinc(nc.gpsimd.dma_start(dbg4[:, :], xln2T[:, :]))

    xgT = sb("xgT", [32, 1024], BF16, ffn_stack)
    nc.tensor.wait_ge(dv.h, dv.n)
    nc.tensor.wait_ge(ac.h, ac.n)
    for half, pp in ((0, ps[0]), (1, ps[1])):
        for kc in range(KC):
            i = nc.tensor.matmul(
                pp[0:32, :], oneh32[:, kc * 32:(kc + 1) * 32],
                xln2T[:, kc * 1024 + half * 512: kc * 1024 + (half + 1) * 512],
                start=(kc == 0), stop=(kc == KC - 1))
    v5 = pe.inc(i)
    nc.vector.wait_ge(pe.h, v5)
    dv.inc(nc.vector.tensor_copy(xgT[:, 0:512], ps[0][0:32, :]))
    v_xg = dv.inc(nc.vector.tensor_copy(xgT[:, 512:1024], ps[1][0:32, :]))

    with ExitStack() as p6:
        ps6 = [psum(f"ps6{i}", [128, 512], F32, p6) for i in range(2)]
        pall = ps + ps6
        NS6 = 3
        gub = [[sb(f"byt{m}{i}", [128, 2048], U8, p6) for i in range(NS6)] for m in range(2)]
        zssl = [[sb(f"zssl{m}{i}", [32, 128], BF16, p6) for i in range(NS6)] for m in range(2)]
        sbc = [[sb(f"sbc{m}{i}", [128, DM], BF16, p6) for i in range(NS6)] for m in range(2)]
        wgu = [[sb(f"wgu{m}{i}", [128, DM], BF16, p6) for i in range(NS6)] for m in range(2)]
        nib = [sb(f"nib{m}", [128, DM], U8, p6) for m in range(2)]
        silu_sb = sb("silu_sb", [128, 1024], BF16, p6)
        h_sb = [sb(f"hsb{i}", [128, 1024], BF16, p6) for i in range(2)]
        nc.sync.wait_ge(lg.h, v_tables)
        v6_ld = [0] * CC
        v6_deq = [0] * CC
        v6_mm = [0] * CC
        v6_h = [0] * CC
        v6_hs = [0] * CC
        PF = 2
        srcs = [(d_gb, s_sgbblk[0], s_zs[0]), (d_ub, s_sgbblk[1], s_zs[1])]
        for it in range(CC + PF):
            c = it
            if c < CC:
                slc = c % NS6
                if c >= NS6:
                    nc.sync.wait_ge(dv.h, v6_deq[c - NS6])
                    nc.sync.wait_ge(pe.h, v6_mm[c - NS6])
                elif c == 0:
                    nc.sync.wait_ge(dv.h, v_xg)
                for m, (dby, dsgb, dzs) in enumerate(srcs):
                    ls.dinc(nc.sync.dma_start(
                        gub[m][slc][:, :], dby[c].rearrange("p kc b -> p (kc b)")))
                    ls.dinc(nc.sync.dma_start(
                        zssl[m][slc][:, :], dzs[:, c * 128:(c + 1) * 128]))
                    ls.dinc(nc.sync.dma_start(
                        sbc[m][slc][:, :],
                        dsgb[c].rearrange("g i -> (g i)").unsqueeze(0)
                        .partition_broadcast(128).squeeze(1)))
                v6_ld[c] = ls.n
            cd = it - 1
            if 0 <= cd < CC:
                slc = cd % NS6
                nc.vector.wait_ge(ls.h, v6_ld[cd])
                if cd >= NS6:
                    nc.vector.wait_ge(pe.h, v6_mm[cd - NS6])
                for m in range(2):
                    bt = gub[m][slc][:, :].rearrange("p (kc b) -> p kc b", kc=KC)
                    nv = nib[m][:, :].rearrange("p (kc b two) -> p kc b two", kc=KC, two=2)
                    dv.inc(nc.vector.tensor_scalar(nv[:, :, :, 0], bt, 15, None, ALU.bitwise_and))
                    dv.inc(nc.vector.tensor_scalar(nv[:, :, :, 1], bt, 4, None, ALU.logical_shift_right))
                    dv.inc(nc.vector.scalar_tensor_tensor(
                        wgu[m][slc][:, :], nib[m][:, :], QS, sbc[m][slc][:, :],
                        ALU.subtract, ALU.mult))
                v6_deq[cd] = dv.n
            cc = it - PF
            if cc < 0:
                continue
            slc = cc % NS6
            pbank = pall[0:4] if cc % 2 == 0 else pall[4:8]
            nc.tensor.wait_ge(dv.h, v6_deq[cc])
            if cc >= 2:
                nc.tensor.wait_ge(dv.h, v6_h[cc - 2])
            for m in range(2):
                for half in range(2):
                    pp = pbank[m * 2 + half]
                    i = nc.tensor.matmul(pp[:, :], zssl[m][slc][:, :],
                                         xgT[:, half * 512:(half + 1) * 512],
                                         start=True, stop=False)
                    for kc in range(KC):
                        i = nc.tensor.matmul(
                            pp[:, :], wgu[m][slc][:, kc * 128:(kc + 1) * 128],
                            xln2T[:, kc * 1024 + half * 512: kc * 1024 + (half + 1) * 512],
                            start=False, stop=(kc == KC - 1))
            v6_mm[cc] = pe.inc(i)
            nc.scalar.wait_ge(pe.h, v6_mm[cc])
            ac.inc(nc.scalar.activation(silu_sb[:, 0:512], pbank[0][:, :], AF.Silu))
            v_silu = ac.inc(nc.scalar.activation(silu_sb[:, 512:1024], pbank[1][:, :], AF.Silu))
            nc.vector.wait_ge(ac.h, v_silu)
            hsl = cc % 2
            if cc >= 2:
                nc.vector.wait_ge(lg.h, v6_hs[cc - 2])
            dv.inc(nc.vector.tensor_tensor(h_sb[hsl][:, 0:512], silu_sb[:, 0:512],
                                           pbank[2][:, :], ALU.mult))
            v6_h[cc] = dv.inc(nc.vector.tensor_tensor(
                h_sb[hsl][:, 512:1024], silu_sb[:, 512:1024], pbank[3][:, :], ALU.mult))
            nc.gpsimd.wait_ge(dv.h, v6_h[cc])
            v6_hs[cc] = lg.dinc(nc.gpsimd.dma_start(s_h[cc], h_sb[hsl][:, :]))
        v_h_done = v6_hs[CC - 1]

    ffn_stack.close()
    hgT_bf = sb("hgT_bf", [CC, 1024], BF16)
    with ExitStack() as p7:
        wd = [[sb(f"wd{b}_{j}", [128, 2048], BF16, p7) for j in range(DB)] for b in range(2)]
        wdby = [sb(f"wdby{j}", [128, 1024], U8, p7) for j in range(DB)]
        h7 = [[sb(f"h7{b}_{j}", [128, 1024], BF16, p7) for j in range(DB)] for b in range(2)]
        sdbc = [sb(f"sdbc{j}", [128, 2048], BF16, p7) for j in range(DB)]
        nibd = sb("nibd", [128, 2048], U8, p7)
        oneh88 = sb("oneh88", [128, DB * CC], BF16, p7)
        zsd_sb = sb("zsd_sb", [CC, 2048], BF16, p7)
        stg = [sb(f"stg{i}", [128, 1024], F32, p7) for i in range(2)]
        nc.sync.wait_ge(lg.h, v_h_done)

        pd = ps[0:4]
        phg = (ps[4][0:CC, :], ps[5][0:CC, :])
        nc.tensor.wait_ge(ac.h, ac.n)
        NG = 2 * NR

        v_acc = {}
        v_stg = {}
        v_evac = {}
        last_deq = 0
        v_band_ld = {}
        v_deq = {}
        v_mmband = {}
        v_hg_pe = 0
        v_hgT = 0
        v_oneh = 0
        v_zsd_ld = [0, 0]

        def issue_band_loads(g):
            nonlocal last_deq
            ph, r = g // NR, g % NR
            buf = g % 2
            if last_deq:
                nc.sync.wait_ge(dv.h, last_deq)
            if g >= 2:
                nc.sync.wait_ge(pe.h, v_mmband[g - 2])
            for j in range(DB):
                c = r * DB + j
                ls.dinc(nc.sync.dma_start(
                    wdby[j][:, :], d_db[c][:, ph * 1024:(ph + 1) * 1024]))
                ls.dinc(nc.sync.dma_start(h7[buf][j][:, :], s_h[c]))
                ls.dinc(nc.sync.dma_start(
                    sdbc[j][:, :],
                    s_dsb[c:c + 1, ph * 2048:(ph + 1) * 2048]
                    .partition_broadcast(128).squeeze(1)))
            v_band_ld[g] = ls.n

        def deq_band(g):
            nonlocal last_deq
            buf = g % 2
            nc.vector.wait_ge(ls.h, v_band_ld[g])
            if g >= 2:
                nc.vector.wait_ge(pe.h, v_mmband[g - 2])
            for j in range(DB):
                nv = nibd[:, :].rearrange("p (b two) -> p b two", two=2)
                dv.inc(nc.vector.tensor_scalar(nv[:, :, 0], wdby[j][:, :], 15, None, ALU.bitwise_and))
                dv.inc(nc.vector.tensor_scalar(nv[:, :, 1], wdby[j][:, :], 4, None, ALU.logical_shift_right))
                dv.inc(nc.vector.scalar_tensor_tensor(
                    wd[buf][j][:, :], nibd[:, :], QS, sdbc[j][:, :],
                    ALU.subtract, ALU.mult))
            dpad()
            last_deq = dv.inc(nc.vector.engine_nop())
            v_deq[g] = last_deq

        def oneh_build(g):
            nonlocal v_oneh
            r = g % NR
            nc.vector.wait_ge(pe.h, v_hg_pe)
            dv.inc(nc.vector.memset(oneh88[:, :], 0.0))
            for j in range(DB):
                c = r * DB + j
                dv.inc(nc.vector.memset(oneh88[:, j * CC + c:j * CC + c + 1], 1.0))
            dpad(16)
            v_oneh = dv.inc(nc.vector.engine_nop())

        def mm_round(g):
            nonlocal v_hg_pe, v_hgT
            ph, r = g // NR, g % NR
            buf = g % 2
            if ph == 0:
                nc.tensor.wait_ge(dv.h, v_oneh)
                nc.tensor.wait_ge(ls.h, v_band_ld[g])
                for j in range(DB):
                    for half, pp in ((0, phg[0]), (1, phg[1])):
                        i = nc.tensor.matmul(
                            pp, oneh88[:, j * CC:(j + 1) * CC],
                            h7[buf][j][:, half * 512:(half + 1) * 512],
                            start=(r == 0 and j == 0),
                            stop=(r == NR - 1 and j == DB - 1))
                v_hg_pe = pe.inc(i)
                if r == NR - 1:
                    nc.vector.wait_ge(pe.h, v_hg_pe)
                    dv.inc(nc.vector.tensor_copy(hgT_bf[:, 0:512], phg[0]))
                    dv.inc(nc.vector.tensor_copy(hgT_bf[:, 512:1024], phg[1]))
                    dpad()
                    v_hgT = dv.inc(nc.vector.engine_nop())
            nc.tensor.wait_ge(dv.h, v_deq[g])
            nc.tensor.wait_ge(ls.h, v_band_ld[g])
            if g % NR == NR - 1:
                nc.tensor.wait_ge(dv.h, v_hgT)
                nc.tensor.wait_ge(ls.h, v_zsd_ld[ph])
            for ol in range(16):
                oc = ph * 16 + ol
                pair = ol % 2
                if pair in v_evac:
                    nc.tensor.wait_ge(ac.h, v_evac[pair])
                ppa, ppb = pd[pair * 2], pd[pair * 2 + 1]
                for half, pp in ((0, ppa), (1, ppb)):
                    for j in range(DB):
                        i = nc.tensor.matmul(
                            pp[:, :], wd[buf][j][:, ol * 128:(ol + 1) * 128],
                            h7[buf][j][:, half * 512:(half + 1) * 512],
                            start=(j == 0), stop=(j == DB - 1 and r != NR - 1))
                    if r == NR - 1:
                        i = nc.tensor.matmul(
                            pp[:, :], zsd_sb[:, ol * 128:(ol + 1) * 128],
                            hgT_bf[:, half * 512:(half + 1) * 512],
                            start=False, stop=True)
                v_mmoc = pe.inc(i)
                nc.scalar.wait_ge(pe.h, v_mmoc)
                slot = ol % 2
                if slot in v_stg:
                    nc.scalar.wait_ge(lg.h, v_stg[slot])
                sg = stg[slot]
                ac.inc(nc.scalar.activation(sg[:, 0:512], ppa[:, :], AF.Identity))
                ac.inc(nc.scalar.activation(sg[:, 512:1024], ppb[:, :], AF.Identity))
                apad(2)
                v_ev = ac.inc(nc.scalar.activation(svp[:, :], svp[:, :], AF.Identity))
                v_evac[pair] = v_ev
                nc.gpsimd.wait_ge(ac.h, v_ev)
                if oc in v_acc:
                    nc.gpsimd.wait_ge(lg.h, v_acc[oc])
                op = ALU.bypass if r == 0 else ALU.add
                v_acc[oc] = lg.dinc(nc.gpsimd.dma_start(
                    s_oacc[oc], sg[:, :], accum_op=op))
                v_stg[slot] = v_acc[oc]
            v_mmband[g] = pe.n

        v_zsd_ld[0] = ls.dinc(nc.sync.dma_start(
            zsd_sb[:, :], s_zsd[:, 0:2048]))
        issue_band_loads(0)
        deq_band(0)
        issue_band_loads(1)
        for g in range(NG):
            if g < NR:
                oneh_build(g)
            if g + 1 < NG:
                deq_band(g + 1)
            mm_round(g)
            if g == NR - 1:
                nc.sync.wait_ge(pe.h, pe.n)
                v_zsd_ld[1] = ls.dinc(nc.sync.dma_start(
                    zsd_sb[:, :], s_zsd[:, 2048:4096]))
            if g + 2 < NG:
                issue_band_loads(g + 2)
        v_oacc_done = max(v_acc.values())

    with ExitStack() as p8:
        oblk = [sb(f"oblk{i}", [128, DM], F32, p8) for i in range(2)]
        x1tt = [sb(f"x1tt{i}", [128, DM], F32, p8) for i in range(2)]
        ftok = sb("ftok", [128, DM], F32, p8)
        sq_scr = sb("sq8", [128, DM], BF16, p8)
        out_sb = [sb(f"outsb{i}", [128, DM], F32, p8) for i in range(2)]
        pst = [ps[0], ps[1]]
        ostate = [0, 0]
        v_o_ld = [0] * NTT
        v_o_free = [0] * NTT
        v_out_sp = [0] * NTT
        nc.sync.wait_ge(lg.h, v_oacc_done)
        for it in range(NTT + 1):
            if it < NTT:
                if it >= 2:
                    nc.sync.wait_ge(dv.h, v_o_free[it - 2])
                ls.dinc(nc.sync.dma_start(
                    oblk[it % 2][:, :].rearrange("p (oc t) -> p oc t", t=128),
                    s_oacc[:, :, it * 128:(it + 1) * 128].transpose([1, 0, 2])))
                v_o_ld[it] = ls.dinc(nc.sync.dma_start(
                    x1tt[it % 2][:, :], s_x1[it * 128:(it + 1) * 128, :]))
            tt = it - 1
            if tt < 0:
                continue
            for k4 in range(KC // 4):
                pb = pst[k4 % 2]
                nc.tensor.wait_ge(ls.h, v_o_ld[tt])
                if ostate[k4 % 2]:
                    nc.tensor.wait_ge(dv.h, ostate[k4 % 2])
                for j in range(4):
                    oc = k4 * 4 + j
                    i = nc.tensor.transpose(
                        pb[:, j * 128:(j + 1) * 128],
                        oblk[tt % 2][:, oc * 128:(oc + 1) * 128], identf[:, :])
                vp = pe.inc(i)
                nc.vector.wait_ge(pe.h, vp)
                ostate[k4 % 2] = dv.inc(nc.vector.tensor_copy(
                    ftok[:, :].rearrange("p (k t) -> p k t", t=128)[:, k4 * 4:k4 * 4 + 4, :],
                    pb[:, :].rearrange("p (j t) -> p j t", t=128)))
            ln_stats(ftok, sq_scr)
            nc.scalar.wait_ge(dv.h, dv.n)
            va = ac.inc(nc.scalar.activation(sq_scr[:, :], ftok[:, :], AF.Identity,
                                             bias=sv[3][:, :], scale=sv[2][:, :]))
            nc.vector.wait_ge(ac.h, va)
            nc.vector.wait_ge(ls.h, v_o_ld[tt])
            if tt >= 2:
                nc.vector.wait_ge(st.h, v_out_sp[tt - 2])
            v_o_free[tt] = dv.inc(nc.vector.tensor_tensor(
                out_sb[tt % 2][:, :], x1tt[tt % 2][:, :], sq_scr[:, :], ALU.add))
            nc.gpsimd.wait_ge(dv.h, v_o_free[tt])
            v_out_sp[tt] = st.dinc(nc.gpsimd.dma_start(
                d_out[tt * 128:(tt + 1) * 128, :], out_sb[tt % 2][:, :]))
        nc.gpsimd.wait_ge(st.h, v_out_sp[NTT - 1])

    top.close()
    return nc


_NC_CACHE = None


def kernel(**inputs):
    global _NC_CACHE
    x_shards, common = _host_prep(inputs)
    if _NC_CACHE is None:
        _NC_CACHE = build_program()
    nc = _NC_CACHE
    in_maps = [dict(common, x=x_shards[c]) for c in range(NCORE)]
    res = run_bass_kernel_spmd(nc, in_maps, core_ids=list(range(NCORE)))
    out = np.concatenate([np.asarray(r["out"]) for r in res.results], axis=0)
    return out.reshape(4, 2048, DM).astype(np.float32)
